# version 01
# baseline (speedup 1.0000x reference)
import numpy as np
import ml_dtypes
from contextlib import ExitStack

import concourse.bass as bass
import concourse.mybir as mybir
from concourse.bass_utils import run_bass_kernel_spmd

F32 = mybir.dt.float32
BF16 = mybir.dt.bfloat16
AF = mybir.ActivationFunctionType
ALU = mybir.AluOpType
AX = mybir.AxisListType
bfloat16 = ml_dtypes.bfloat16

P = 128
T = 2048
C = 2048
NT = 16
HQ, HKV, D = 32, 8, 64
KVW = HKV * D
NE = 8
HEX = 1408
HSH = 2816
EXP_C = 9.0
EPS = 1e-5

NSLOT = 45
SLOT = 2048

XFM = list(range(16))
QTM = list(range(16, 32))
KTM = [(32 + tb // 4, (tb % 4) * 512) for tb in range(NT)]
VTM = [(36 + tb // 4, (tb % 4) * 512) for tb in range(NT)]
WST = [(40 + c // 4, (c % 4) * 512) for c in range(16)]
SCR44 = 44
KDUP = list(range(8))
QFM = [8, 9, 10, 11, 12, 13, 14, 15, 32, 33, 34, 35, 40, 41, 42, 43]
PTR = 16
PTM = [24, 25]
ATP = list(range(16, 32))
WOS = [0, 2]
LN1FM = [0, 1, 2, 3, 4, 5, 6, 7, 36, 37, 38, 39, 40, 41, 42, 43]
LN1TMS = [44, 34]
GWSB = 8
GLOG = 9
WER8 = 43
LN1H = [(c // 2, (c % 2) * 1024) for c in range(16)]
ACC = list(range(8, 16))
HSL = [(16 + kb // 2, (kb % 2) * 1024) for kb in range(11)]
DWT = list(range(22, 33))
UPT = [[(33 + c // 4, (c % 4) * 512) for c in range(16)],
       [(37 + c // 4, (c % 4) * 512) for c in range(16)]]
WEBC = (41, 0)
SGS = (41, 1024)
SWC = (42, 0)
LN2TM = 44

CP_EXPB, CP_SCL, CP_NEG, CP_INV, CP_EPS, CP_ONE = 0, 1, 2, 3, 4, 5


class Sem:
    def __init__(self, nc, es, name):
        self.h = es.enter_context(nc.semaphore(name))
        self.n = 0

    def inc(self, inst, k):
        inst.then_inc(self.h, k)
        self.n += k
        return self.n


def build_program(stop_phase=None):
    import os
    if stop_phase is None:
        stop_phase = os.environ.get("KRN_STOP_PHASE", "all")
    PHASES = ["b0", "b1", "pk", "pv", "pq1", "proj", "ktr", "qtr", "attn",
              "wo", "ln1", "rout", "ffn", "all"]
    plim = PHASES.index(stop_phase)

    def phase_on(name):
        return PHASES.index(name) <= plim

    nc = bass.Bass("TRN2", target_bir_lowering=False)

    debug_tap = plim < PHASES.index("all")

    x_in = nc.dram_tensor("x", [T, C], F32, kind="ExternalInput")
    wq_d = nc.dram_tensor("wq", [C, C], F32, kind="ExternalInput")
    bq_d = nc.dram_tensor("bq", [1, C], F32, kind="ExternalInput")
    wk_d = nc.dram_tensor("wk", [C, KVW], F32, kind="ExternalInput")
    bk_d = nc.dram_tensor("bk", [1, KVW], F32, kind="ExternalInput")
    wv_d = nc.dram_tensor("wv", [C, KVW], F32, kind="ExternalInput")
    bv_d = nc.dram_tensor("bv", [1, KVW], F32, kind="ExternalInput")
    wo_d = nc.dram_tensor("wo", [C, C], F32, kind="ExternalInput")
    bo_d = nc.dram_tensor("bo_c", [P, NT], F32, kind="ExternalInput")
    gw_d = nc.dram_tensor("gw", [NE, C], F32, kind="ExternalInput")
    gb_d = nc.dram_tensor("gb", [1, NE], F32, kind="ExternalInput")
    sgw_d = nc.dram_tensor("sgw", [2 * HSH, C], F32, kind="ExternalInput")
    sdw_d = nc.dram_tensor("sdw", [C, HSH], F32, kind="ExternalInput")
    upw_d = nc.dram_tensor("upw", [NE, 2 * HEX, C], F32, kind="ExternalInput")
    dww_d = nc.dram_tensor("dww", [NE, C, HEX], F32, kind="ExternalInput")
    cos_d = nc.dram_tensor("cosp", [P, NT * 32], BF16, kind="ExternalInput")
    sin_d = nc.dram_tensor("sinp", [P, NT * 32], BF16, kind="ExternalInput")
    eyeb_d = nc.dram_tensor("eyeb", [P, P], BF16, kind="ExternalInput")
    eyef_d = nc.dram_tensor("eyef", [P, P], F32, kind="ExternalInput")
    ub_d = nc.dram_tensor("ubias", [P, P], BF16, kind="ExternalInput")
    sel8_d = nc.dram_tensor("sel8", [8, 8 * P], BF16, kind="ExternalInput")
    out_d = nc.dram_tensor("out", [T, C], F32, kind="ExternalOutput")

    x_bf = nc.dram_tensor("x_bfs", [T, C], BF16, kind="Internal")
    sgw_bf = nc.dram_tensor("sgw_bfs", [2 * HSH, C], BF16, kind="Internal")
    sdw_bf = nc.dram_tensor("sdw_bfs", [C, HSH], BF16, kind="Internal")
    up_bf = nc.dram_tensor("up_bfs", [NE, 2 * HEX, C], BF16, kind="Internal")
    dw_bf = nc.dram_tensor("dw_bfs", [NE, C, HEX], BF16, kind="Internal")
    ln1tm_d = nc.dram_tensor("ln1tm_d", [NT, P, C], BF16, kind="Internal")
    ln1fm_d = nc.dram_tensor("ln1fm_d", [NT, P, T], BF16, kind="Internal")
    attn_tap_h = int(os.environ.get("KRN_ATTN_TAP_H", "-1"))
    if attn_tap_h >= 0:
        debug_tap = True
    if debug_tap:
        tap_d = nc.dram_tensor("tap", [P, NSLOT * SLOT], BF16, kind="ExternalOutput")
        tapf_d = nc.dram_tensor("tapf", [P, 4096], F32, kind="ExternalOutput")
        tapw_d = nc.dram_tensor("tapw", [P, NT * 8], F32, kind="ExternalOutput")
        tapd_d = nc.dram_tensor("tapd", [P, 8], F32, kind="ExternalOutput")
        tapr_d = nc.dram_tensor("tapr", [P, 48], F32, kind="ExternalOutput")
        tapg_d = nc.dram_tensor("tapg", [P, 2 * P], BF16, kind="ExternalOutput")

    es = ExitStack()
    with es:
        arena = es.enter_context(nc.sbuf_tensor("arena", [P, NSLOT * SLOT], BF16))
        fs = es.enter_context(nc.sbuf_tensor("fs", [P, 4096], F32))
        browt = es.enter_context(nc.sbuf_tensor("browt", [P, SLOT], BF16))
        cos_sb = es.enter_context(nc.sbuf_tensor("cos_sb", [P, NT * 32], BF16))
        sin_sb = es.enter_context(nc.sbuf_tensor("sin_sb", [P, NT * 32], BF16))
        eye_bf = es.enter_context(nc.sbuf_tensor("eye_bf", [P, P], BF16))
        eye_f = es.enter_context(nc.sbuf_tensor("eye_f", [P, P], F32))
        ub_sb = es.enter_context(nc.sbuf_tensor("ub_sb", [P, P], BF16))
        sel8_sb = es.enter_context(nc.sbuf_tensor("sel8_sb", [8, 8 * P], BF16))
        ones_bf = es.enter_context(nc.sbuf_tensor("ones_bf", [P, P], BF16))
        ones_f = es.enter_context(nc.sbuf_tensor("ones_f", [1, P], F32))
        cpool = es.enter_context(nc.sbuf_tensor("cpool", [P, 8], F32))
        bo_sb = es.enter_context(nc.sbuf_tensor("bo_sb", [P, NT], F32))
        gwt_sb = es.enter_context(nc.sbuf_tensor("gwt_sb", [P, P], BF16))
        dn_sb = es.enter_context(nc.sbuf_tensor("dn_sb", [P, 2], F32))
        rc_sb = es.enter_context(nc.sbuf_tensor("rc_sb", [P, 1], F32))
        diag_sb = es.enter_context(nc.sbuf_tensor("diag_sb", [P, 2 * P], BF16))
        st_sb = es.enter_context(nc.sbuf_tensor("st_sb", [P, 8], F32))
        rt_sb = es.enter_context(nc.sbuf_tensor("rt_sb", [P, 48], F32))
        we_all = es.enter_context(nc.sbuf_tensor("we_all", [P, NT * 8], F32))
        pS = es.enter_context(nc.psum_tensor("pS", [P, 2048], F32))
        pT = es.enter_context(nc.psum_tensor("pT", [P, 512], F32))
        pT2 = es.enter_context(nc.psum_tensor("pT2", [P, 512], F32))
        pV = es.enter_context(nc.psum_tensor("pV", [P, 1024], F32))

        sQ = Sem(nc, es, "sQ")
        sS = Sem(nc, es, "sS")
        sPE = Sem(nc, es, "sPE")
        sACT = Sem(nc, es, "sACT")
        sDVE = Sem(nc, es, "sDVE")
        sR = Sem(nc, es, "sR")

        def vedge(v, inst):
            sR.inc(inst, 1)
            v.wait_ge(sR.h, sR.n)

        def sl(i, off=0, ln=SLOT):
            return arena[:, i * SLOT + off: i * SLOT + off + ln]

        def strided(view, stride, count, width):
            return bass.AP(view.tensor, view.offset,
                           [view.ap[0], [stride, count], [1, width]])

        cnt = {}

        def emit_dump():
            with nc.Block() as blk:
                @blk.sync
                def _(sy):
                    sS.inc(sy.dma_start(tap_d.ap(), arena[:, :]), 16)
                    sS.inc(sy.dma_start(tapf_d.ap(), fs[:, :]), 16)
                    sS.inc(sy.dma_start(tapw_d.ap(), we_all[:, :]), 16)
                    sS.inc(sy.dma_start(tapd_d.ap()[:, 0:2], dn_sb[:, :], allow_slow_non_contiguous=True), 16)
                    sS.inc(sy.dma_start(tapd_d.ap()[:, 2:3], rc_sb[:, :], allow_slow_non_contiguous=True), 16)
                    sS.inc(sy.dma_start(tapg_d.ap(), diag_sb[:, :]), 16)
                    sS.inc(sy.dma_start(tapr_d.ap(), rt_sb[:, :]), 16)
                    sy.wait_ge(sS.h, sS.n)
            return nc

        with nc.Block() as blk:
            @blk.gpsimd
            def _(g):
                cnt['x_bf'] = sQ.inc(g.dma_start(x_bf.ap(), x_in.ap()), 16)
                cnt['sgw'] = sQ.inc(g.dma_start(sgw_bf.ap(), sgw_d.ap()), 16)
                cnt['sdw'] = sQ.inc(g.dma_start(sdw_bf.ap(), sdw_d.ap()), 16)
                for e in range(NE):
                    cnt[f'up{e}'] = sQ.inc(g.dma_start(up_bf.ap()[e], upw_d.ap()[e]), 16)
                    cnt[f'dw{e}'] = sQ.inc(g.dma_start(dw_bf.ap()[e], dww_d.ap()[e]), 16)

            @blk.sync
            def _(sy):
                sS.inc(sy.dma_start(cos_sb[:, :], cos_d.ap()), 16)
                sS.inc(sy.dma_start(sin_sb[:, :], sin_d.ap()), 16)
                sS.inc(sy.dma_start(eye_bf[:, :], eyeb_d.ap()), 16)
                sS.inc(sy.dma_start(eye_f[:, :], eyef_d.ap()), 16)
                sS.inc(sy.dma_start(ub_sb[:, :], ub_d.ap()), 16)
                sS.inc(sy.dma_start(sel8_sb[0:8, :], sel8_d.ap()), 16)
                sS.inc(sy.dma_start(bo_sb[:, :], bo_d.ap()), 16)
                sS.inc(sy.dma_start(fs[0:1, 2048:4096], bq_d.ap()), 16)
                sS.inc(sy.dma_start(fs[0:1, 0:KVW], bk_d.ap()), 16)
                cnt['consts'] = sS.inc(sy.dma_start(fs[0:1, 512:1024], bv_d.ap()), 16)

            @blk.vector
            def _(v):
                v.memset(ones_bf[:, :], 1.0)
                v.memset(ones_f[0:1, :], 1.0)
                v.memset(cpool[:, CP_EXPB:CP_EXPB + 1], -EXP_C)
                v.memset(cpool[:, CP_SCL:CP_SCL + 1], 0.125)
                v.memset(cpool[:, CP_NEG:CP_NEG + 1], -1e9)
                v.memset(cpool[:, CP_INV:CP_INV + 1], 1.0 / C)
                v.memset(cpool[:, CP_EPS:CP_EPS + 1], EPS)
                v.memset(cpool[:, CP_ONE:CP_ONE + 1], 1.0)

        if not phase_on("b1"):
            return emit_dump()
        with nc.Block() as blk:
            @blk.sync
            def _(sy):
                sy.wait_ge(sQ.h, cnt['x_bf'])
                for c in range(16):
                    sS.inc(sy.dma_start(sl(XFM[c]), x_bf.ap()[:, c * P:(c + 1) * P],
                                        transpose=True), 16)
                cnt['xfm'] = sS.n

            @blk.gpsimd
            def _(g):
                dst = arena[:, 40 * SLOT: 40 * SLOT + 16 * 512].rearrange(
                    "p (c s) -> p c s", s=512)
                src = wk_d.ap().rearrange("(c p) s -> p c s", p=P)
                cnt['wk'] = sQ.inc(g.dma_start(dst, src), 16)

            @blk.scalar
            def _(sc):
                sc.wait_ge(sS.h, cnt['consts'])
                cc = sc.copy(browt[0:1, 0:1024], fs[0:1, 0:1024])
                cnt['bias_kv_hi'] = sACT.inc(cc, 1)

            @blk.vector
            def _(v):
                v.wait_ge(sACT.h, cnt['bias_kv_hi'])
                vv = v.tensor_sub(fs[0:1, 1024:2048], fs[0:1, 0:1024],
                                  browt[0:1, 0:1024])
                cnt['bias_kv_lo_f'] = sDVE.inc(vv, 1)

        with nc.Block() as blk:
            @blk.scalar
            def _(sc):
                sc.wait_ge(sDVE.h, cnt['bias_kv_lo_f'])
                cc = sc.copy(browt[32:33, 0:1024], fs[0:1, 1024:2048])
                cnt['bias_kv_lo32'] = sACT.inc(cc, 1)

            @blk.gpsimd
            def _(g):
                g.wait_ge(sS.h, cnt['xfm'])
                g.wait_ge(sACT.h, cnt['bias_kv_lo32'])
                g.dma_start(browt[1:2, 0:1024], browt[32:33, 0:1024]).then_inc(sQ.h, 16)
                sQ.n += 16
                cnt['bias_kv_done'] = sQ.n

        def emit_prep_bq():
            with nc.Block() as blk:
                @blk.scalar
                def _(sc):
                    cc = sc.copy(browt[0:1, 0:C], fs[0:1, 2048:4096])
                    cnt['bias_q_hi'] = sACT.inc(cc, 1)

                @blk.vector
                def _(v):
                    v.wait_ge(sACT.h, cnt['bias_q_hi'])
                    vv = v.tensor_sub(fs[0:1, 0:2048], fs[0:1, 2048:4096],
                                      browt[0:1, 0:C])
                    cnt['bias_q_lo_f'] = sDVE.inc(vv, 1)

            with nc.Block() as blk:
                @blk.scalar
                def _(sc):
                    sc.wait_ge(sDVE.h, cnt['bias_q_lo_f'])
                    cc = sc.copy(browt[32:33, 0:C], fs[0:1, 0:2048])
                    cnt['bias_q_lo32'] = sACT.inc(cc, 1)

                @blk.gpsimd
                def _(g):
                    g.wait_ge(sACT.h, cnt['bias_q_lo32'])
                    g.dma_start(browt[1:2, 0:C], browt[32:33, 0:C]).then_inc(sQ.h, 16)
                    sQ.n += 16
                    cnt['bias_q_done'] = sQ.n

        def emit_rope(v, scr512, dst512, tb):
            sv = scr512.rearrange("p (h two i) -> p h two i", two=2, i=32)
            dv = dst512.rearrange("p (h two i) -> p h two i", two=2, i=32)
            cs = cos_sb[:, tb * 32:(tb + 1) * 32].rearrange("p (o i) -> p o i", o=1)
            sn = sin_sb[:, tb * 32:(tb + 1) * 32].rearrange("p (o i) -> p o i", o=1)
            csb = bass.AP(cs.tensor, cs.offset, [cs.ap[0], [0, 8], cs.ap[2]])
            snb = bass.AP(sn.tensor, sn.offset, [sn.ap[0], [0, 8], sn.ap[2]])
            ev, od = sv[:, :, 0, :], sv[:, :, 1, :]
            t1 = fs[:, 0:256].rearrange("p (h i) -> p h i", i=32)
            t2 = fs[:, 256:512].rearrange("p (h i) -> p h i", i=32)
            v.tensor_mul(t1, ev, csb)
            vedge(v, v.tensor_mul(t2, od, snb))
            vedge(v, v.tensor_sub(dv[:, :, 0, :], t1, t2))
            v.tensor_mul(t1, od, csb)
            vedge(v, v.tensor_mul(t2, ev, snb))
            return v.tensor_add(dv[:, :, 1, :], t1, t2)

        def emit_proj_pass(wkey, bias_rows, bias_key, dst_fn, do_rope, next_w_load):
            with nc.Block() as blk:
                pe0, ac0, dv0 = sPE.n, sACT.n, sDVE.n

                @blk.gpsimd
                def _(g):
                    if next_w_load is not None:
                        name, src_ap = next_w_load
                        g.wait_ge(sPE.h, pe0 + NT)
                        dst = arena[:, 40 * SLOT: 40 * SLOT + 16 * 512].rearrange(
                            "p (c s) -> p c s", s=512)
                        cnt[name] = sQ.inc(g.dma_start(dst, src_ap), 16)

                @blk.tensor
                def _(t):
                    t.wait_ge(sS.h, cnt['xfm'])
                    t.wait_ge(sQ.h, cnt[wkey])
                    t.wait_ge(sQ.h, cnt[bias_key])
                    for tb in range(NT):
                        q4 = tb % 4
                        ps = pS[:, q4 * 512:(q4 + 1) * 512]
                        if tb >= 4:
                            t.wait_ge(sACT.h, ac0 + (tb - 3))
                        for c in range(16):
                            t.matmul(ps, sl(XFM[c], tb * P, P), sl(*WST[c], ln=512),
                                     start=(c == 0), stop=False)
                        mm = t.matmul(ps, bias_rows[0], bias_rows[1],
                                      start=False, stop=True)
                        sPE.inc(mm, 1)

                @blk.scalar
                def _(sc):
                    for tb in range(NT):
                        q4 = tb % 4
                        ps = pS[:, q4 * 512:(q4 + 1) * 512]
                        sc.wait_ge(sPE.h, pe0 + tb + 1)
                        if do_rope:
                            if tb >= 2:
                                sc.wait_ge(sDVE.h, dv0 + 6 * (tb - 1))
                            cc = sc.copy(sl(SCR44, (tb % 2) * 512, 512), ps)
                        else:
                            cc = sc.copy(dst_fn(tb), ps)
                        sACT.inc(cc, 1)

                @blk.vector
                def _(v):
                    if not do_rope:
                        return
                    for tb in range(NT):
                        v.wait_ge(sACT.h, ac0 + tb + 1)
                        vv = emit_rope(v, sl(SCR44, (tb % 2) * 512, 512),
                                       dst_fn(tb), tb)
                        sDVE.inc(vv, 6)

        if not phase_on("pk"):
            return emit_dump()
        bias_k = (ones_bf[0:2, 0:P], browt[0:2, 0:512])
        bias_v = (ones_bf[0:2, 0:P], browt[0:2, 512:1024])

        emit_proj_pass('wk', bias_k, 'bias_kv_done',
                       lambda tb: sl(*KTM[tb], ln=512), True,
                       ('wv', wv_d.ap().rearrange("(c p) s -> p c s", p=P)))
        if not phase_on("pv"):
            return emit_dump()
        emit_proj_pass('wv', bias_v, 'bias_kv_done',
                       lambda tb: sl(*VTM[tb], ln=512), False,
                       ('wq0', wq_d.ap()[:, 0:512].rearrange("(c p) s -> p c s", p=P)))
        if not phase_on("pq1"):
            return emit_dump()
        emit_prep_bq()
        for qq in range(4):
            if qq >= 1 and not phase_on("proj"):
                break
            bias_q = (ones_bf[0:2, 0:P], browt[0:2, qq * 512:(qq + 1) * 512])
            nxt = None
            if qq < 3:
                nxt = (f'wq{qq + 1}',
                       wq_d.ap()[:, (qq + 1) * 512:(qq + 2) * 512].rearrange(
                           "(c p) s -> p c s", p=P))
            emit_proj_pass(f'wq{qq}', bias_q, 'bias_q_done',
                           lambda tb, qq=qq: sl(QTM[tb], qq * 512, 512), True, nxt)

        if not phase_on("ktr"):
            return emit_dump()
        with nc.Block() as blk:
            pe0, dv0 = sPE.n, sDVE.n

            @blk.tensor
            def _(t):
                for tb in range(NT):
                    if tb >= 1:
                        t.wait_ge(sDVE.h, dv0 + 8 * tb)
                    b1 = [pT, pT2][tb % 2]
                    b2 = [pT2, pT][tb % 2]
                    ktile = sl(*KTM[tb], ln=512)
                    mm = None
                    for g in range(HKV):
                        lhs = ktile[:, g * 64:(g + 1) * 64]
                        bank = b1 if g < 4 else b2
                        ps = bank[:, (g % 4) * P:(g % 4) * P + P]
                        t.matmul(ps[0:64, :], lhs, eye_bf[:, :], start=True, stop=True)
                        mm = t.matmul(ps[64:128, :], lhs, eye_bf[:, :], start=True, stop=True)
                    sPE.inc(mm, 1)

            @blk.vector
            def _(v):
                for tb in range(NT):
                    v.wait_ge(sPE.h, pe0 + tb + 1)
                    b1 = [pT, pT2][tb % 2]
                    b2 = [pT2, pT][tb % 2]
                    vv = None
                    for g in range(HKV):
                        bank = b1 if g < 4 else b2
                        ps = bank[:, (g % 4) * P:(g % 4) * P + P]
                        vv = v.tensor_copy(sl(KDUP[g], tb * P, P), ps)
                    sDVE.inc(vv, 8)

        if not phase_on("qtr"):
            return emit_dump()
        with nc.Block() as blk:
            pe0, ac0, dv0 = sPE.n, sACT.n, sDVE.n

            @blk.tensor
            def _(t):
                for tb in range(NT):
                    for grp in range(4):
                        bank = pS[:, grp * 512:(grp + 1) * 512]
                        if tb >= 1:
                            if grp % 2 == 0:
                                t.wait_ge(sDVE.h, dv0 + (tb - 1) * 2 + grp // 2 + 1)
                            else:
                                t.wait_ge(sACT.h, ac0 + (tb - 1) * 2 + grp // 2 + 1)
                        mm = None
                        for j in range(4):
                            d = grp * 4 + j
                            mm = t.matmul(bank[:, j * P:(j + 1) * P],
                                          sl(QTM[tb], d * P, P), eye_bf[:, :],
                                          start=True, stop=True)
                        sPE.inc(mm, 1)

            @blk.vector
            def _(v):
                for tb in range(NT):
                    for grp in (0, 2):
                        bank = pS[:, grp * 512:(grp + 1) * 512]
                        v.wait_ge(sPE.h, pe0 + tb * 4 + grp + 1)
                        vv = None
                        for j in range(4):
                            d = grp * 4 + j
                            vv = v.tensor_copy(sl(QFM[d], tb * P, P),
                                               bank[:, j * P:(j + 1) * P])
                        sDVE.inc(vv, 1)

            @blk.scalar
            def _(sc):
                for tb in range(NT):
                    for grp in (1, 3):
                        bank = pS[:, grp * 512:(grp + 1) * 512]
                        sc.wait_ge(sPE.h, pe0 + tb * 4 + grp + 1)
                        cc = None
                        for j in range(4):
                            d = grp * 4 + j
                            cc = sc.copy(sl(QFM[d], tb * P, P),
                                         bank[:, j * P:(j + 1) * P])
                        sACT.inc(cc, 1)

        if not phase_on("attn"):
            return emit_dump()
        for h in range(HQ):
            if attn_tap_h >= 0 and h > attn_tap_h:
                return emit_dump()
            g = h // 4
            base = (h % 2) * 64
            hp = h // 2

            chunks = []
            for qt in range(NT):
                klen = (qt + 1) * P
                if klen <= 1024:
                    chunks.append((qt, 0, klen))
                else:
                    chunks.append((qt, 0, 1024))
                    chunks.append((qt, 1, klen - 1024))
            chunk_of = {(q, ch): ci for ci, (q, ch, _) in enumerate(chunks)}
            last_ch = {qt: (1 if (qt + 1) * P > 1024 else 0) for qt in range(NT)}

            pe_items = []
            for qt in range(NT):
                for (q, ch, cw) in chunks:
                    if q == qt:
                        pe_items.append(("S", qt, ch, cw))
                if qt == 9:
                    pe_items.append(("PV", 0))
                if qt >= 1:
                    pe_items.append(("PT", qt - 1))
            pe_items.append(("PT", NT - 1))
            pe_items.append(("PV", 1))

            pe0, ac0, dv0 = sPE.n, sACT.n, sDVE.n
            s_mark, pt_gmarks, pv_mark = {}, {}, {}
            m = 0
            for it in pe_items:
                if it[0] == "S":
                    m += 1
                    s_mark[(it[1], it[2])] = pe0 + m
                elif it[0] == "PT":
                    qtp = it[1]
                    ng = (qtp + 4) // 4
                    marks = []
                    for _ in range(ng):
                        m += 1
                        marks.append(pe0 + m)
                    pt_gmarks[qtp] = marks
                else:
                    m += 1
                    pv_mark[it[1]] = pe0 + m

            exp_mark = {}
            a = 0
            act_items = []
            for (qt, ch, cw) in chunks:
                act_items.append(("E", qt, ch, cw))
                a += 1
                exp_mark[(qt, ch)] = ac0 + a
                if qt == 9 and ch == last_ch[9]:
                    act_items.append(("O", 0))
                    a += 1
                    oev0_mark = ac0 + a
            act_items.append(("O", 1))
            a += 1

            dve_items = []
            d = 0
            diag_mark = {}
            evac_gmark = []
            evac_last_of_qt = {}
            for qt in range(NT):
                nch = 2 if (qt + 1) * P > 1024 else 1
                dve_items.append(("DG", qt, nch))
                d += (1 if nch == 2 else 0) + 2
                diag_mark[qt] = dv0 + d
                if qt >= 1:
                    qtp = qt - 1
                    ng = (qtp + 4) // 4
                    dve_items.append(("EV", qtp, ng))
                    for _ in range(ng):
                        d += 1
                        evac_gmark.append(dv0 + d)
                    evac_last_of_qt[qtp] = dv0 + d
            qtp = NT - 1
            ng = (qtp + 4) // 4
            dve_items.append(("EV", qtp, ng))
            for _ in range(ng):
                d += 1
                evac_gmark.append(dv0 + d)
            evac_last_of_qt[qtp] = dv0 + d

            with nc.Block() as blk:
                @blk.tensor
                def _(t, h=h, g=g, base=base, hp=hp, chunks=chunks,
                      pe_items=pe_items, chunk_of=chunk_of, exp_mark=exp_mark,
                      diag_mark=diag_mark, evac_gmark=evac_gmark,
                      evac_last_of_qt=evac_last_of_qt, last_ch=last_ch,
                      oev0_mark=oev0_mark):
                    ggl = [0]

                    def do_S(qt, ch, cw):
                        ci = chunk_of[(qt, ch)]
                        ps = pS[:, (ci % 2) * 1024: (ci % 2) * 1024 + cw]
                        if ci >= 2:
                            pq, pc, _ = chunks[ci - 2]
                            t.wait_ge(sACT.h, exp_mark[(pq, pc)])
                        lhs = arena[base:base + 64,
                                    QFM[hp] * SLOT + qt * P: QFM[hp] * SLOT + (qt + 1) * P]
                        dpos = qt * P - ch * 1024
                        mm = None
                        pos = 0
                        while pos < cw:
                            w = min(512, cw - pos)
                            kcol = ch * 1024 + pos
                            rhs = arena[base:base + 64,
                                        KDUP[g] * SLOT + kcol: KDUP[g] * SLOT + kcol + w]
                            has_diag = (pos <= dpos < pos + w)
                            mm = t.matmul(ps[:, pos:pos + w], lhs, rhs,
                                          start=True, stop=not has_diag)
                            pos += w
                        if 0 <= dpos < cw:
                            mm = t.matmul(ps[:, dpos:dpos + P], eye_bf[:, :],
                                          ub_sb[:, :], start=False, stop=True)
                        sPE.inc(mm, 1)

                    def do_PT(qtp):
                        nkb = qtp + 1
                        t.wait_ge(sDVE.h, diag_mark[qtp])
                        t.wait_ge(sACT.h, exp_mark[(qtp, last_ch[qtp])])
                        for g0 in range(0, nkb, 4):
                            gw_ = min(4, nkb - g0)
                            gi = ggl[0]
                            bank = [pT, pT2][gi % 2]
                            if gi >= 2:
                                t.wait_ge(sDVE.h, evac_gmark[gi - 2])
                            mm = None
                            for j in range(gw_):
                                kb = g0 + j
                                mm = t.matmul(bank[:, j * P:(j + 1) * P],
                                              sl(PTM[qtp % 2], kb * P, P),
                                              diag_sb[:, (qtp % 2) * P:(qtp % 2 + 1) * P],
                                              start=True, stop=True)
                            sPE.inc(mm, 1)
                            ggl[0] += 1

                    def do_PV(half):
                        kmax = 8 if half == 0 else 16
                        lastq = 7 if half == 0 else 15
                        t.wait_ge(sDVE.h, evac_last_of_qt[lastq])
                        if half == 1:
                            t.wait_ge(sACT.h, oev0_mark)
                        mm = None
                        for kb in range(kmax):
                            qs = max(0, kb * P - half * 1024)
                            vt = sl(*VTM[kb], ln=512)
                            lhs = vt[:, g * 64:(g + 1) * 64]
                            pos = qs
                            while pos < 1024:
                                w = min(512 - (pos % 512), 1024 - pos)
                                mm = t.matmul(pV[0:64, pos:pos + w], lhs,
                                              arena[:, PTR * SLOT + kb * 1024 + pos:
                                                    PTR * SLOT + kb * 1024 + pos + w],
                                              start=(kb == 0),
                                              stop=(kb == kmax - 1))
                                pos += w
                        sPE.inc(mm, 1)

                    for it in pe_items:
                        if it[0] == "S":
                            do_S(it[1], it[2], it[3])
                        elif it[0] == "PT":
                            do_PT(it[1])
                        else:
                            do_PV(it[1])

                @blk.scalar
                def _(sc, base=base, hp=hp, act_items=act_items, s_mark=s_mark,
                      pt_gmarks=pt_gmarks, pv_mark=pv_mark, diag_mark=diag_mark,
                      chunk_of=chunk_of):
                    for it in act_items:
                        if it[0] == "E":
                            qt, ch, cw = it[1], it[2], it[3]
                            ci = chunk_of[(qt, ch)]
                            ps = pS[:, (ci % 2) * 1024:(ci % 2) * 1024 + cw]
                            sc.wait_ge(sPE.h, s_mark[(qt, ch)])
                            if qt >= 2:
                                sc.wait_ge(sPE.h, pt_gmarks[qt - 2][-1])
                            if qt >= 1 and ch == 0:
                                sc.wait_ge(sDVE.h, diag_mark[qt - 1])
                            cc = sc.activation(sl(PTM[qt % 2], ch * 1024, cw), ps,
                                               AF.Exp,
                                               bias=cpool[:, CP_EXPB:CP_EXPB + 1],
                                               scale=cpool[:, CP_SCL:CP_SCL + 1],
                                               accum_out=dn_sb[:, ch:ch + 1])
                            sACT.inc(cc, 1)
                        else:
                            half = it[1]
                            sc.wait_ge(sPE.h, pv_mark[half])
                            obase = QFM[hp] * SLOT + half * 1024
                            sc.copy(arena[base:base + 64, obase:obase + 512],
                                    pV[0:64, 0:512])
                            cc = sc.copy(arena[base:base + 64, obase + 512:obase + 1024],
                                         pV[0:64, 512:1024])
                            sACT.inc(cc, 1)

                @blk.vector
                def _(v, dve_items=dve_items, exp_mark=exp_mark,
                      pt_gmarks=pt_gmarks, last_ch=last_ch):
                    gi2 = [0]
                    for it in dve_items:
                        if it[0] == "DG":
                            qt, nch = it[1], it[2]
                            v.wait_ge(sACT.h, exp_mark[(qt, last_ch[qt])])
                            if qt >= 2:
                                v.wait_ge(sPE.h, pt_gmarks[qt - 2][-1])
                            k = 0
                            if nch == 2:
                                aa = v.tensor_add(st_sb[:, 0:1], dn_sb[:, 0:1],
                                                  dn_sb[:, 1:2])
                                sR.inc(aa, 1)
                                v.wait_ge(sR.h, sR.n)
                                src = st_sb[:, 0:1]
                                k = 1
                            else:
                                src = dn_sb[:, 0:1]
                            rr = v.reciprocal(rc_sb[:, 0:1], src)
                            sR.inc(rr, 1)
                            v.wait_ge(sR.h, sR.n)
                            vv = v.tensor_scalar(
                                diag_sb[:, (qt % 2) * P:(qt % 2 + 1) * P],
                                eye_bf[:, :], rc_sb[:, 0:1], None, ALU.mult)
                            sDVE.inc(vv, k + 2)
                        else:
                            qtp, ng = it[1], it[2]
                            for gg in range(ng):
                                v.wait_ge(sPE.h, pt_gmarks[qtp][gg])
                                gi = gi2[0]
                                bank = [pT, pT2][gi % 2]
                                gw_ = min(4, qtp + 1 - gg * 4)
                                dstb = arena[:, PTR * SLOT + (gg * 4) * 1024 + (qtp % 8) * P:
                                             PTR * SLOT + (gg * 4) * 1024 + (qtp % 8) * P + P]
                                dst = strided(dstb, 1024, gw_, P)
                                vv = v.tensor_copy(dst, bank[:, 0:gw_ * P])
                                sDVE.inc(vv, 1)
                                gi2[0] += 1

        if not phase_on("wo"):
            return emit_dump()
        with nc.Block() as blk:
            pe0, ac0, dv0, q0 = sPE.n, sACT.n, sDVE.n, sQ.n

            @blk.gpsimd
            def _(g):
                for cb in range(16):
                    if cb >= 2:
                        g.wait_ge(sPE.h, pe0 + cb - 1)
                    b = WOS[cb % 2]
                    dst = arena[:, b * SLOT: b * SLOT + 16 * P].rearrange(
                        "p (c s) -> p c s", s=P)
                    src = wo_d.ap()[:, cb * P:(cb + 1) * P].rearrange("(c p) s -> p c s", p=P)
                    sQ.inc(g.dma_start(dst, src), 16)

            @blk.tensor
            def _(t):
                for cb in range(16):
                    t.wait_ge(sQ.h, q0 + 16 * (cb + 1))
                    if cb >= 1:
                        t.wait_ge(sDVE.h, dv0 + cb)
                        t.wait_ge(sACT.h, ac0 + cb)
                    b = WOS[cb % 2]
                    mm = None
                    for cin in range(16):
                        lhs = sl(b, cin * P, P)
                        for tch in range(4):
                            mm = t.matmul(pS[:, tch * 512:(tch + 1) * 512], lhs,
                                          sl(QFM[cin], tch * 512, 512),
                                          start=(cin == 0), stop=(cin == 15))
                    sPE.inc(mm, 1)

            @blk.vector
            def _(v):
                for cb in range(16):
                    v.wait_ge(sPE.h, pe0 + cb + 1)
                    vv = None
                    for tch in (0, 1):
                        vv = v.tensor_scalar(sl(ATP[cb], tch * 512, 512),
                                             pS[:, tch * 512:(tch + 1) * 512],
                                             bo_sb[:, cb:cb + 1], None, ALU.add)
                    sDVE.inc(vv, 1)

            @blk.scalar
            def _(sc):
                for cb in range(16):
                    sc.wait_ge(sPE.h, pe0 + cb + 1)
                    cc = None
                    for tch in (2, 3):
                        cc = sc.add(sl(ATP[cb], tch * 512, 512),
                                    pS[:, tch * 512:(tch + 1) * 512],
                                    bo_sb[:, cb:cb + 1])
                    sACT.inc(cc, 1)

        if not phase_on("ln1"):
            return emit_dump()
        with nc.Block() as blk:
            pe0, ac0, dv0, s0 = sPE.n, sACT.n, sDVE.n, sS.n
            NDV, NAC = 5, 3

            def dvm(tb, i):
                return dv0 + NDV * tb + i

            def acm(tb, i):
                return ac0 + NAC * tb + i

            @blk.sync
            def _(sy):
                for tb in range(NT):
                    if tb >= 1:
                        sy.wait_ge(sDVE.h, dvm(tb - 1, 1))
                    sS.inc(sy.dma_start(fs[:, 2048:4096],
                                        x_in.ap()[tb * P:(tb + 1) * P, :]), 16)
                    sy.wait_ge(sDVE.h, dvm(tb, 4))
                    sS.inc(sy.dma_start(ln1tm_d.ap()[tb], sl(LN1TMS[tb % 2])), 16)
                sy.wait_ge(sDVE.h, dv0 + NDV * NT)
                sy.wait_ge(sACT.h, ac0 + NAC * NT)
                for c in range(16):
                    sS.inc(sy.dma_start(ln1fm_d.ap()[c], sl(LN1FM[c])), 16)
                cnt['ln1fm_stored'] = sS.n

            @blk.tensor
            def _(t):
                for tb in range(NT):
                    if tb >= 1:
                        t.wait_ge(sACT.h, acm(tb - 1, 1))
                        t.wait_ge(sDVE.h, dvm(tb - 1, 5))
                        t.wait_ge(sACT.h, acm(tb - 1, 3))
                    mm = None
                    for c in range(16):
                        mm = t.matmul(pS[:, c * P:(c + 1) * P],
                                      sl(ATP[c], tb * P, P), eye_bf[:, :],
                                      start=True, stop=True)
                    sPE.inc(mm, 1)
                    t.wait_ge(sDVE.h, dvm(tb, 4))
                    t.wait_ge(sACT.h, acm(tb, 1))
                    mm = None
                    for c in range(16):
                        mm = t.matmul(pS[:, c * P:(c + 1) * P],
                                      sl(LN1TMS[tb % 2], c * P, P), eye_bf[:, :],
                                      start=True, stop=True)
                    sPE.inc(mm, 1)

            @blk.vector
            def _(v):
                for tb in range(NT):
                    v.wait_ge(sPE.h, pe0 + 2 * tb + 1)
                    v.wait_ge(sS.h, s0 + 32 * tb + 16)
                    vv = None
                    for q4 in range(4):
                        vv = v.tensor_add(fs[:, q4 * 512:(q4 + 1) * 512],
                                          pS[:, q4 * 512:(q4 + 1) * 512],
                                          fs[:, 2048 + q4 * 512:2048 + (q4 + 1) * 512])
                    sDVE.inc(vv, 1)
                    v.wait_ge(sDVE.h, sDVE.n)
                    vedge(v, v.reduce_sum(st_sb[:, 0:1], fs[:, 0:2048], axis=AX.X))
                    vedge(v, v.tensor_scalar(st_sb[:, 1:2], st_sb[:, 0:1],
                                             cpool[:, CP_INV:CP_INV + 1], None, ALU.mult))
                    vv = v.tensor_mul(st_sb[:, 2:3], st_sb[:, 0:1], st_sb[:, 1:2])
                    sDVE.inc(vv, 1)
                    v.wait_ge(sDVE.h, sDVE.n)
                    v.wait_ge(sACT.h, acm(tb, 1))
                    vedge(v, v.tensor_sub(st_sb[:, 4:5], st_sb[:, 3:4], st_sb[:, 2:3]))
                    vv = v.tensor_scalar(st_sb[:, 5:6], st_sb[:, 4:5],
                                         cpool[:, CP_INV:CP_INV + 1], None, ALU.mult)
                    sDVE.inc(vv, 1)
                    v.wait_ge(sDVE.h, sDVE.n)
                    v.wait_ge(sACT.h, acm(tb, 2))
                    rr = v.reciprocal(st_sb[:, 7:8], st_sb[:, 6:7])
                    sR.inc(rr, 1)
                    v.wait_ge(sR.h, sR.n)
                    if tb >= 2:
                        v.wait_ge(sS.h, s0 + 32 * (tb - 2) + 32)
                    vv = v.tensor_scalar(sl(LN1TMS[tb % 2]), fs[:, 0:2048],
                                         st_sb[:, 1:2], st_sb[:, 7:8],
                                         ALU.subtract, ALU.mult)
                    sDVE.inc(vv, 1)
                    v.wait_ge(sPE.h, pe0 + 2 * tb + 2)
                    vv = None
                    for grp in (0, 2):
                        c0 = grp * 4
                        dstb = arena[:, LN1FM[c0] * SLOT + tb * P:
                                     LN1FM[c0] * SLOT + tb * P + P]
                        vv = v.tensor_copy(strided(dstb, SLOT, 4, P),
                                           pS[:, c0 * P:c0 * P + 4 * P])
                    sDVE.inc(vv, 1)

            @blk.scalar
            def _(sc):
                for tb in range(NT):
                    sc.wait_ge(sDVE.h, dvm(tb, 1))
                    cc = sc.activation(pS[:, 0:2048], fs[:, 0:2048], AF.Square,
                                       accum_out=st_sb[:, 3:4])
                    sACT.inc(cc, 1)
                    sc.wait_ge(sDVE.h, dvm(tb, 3))
                    cc = sc.activation(st_sb[:, 6:7], st_sb[:, 5:6], AF.Sqrt,
                                       bias=cpool[:, CP_EPS:CP_EPS + 1],
                                       scale=cpool[:, CP_ONE:CP_ONE + 1])
                    sACT.inc(cc, 1)
                    sc.wait_ge(sPE.h, pe0 + 2 * tb + 2)
                    cc = None
                    for grp in (1, 3):
                        c0 = grp * 4
                        dstb = arena[:, LN1FM[c0] * SLOT + tb * P:
                                     LN1FM[c0] * SLOT + tb * P + P]
                        cc = sc.copy(strided(dstb, SLOT, 4, P),
                                     pS[:, c0 * P:c0 * P + 4 * P])
                    sACT.inc(cc, 1)

        if not phase_on("rout"):
            return emit_dump()
        with nc.Block() as blk:
            pe0, ac0, dv0 = sPE.n, sACT.n, sDVE.n

            @blk.gpsimd
            def _(g):
                cnt['gw'] = sQ.inc(g.dma_start(arena[0:8, GWSB * SLOT:GWSB * SLOT + C],
                                               gw_d.ap()), 16)

            @blk.sync
            def _(sy):
                cnt['gb'] = sS.inc(sy.dma_start(fs[0:1, 0:8], gb_d.ap()), 16)

            @blk.tensor
            def _(t):
                t.wait_ge(sQ.h, cnt['gw'])
                mm = None
                for c in range(16):
                    mm = t.matmul(pT[:, c * 8:(c + 1) * 8],
                                  arena[0:8, GWSB * SLOT + c * P:GWSB * SLOT + (c + 1) * P],
                                  eye_bf[0:8, 0:8], start=True, stop=True)
                sPE.inc(mm, 1)
                t.wait_ge(sS.h, cnt['gb'])
                mm = t.matmul(pT2[:, 0:8], ones_f[0:1, 0:P], fs[0:1, 0:8],
                              start=True, stop=True)
                sPE.inc(mm, 1)
                t.wait_ge(sDVE.h, dv0 + 1)
                mm = None
                for c in range(16):
                    for tch in range(4):
                        mm = t.matmul(pS[0:8, tch * 512:(tch + 1) * 512],
                                      gwt_sb[:, c * 8:(c + 1) * 8],
                                      sl(LN1FM[c], tch * 512, 512),
                                      start=(c == 0), stop=(c == 15))
                sPE.inc(mm, 1)
                for tb in range(NT):
                    t.wait_ge(sACT.h, ac0 + 1 + tb)
                    mm = t.matmul(pT[:, 0:8],
                                  arena[0:8, GLOG * SLOT + tb * P:GLOG * SLOT + (tb + 1) * P],
                                  eye_bf[0:8, 0:8], start=True, stop=True)
                    sPE.inc(mm, 1)
                    t.wait_ge(sDVE.h, dv0 + 1 + 10 * (tb + 1))
                    mm = t.matmul(pS[0:8, tb * P:(tb + 1) * P],
                                  we_all[:, tb * 8:(tb + 1) * 8], eye_f[:, :],
                                  start=True, stop=True)
                    sPE.inc(mm, 1)

            @blk.vector
            def _(v):
                v.wait_ge(sPE.h, pe0 + 1)
                vv = v.tensor_copy(gwt_sb[:, :], pT[:, 0:P])
                sDVE.inc(vv, 1)
                for tb in range(NT):
                    v.wait_ge(sACT.h, ac0 + 2 + tb)
                    sc_ = rt_sb[:, 0:8]
                    vedge(v, v.tensor_add(rt_sb[:, 8:16], sc_, pT2[:, 0:8]))
                    vedge(v, v.reduce_max(rt_sb[:, 16:17], rt_sb[:, 8:16], axis=AX.X))
                    vedge(v, v.tensor_scalar(rt_sb[:, 24:32], rt_sb[:, 8:16],
                                             rt_sb[:, 16:17], None, ALU.is_ge))
                    vedge(v, v.scalar_tensor_tensor(rt_sb[:, 32:40], rt_sb[:, 24:32],
                                                    cpool[:, CP_NEG:CP_NEG + 1],
                                                    rt_sb[:, 8:16], ALU.mult, ALU.add))
                    vedge(v, v.reduce_max(rt_sb[:, 17:18], rt_sb[:, 32:40], axis=AX.X))
                    vedge(v, v.tensor_scalar(rt_sb[:, 24:32], rt_sb[:, 8:16],
                                             rt_sb[:, 17:18], None, ALU.is_ge))
                    vedge(v, v.tensor_mul(rt_sb[:, 32:40], rt_sb[:, 24:32], sc_))
                    ss = v.reduce_sum(rt_sb[:, 18:19], rt_sb[:, 32:40], axis=AX.X)
                    sR.inc(ss, 1)
                    v.wait_ge(sR.h, sR.n)
                    rr = v.reciprocal(rt_sb[:, 19:20], rt_sb[:, 18:19])
                    sR.inc(rr, 1)
                    v.wait_ge(sR.h, sR.n)
                    vv = v.tensor_scalar(we_all[:, tb * 8:(tb + 1) * 8],
                                         rt_sb[:, 32:40], rt_sb[:, 19:20], None,
                                         ALU.mult)
                    sDVE.inc(vv, 10)

            @blk.scalar
            def _(sc):
                sc.wait_ge(sPE.h, pe0 + 3)
                cc = sc.copy(arena[0:8, GLOG * SLOT:GLOG * SLOT + T], pS[0:8, 0:T])
                sACT.inc(cc, 1)
                for tb in range(NT):
                    sc.wait_ge(sPE.h, pe0 + 4 + 2 * tb)
                    cc = sc.activation(rt_sb[:, 0:8], pT[:, 0:8], AF.Sigmoid)
                    sACT.inc(cc, 1)
                sc.wait_ge(sPE.h, pe0 + 5 + 2 * (NT - 1))
                cc = sc.copy(arena[0:8, WER8 * SLOT:WER8 * SLOT + T], pS[0:8, 0:T])
                cnt['we_r8'] = sACT.inc(cc, 1)

        if not phase_on("ffn"):
            return emit_dump()
        jobs = []
        jobs.append(dict(name="shA", up=sgw_bf.ap()[0:HSH, :],
                         dn=lambda kb: sdw_bf.ap()[:, kb * P:(kb + 1) * P],
                         wcnt='sgw', dcnt='sdw', expert=None, first=True))
        jobs.append(dict(name="shB", up=sgw_bf.ap()[HSH:2 * HSH, :],
                         dn=lambda kb: sdw_bf.ap()[:, HEX + kb * P:HEX + (kb + 1) * P],
                         wcnt='sgw', dcnt='sdw', expert=None, first=False))
        for e in range(NE):
            jobs.append(dict(name=f"e{e}", up=up_bf.ap()[e],
                             dn=lambda kb, e=e: dw_bf.ap()[e][:, kb * P:(kb + 1) * P],
                             wcnt=f'up{e}', dcnt=f'dw{e}', expert=e, first=False))

        RW = [512, 512, 512, 512, 512, 256]

        for ph in range(2):
            for ji, job in enumerate(jobs):
                with nc.Block() as blk:
                    pe0, ac0, dv0, s0 = sPE.n, sACT.n, sDVE.n, sS.n
                    exp = job['expert']
                    npair = 11
                    pe_webc = 1 if exp is not None else 0
                    n_ln1h = 16 if ji == 0 else 0
                    range_mark = {}

                    @blk.sync
                    def _(sy, job=job, ji=ji, ph=ph, pe0=pe0, s0=s0, n_ln1h=n_ln1h,
                          range_mark=range_mark, pe_webc=pe_webc):
                        if ji == 0:
                            sy.wait_ge(sS.h, cnt['ln1fm_stored'])
                            for c in range(16):
                                sS.inc(sy.dma_start(
                                    sl(*LN1H[c], ln=1024),
                                    ln1fm_d.ap()[c][:, ph * 1024:(ph + 1) * 1024]), 16)
                        sy.wait_ge(sQ.h, cnt[job['dcnt']])
                        for kb in range(11):
                            sS.inc(sy.dma_start(sl(DWT[kb]), job['dn'](kb),
                                                transpose=True), 16)
                        cnt['dwT_done'] = sS.n
                        sy.wait_ge(sQ.h, cnt[job['wcnt']])
                        for r in range(6):
                            if r >= 2:
                                lastpair = 2 * (r - 2) + 1
                                sy.wait_ge(sPE.h, pe0 + pe_webc + lastpair + 1)
                            rw = RW[r]
                            for c in range(16):
                                sS.inc(sy.dma_start(
                                    sl(*UPT[r % 2][c], ln=rw),
                                    job['up'][r * 512:r * 512 + rw, c * P:(c + 1) * P],
                                    transpose=True), 16)
                            range_mark[r] = sS.n

                    @blk.tensor
                    def _(t, job=job, ji=ji, ph=ph, pe0=pe0, ac0=ac0, dv0=dv0,
                          s0=s0, exp=exp, range_mark=range_mark, pe_webc=pe_webc,
                          n_ln1h=n_ln1h):
                        if exp is not None:
                            mm = None
                            for cq in range(2):
                                src = arena[0:8, WER8 * SLOT + ph * 1024 + cq * 512:
                                            WER8 * SLOT + ph * 1024 + (cq + 1) * 512]
                                mm = t.matmul([pT, pT2][cq][:, 0:512],
                                              sel8_sb[0:8, exp * P:(exp + 1) * P],
                                              src, start=True, stop=True)
                            sPE.inc(mm, 1)
                        for i in range(npair):
                            r = i // 2
                            t.wait_ge(sS.h, range_mark[r])
                            if ji == 0 and i == 0:
                                t.wait_ge(sS.h, s0 + 16 * n_ln1h)
                            if exp is not None and i == 0:
                                t.wait_ge(sACT.h, ac0 + 1)
                            if i >= 1:
                                t.wait_ge(sDVE.h, dv0 + i)
                            mm = None
                            for blkrow, pof in ((2 * i, 0), (2 * i + 1, 1024)):
                                sub = (blkrow % 4) * P
                                for c in range(16):
                                    lhs = sl(*UPT[r % 2][c], ln=512)
                                    lhs = lhs[:, sub:sub + P] if False else \
                                        arena[:, UPT[r % 2][c][0] * SLOT + UPT[r % 2][c][1] + sub:
                                              UPT[r % 2][c][0] * SLOT + UPT[r % 2][c][1] + sub + P]
                                    for ch in range(2):
                                        mm = t.matmul(
                                            pS[:, pof + ch * 512: pof + (ch + 1) * 512],
                                            lhs, sl(*LN1H[c], off2 := ch * 512, ln=512)
                                            if False else
                                            arena[:, LN1H[c][0] * SLOT + LN1H[c][1] + ch * 512:
                                                  LN1H[c][0] * SLOT + LN1H[c][1] + (ch + 1) * 512],
                                            start=(c == 0), stop=(c == 15))
                            sPE.inc(mm, 1)
                        t.wait_ge(sDVE.h, dv0 + npair)
                        t.wait_ge(sS.h, cnt['dwT_done'])
                        for tbl in range(8):
                            banks = ([pS[:, 0:512], pS[:, 512:1024], pS[:, 1024:1536],
                                      pS[:, 1536:2048]] if tbl % 2 == 0 else
                                     [pT[:, :], pT2[:, :], pV[:, 0:512], pV[:, 512:1024]])
                            if tbl >= 2:
                                if job['first']:
                                    t.wait_ge(sACT.h, ac0 + npair + (tbl - 1))
                                else:
                                    t.wait_ge(sDVE.h, dv0 + npair + (tbl - 1))
                            if tbl < 2:
                                t.wait_ge(sACT.h, ac0 + (1 if exp is not None else 0) + npair)
                            mm = None
                            for kb in range(11):
                                lhs = arena[:, HSL[kb][0] * SLOT + HSL[kb][1] + tbl * P:
                                            HSL[kb][0] * SLOT + HSL[kb][1] + (tbl + 1) * P]
                                for cc4 in range(4):
                                    mm = t.matmul(banks[cc4],
                                                  lhs,
                                                  sl(DWT[kb], cc4 * 512, 512),
                                                  start=(kb == 0), stop=(kb == 10))
                            sPE.inc(mm, 1)

                    @blk.scalar
                    def _(sc, job=job, exp=exp, pe0=pe0, ac0=ac0, dv0=dv0,
                          pe_webc=pe_webc):
                        if exp is not None:
                            sc.wait_ge(sPE.h, pe0 + 1)
                            sc.copy(sl(*WEBC, ln=512), pT[:, 0:512])
                            cc = sc.copy(arena[:, WEBC[0] * SLOT + WEBC[1] + 512:
                                               WEBC[0] * SLOT + WEBC[1] + 1024],
                                         pT2[:, 0:512])
                            sACT.inc(cc, 1)
                        for i in range(11):
                            sc.wait_ge(sPE.h, pe0 + pe_webc + i + 1)
                            cc = sc.activation(sl(*SGS, ln=1024), pS[:, 0:1024], AF.Silu)
                            sACT.inc(cc, 1)
                        if job['first']:
                            for tbl in range(8):
                                sc.wait_ge(sPE.h, pe0 + pe_webc + 11 + tbl + 1)
                                banks = ([pS[:, 0:512], pS[:, 512:1024], pS[:, 1024:1536],
                                          pS[:, 1536:2048]] if tbl % 2 == 0 else
                                         [pT[:, :], pT2[:, :], pV[:, 0:512], pV[:, 512:1024]])
                                cc = None
                                for cc4 in range(4):
                                    cc = sc.copy(sl(ACC[tbl], cc4 * 512, 512), banks[cc4])
                                sACT.inc(cc, 1)

                    @blk.vector
                    def _(v, job=job, exp=exp, pe0=pe0, ac0=ac0, dv0=dv0,
                          pe_webc=pe_webc):
                        for i in range(11):
                            v.wait_ge(sACT.h, ac0 + (1 if exp is not None else 0) + i + 1)
                            v.wait_ge(sPE.h, pe0 + pe_webc + i + 1)
                            hdst = arena[:, HSL[i][0] * SLOT + HSL[i][1]:
                                         HSL[i][0] * SLOT + HSL[i][1] + 1024]
                            if exp is not None:
                                vedge(v, v.tensor_mul(sl(*SWC, ln=1024),
                                                      sl(*SGS, ln=1024),
                                                      sl(*WEBC, ln=1024)))
                                vv = v.tensor_mul(hdst, sl(*SWC, ln=1024),
                                                  pS[:, 1024:2048])
                            else:
                                vv = v.tensor_mul(hdst, sl(*SGS, ln=1024),
                                                  pS[:, 1024:2048])
                            sDVE.inc(vv, 1)
                        if not job['first']:
                            for tbl in range(8):
                                v.wait_ge(sPE.h, pe0 + pe_webc + 11 + tbl + 1)
                                banks = ([pS[:, 0:512], pS[:, 512:1024], pS[:, 1024:1536],
                                          pS[:, 1536:2048]] if tbl % 2 == 0 else
                                         [pT[:, :], pT2[:, :], pV[:, 0:512], pV[:, 512:1024]])
                                vv = None
                                for cc4 in range(4):
                                    acs = sl(ACC[tbl], cc4 * 512, 512)
                                    vv = v.tensor_add(acs, acs, banks[cc4])
                                sDVE.inc(vv, 1)

            with nc.Block() as blk:
                pe0, ac0, dv0, s0 = sPE.n, sACT.n, sDVE.n, sS.n
                NDV2, NAC2 = 4, 2

                def dvm2(tbl, i):
                    return dv0 + NDV2 * tbl + i

                def acm2(tbl, i):
                    return ac0 + NAC2 * tbl + i

                @blk.sync
                def _(sy, ph=ph, s0=s0, dv0=dv0):
                    for tbl in range(8):
                        if tbl >= 1:
                            sy.wait_ge(sDVE.h, dvm2(tbl - 1, 1))
                        sS.inc(sy.dma_start(sl(LN2TM), ln1tm_d.ap()[ph * 8 + tbl]), 16)
                        sy.wait_ge(sDVE.h, dvm2(tbl, 4))
                        sS.inc(sy.dma_start(
                            out_d.ap()[(ph * 8 + tbl) * P:(ph * 8 + tbl + 1) * P, :],
                            fs[:, 2048:4096]), 16)
                    cnt[f'out_st{ph}'] = sS.n

                @blk.vector
                def _(v, ph=ph, s0=s0, ac0=ac0):
                    for tbl in range(8):
                        v.wait_ge(sS.h, s0 + 32 * tbl + 16)
                        if tbl >= 1:
                            v.wait_ge(sS.h, s0 + 32 * tbl)
                        vv = v.tensor_add(fs[:, 0:2048], sl(LN2TM), sl(ACC[tbl]))
                        sDVE.inc(vv, 1)
                        v.wait_ge(sDVE.h, sDVE.n)
                        vedge(v, v.reduce_sum(st_sb[:, 0:1], fs[:, 0:2048], axis=AX.X))
                        vedge(v, v.tensor_scalar(st_sb[:, 1:2], st_sb[:, 0:1],
                                                 cpool[:, CP_INV:CP_INV + 1], None,
                                                 ALU.mult))
                        vv = v.tensor_mul(st_sb[:, 2:3], st_sb[:, 0:1], st_sb[:, 1:2])
                        sDVE.inc(vv, 1)
                        v.wait_ge(sDVE.h, sDVE.n)
                        v.wait_ge(sACT.h, acm2(tbl, 1))
                        vedge(v, v.tensor_sub(st_sb[:, 4:5], st_sb[:, 3:4],
                                              st_sb[:, 2:3]))
                        vv = v.tensor_scalar(st_sb[:, 5:6], st_sb[:, 4:5],
                                             cpool[:, CP_INV:CP_INV + 1], None, ALU.mult)
                        sDVE.inc(vv, 1)
                        v.wait_ge(sDVE.h, sDVE.n)
                        v.wait_ge(sACT.h, acm2(tbl, 2))
                        rr = v.reciprocal(st_sb[:, 7:8], st_sb[:, 6:7])
                        sR.inc(rr, 1)
                        v.wait_ge(sR.h, sR.n)
                        vv = v.tensor_scalar(fs[:, 2048:4096], fs[:, 0:2048],
                                             st_sb[:, 1:2], st_sb[:, 7:8],
                                             ALU.subtract, ALU.mult)
                        sDVE.inc(vv, 1)

                @blk.scalar
                def _(sc, dv0=dv0):
                    for tbl in range(8):
                        sc.wait_ge(sDVE.h, dvm2(tbl, 1))
                        cc = sc.activation(pS[:, 0:2048], fs[:, 0:2048], AF.Square,
                                           accum_out=st_sb[:, 3:4])
                        sACT.inc(cc, 1)
                        sc.wait_ge(sDVE.h, dvm2(tbl, 3))
                        cc = sc.activation(st_sb[:, 6:7], st_sb[:, 5:6], AF.Sqrt,
                                           bias=cpool[:, CP_EPS:CP_EPS + 1],
                                           scale=cpool[:, CP_ONE:CP_ONE + 1])
                        sACT.inc(cc, 1)

        with nc.Block() as blk:
            @blk.sync
            def _(sy):
                sy.wait_ge(sS.h, cnt['out_st1'])
                sy.wait_ge(sS.h, cnt['out_st0'])

    return nc



_PROG_CACHE = {}


def _get_prog():
    if 'nc' not in _PROG_CACHE:
        _PROG_CACHE['nc'] = build_program()
    return _PROG_CACHE['nc']


def kernel(L0_attn_x, L0_attn_Wq, L0_attn_bq, L0_attn_Wk, L0_attn_bk,
           L0_attn_Wv, L0_attn_bv, L0_attn_Wo, L0_attn_bo,
           L0_ffn_gate_w, L0_ffn_gate_bias, L0_ffn_shared_gw, L0_ffn_shared_dw,
           L0_ffn_up_w, L0_ffn_down_w):
    x = np.asarray(L0_attn_x, dtype=np.float32)

    perm = np.concatenate([np.arange(0, D, 2), np.arange(1, D, 2)])
    qcols = np.concatenate([h * D + perm for h in range(HQ)])
    kcols = np.concatenate([h * D + perm for h in range(HKV)])
    wq = np.ascontiguousarray(np.asarray(L0_attn_Wq, np.float32)[:, qcols])
    bq = np.ascontiguousarray(np.asarray(L0_attn_bq, np.float32)[qcols])[None, :]
    wk = np.ascontiguousarray(np.asarray(L0_attn_Wk, np.float32)[:, kcols])
    bk = np.ascontiguousarray(np.asarray(L0_attn_bk, np.float32)[kcols])[None, :]
    wv = np.ascontiguousarray(np.asarray(L0_attn_Wv, np.float32))
    bv = np.ascontiguousarray(np.asarray(L0_attn_bv, np.float32))[None, :]
    wo = np.ascontiguousarray(np.asarray(L0_attn_Wo, np.float32))
    bo = np.ascontiguousarray(
        np.asarray(L0_attn_bo, np.float32).reshape(NT, P).T)

    gw = np.ascontiguousarray(np.asarray(L0_ffn_gate_w, np.float32))
    gb = np.ascontiguousarray(np.asarray(L0_ffn_gate_bias, np.float32))[None, :]

    sgw = np.asarray(L0_ffn_shared_gw, np.float32)
    yb = sgw[0:HSH].reshape(22, P, C)
    gbk = sgw[HSH:2 * HSH].reshape(22, P, C)
    sgw_p = np.ascontiguousarray(
        np.stack([gbk, yb], axis=1).reshape(2 * HSH, C))
    sdw = np.ascontiguousarray(np.asarray(L0_ffn_shared_dw, np.float32))

    upw = np.asarray(L0_ffn_up_w, np.float32)
    gpart = upw[:, 0:HEX].reshape(NE, 11, P, C)
    upart = upw[:, HEX:2 * HEX].reshape(NE, 11, P, C)
    upw_p = np.ascontiguousarray(
        np.stack([gpart, upart], axis=2).reshape(NE, 2 * HEX, C))
    dww = np.ascontiguousarray(np.asarray(L0_ffn_down_w, np.float32))

    inv = (1.0 / (10000.0 ** (np.arange(0, D, 2, dtype=np.float32) / D))).astype(np.float32)
    pos = np.arange(T, dtype=np.float32)
    fr = np.outer(pos, inv).astype(np.float32)
    cosp = np.ascontiguousarray(
        np.cos(fr).astype(np.float32).reshape(NT, P, 32).transpose(1, 0, 2
                                                                   ).reshape(P, NT * 32)).astype(bfloat16)
    sinp = np.ascontiguousarray(
        np.sin(fr).astype(np.float32).reshape(NT, P, 32).transpose(1, 0, 2
                                                                   ).reshape(P, NT * 32)).astype(bfloat16)

    eyeb = np.eye(P, dtype=np.float32).astype(bfloat16)
    eyef = np.eye(P, dtype=np.float32)
    ub = np.zeros((P, P), dtype=np.float32)
    ub[np.triu_indices(P, 1)] = -1e9
    ubb = ub.astype(bfloat16)
    sel8 = np.zeros((8, 8 * P), dtype=np.float32)
    for e in range(8):
        sel8[e, e * P:(e + 1) * P] = 1.0
    sel8 = sel8.astype(bfloat16)

    shared = dict(wq=wq, bq=bq, wk=wk, bk=bk, wv=wv, bv=bv, wo=wo, bo_c=bo,
                  gw=gw, gb=gb, sgw=sgw_p, sdw=sdw, upw=upw_p, dww=dww,
                  cosp=cosp, sinp=sinp, eyeb=eyeb, eyef=eyef, ubias=ubb,
                  sel8=sel8)
    in_maps = []
    for c in range(8):
        m = dict(shared)
        m['x'] = np.ascontiguousarray(x[c])
        in_maps.append(m)

    nc = _get_prog()
    res = run_bass_kernel_spmd(nc, in_maps, core_ids=list(range(8)))
    globals()['LAST'] = res
    return np.stack([res.results[i]['out'] for i in range(8)], axis=0)


# version 02
# speedup vs baseline: 1.0076x; 1.0076x over previous
import numpy as np
import ml_dtypes
from contextlib import ExitStack

import concourse.bass as bass
import concourse.mybir as mybir
from concourse.bass_utils import run_bass_kernel_spmd

F32 = mybir.dt.float32
BF16 = mybir.dt.bfloat16
AF = mybir.ActivationFunctionType
ALU = mybir.AluOpType
AX = mybir.AxisListType
bfloat16 = ml_dtypes.bfloat16

P = 128
T = 2048
C = 2048
NT = 16
HQ, HKV, D = 32, 8, 64
KVW = HKV * D
NE = 8
HEX = 1408
HSH = 2816
EXP_C = 9.0
EPS = 1e-5

NSLOT = 45
SLOT = 2048

XFM = list(range(16))
QTM = list(range(16, 32))
KTM = [(32 + tb // 4, (tb % 4) * 512) for tb in range(NT)]
VTM = [(36 + tb // 4, (tb % 4) * 512) for tb in range(NT)]
WST = [(40 + c // 4, (c % 4) * 512) for c in range(16)]
SCR44 = 44
KDUP = list(range(8))
QFM = [8, 9, 10, 11, 12, 13, 14, 15, 32, 33, 34, 35, 40, 41, 42, 43]
PTR = 16
PTM = [24, 25]
ATP = list(range(16, 32))
WOS = [0, 2]
LN1FM = [0, 1, 2, 3, 4, 5, 6, 7, 36, 37, 38, 39, 40, 41, 42, 43]
LN1TMS = [44, 34]
GWSB = 8
GLOG = 9
WER8 = 43
LN1H = [(c // 2, (c % 2) * 1024) for c in range(16)]
ACC = list(range(8, 16))
HSL = [(16 + kb // 2, (kb % 2) * 1024) for kb in range(11)]
DWT = list(range(22, 33))
UPT = [[(33 + c // 4, (c % 4) * 512) for c in range(16)],
       [(37 + c // 4, (c % 4) * 512) for c in range(16)]]
WEBC = (41, 0)
SGS = (41, 1024)
SWC = (42, 0)
LN2TM = 44

CP_EXPB, CP_SCL, CP_NEG, CP_INV, CP_EPS, CP_ONE = 0, 1, 2, 3, 4, 5


class Sem:
    def __init__(self, nc, es, name):
        self.h = es.enter_context(nc.semaphore(name))
        self.n = 0

    def inc(self, inst, k):
        inst.then_inc(self.h, k)
        self.n += k
        return self.n


def build_program(stop_phase=None):
    import os
    if stop_phase is None:
        stop_phase = os.environ.get("KRN_STOP_PHASE", "all")
    PHASES = ["b0", "b1", "pk", "pv", "pq1", "proj", "ktr", "qtr", "attn",
              "wo", "ln1", "rout", "ffn", "all"]
    plim = PHASES.index(stop_phase)

    def phase_on(name):
        return PHASES.index(name) <= plim

    nc = bass.Bass("TRN2", target_bir_lowering=False)

    debug_tap = plim < PHASES.index("all")

    x_in = nc.dram_tensor("x", [T, C], F32, kind="ExternalInput")
    wq_d = nc.dram_tensor("wq", [C, C], F32, kind="ExternalInput")
    bq_d = nc.dram_tensor("bq", [1, C], F32, kind="ExternalInput")
    wk_d = nc.dram_tensor("wk", [C, KVW], F32, kind="ExternalInput")
    bk_d = nc.dram_tensor("bk", [1, KVW], F32, kind="ExternalInput")
    wv_d = nc.dram_tensor("wv", [C, KVW], F32, kind="ExternalInput")
    bv_d = nc.dram_tensor("bv", [1, KVW], F32, kind="ExternalInput")
    wo_d = nc.dram_tensor("wo", [C, C], F32, kind="ExternalInput")
    bo_d = nc.dram_tensor("bo_c", [P, NT], F32, kind="ExternalInput")
    gw_d = nc.dram_tensor("gw", [NE, C], F32, kind="ExternalInput")
    gb_d = nc.dram_tensor("gb", [1, NE], F32, kind="ExternalInput")
    sgw_d = nc.dram_tensor("sgw", [2 * HSH, C], F32, kind="ExternalInput")
    sdw_d = nc.dram_tensor("sdw", [C, HSH], F32, kind="ExternalInput")
    upw_d = nc.dram_tensor("upw", [NE, 2 * HEX, C], F32, kind="ExternalInput")
    dww_d = nc.dram_tensor("dww", [NE, C, HEX], F32, kind="ExternalInput")
    cos_d = nc.dram_tensor("cosp", [P, NT * 32], BF16, kind="ExternalInput")
    sin_d = nc.dram_tensor("sinp", [P, NT * 32], BF16, kind="ExternalInput")
    eyeb_d = nc.dram_tensor("eyeb", [P, P], BF16, kind="ExternalInput")
    eyef_d = nc.dram_tensor("eyef", [P, P], F32, kind="ExternalInput")
    ub_d = nc.dram_tensor("ubias", [P, P], BF16, kind="ExternalInput")
    sel8_d = nc.dram_tensor("sel8", [8, 8 * P], BF16, kind="ExternalInput")
    out_d = nc.dram_tensor("out", [T, C], F32, kind="ExternalOutput")

    x_bf = nc.dram_tensor("x_bfs", [T, C], BF16, kind="Internal")
    sgw_bf = nc.dram_tensor("sgw_bfs", [2 * HSH, C], BF16, kind="Internal")
    sdw_bf = nc.dram_tensor("sdw_bfs", [C, HSH], BF16, kind="Internal")
    up_bf = nc.dram_tensor("up_bfs", [NE, 2 * HEX, C], BF16, kind="Internal")
    dw_bf = nc.dram_tensor("dw_bfs", [NE, C, HEX], BF16, kind="Internal")
    ln1tm_d = nc.dram_tensor("ln1tm_d", [NT, P, C], BF16, kind="Internal")
    ln1fm_d = nc.dram_tensor("ln1fm_d", [NT, P, T], BF16, kind="Internal")
    attn_tap_h = int(os.environ.get("KRN_ATTN_TAP_H", "-1"))
    if attn_tap_h >= 0:
        debug_tap = True
    if debug_tap:
        tap_d = nc.dram_tensor("tap", [P, NSLOT * SLOT], BF16, kind="ExternalOutput")
        tapf_d = nc.dram_tensor("tapf", [P, 4096], F32, kind="ExternalOutput")
        tapw_d = nc.dram_tensor("tapw", [P, NT * 8], F32, kind="ExternalOutput")
        tapd_d = nc.dram_tensor("tapd", [P, 8], F32, kind="ExternalOutput")
        tapr_d = nc.dram_tensor("tapr", [P, 48], F32, kind="ExternalOutput")
        tapg_d = nc.dram_tensor("tapg", [P, 2 * P], BF16, kind="ExternalOutput")

    es = ExitStack()
    with es:
        arena = es.enter_context(nc.sbuf_tensor("arena", [P, NSLOT * SLOT], BF16))
        fs = es.enter_context(nc.sbuf_tensor("fs", [P, 4096], F32))
        browt = es.enter_context(nc.sbuf_tensor("browt", [P, SLOT], BF16))
        cos_sb = es.enter_context(nc.sbuf_tensor("cos_sb", [P, NT * 32], BF16))
        sin_sb = es.enter_context(nc.sbuf_tensor("sin_sb", [P, NT * 32], BF16))
        eye_bf = es.enter_context(nc.sbuf_tensor("eye_bf", [P, P], BF16))
        eye_f = es.enter_context(nc.sbuf_tensor("eye_f", [P, P], F32))
        ub_sb = es.enter_context(nc.sbuf_tensor("ub_sb", [P, P], BF16))
        sel8_sb = es.enter_context(nc.sbuf_tensor("sel8_sb", [8, 8 * P], BF16))
        ones_bf = es.enter_context(nc.sbuf_tensor("ones_bf", [P, P], BF16))
        ones_f = es.enter_context(nc.sbuf_tensor("ones_f", [1, P], F32))
        cpool = es.enter_context(nc.sbuf_tensor("cpool", [P, 8], F32))
        bo_sb = es.enter_context(nc.sbuf_tensor("bo_sb", [P, NT], F32))
        gwt_sb = es.enter_context(nc.sbuf_tensor("gwt_sb", [P, P], BF16))
        dn_sb = es.enter_context(nc.sbuf_tensor("dn_sb", [P, 2], F32))
        rc_sb = es.enter_context(nc.sbuf_tensor("rc_sb", [P, 1], F32))
        diag_sb = es.enter_context(nc.sbuf_tensor("diag_sb", [P, 2 * P], BF16))
        st_sb = es.enter_context(nc.sbuf_tensor("st_sb", [P, 8], F32))
        rt_sb = es.enter_context(nc.sbuf_tensor("rt_sb", [P, 48], F32))
        we_all = es.enter_context(nc.sbuf_tensor("we_all", [P, NT * 8], F32))
        pS = es.enter_context(nc.psum_tensor("pS", [P, 2048], F32))
        pT = es.enter_context(nc.psum_tensor("pT", [P, 512], F32))
        pT2 = es.enter_context(nc.psum_tensor("pT2", [P, 512], F32))
        pV = es.enter_context(nc.psum_tensor("pV", [P, 1024], F32))

        sQ = Sem(nc, es, "sQ")
        sS = Sem(nc, es, "sS")
        sPE = Sem(nc, es, "sPE")
        sACT = Sem(nc, es, "sACT")
        sDVE = Sem(nc, es, "sDVE")
        sR = Sem(nc, es, "sR")

        def vedge(v, inst):
            sR.inc(inst, 1)
            v.wait_ge(sR.h, sR.n)

        def sl(i, off=0, ln=SLOT):
            return arena[:, i * SLOT + off: i * SLOT + off + ln]

        def strided(view, stride, count, width):
            return bass.AP(view.tensor, view.offset,
                           [view.ap[0], [stride, count], [1, width]])

        cnt = {}

        def emit_dump():
            with nc.Block() as blk:
                @blk.sync
                def _(sy):
                    sS.inc(sy.dma_start(tap_d.ap(), arena[:, :]), 16)
                    sS.inc(sy.dma_start(tapf_d.ap(), fs[:, :]), 16)
                    sS.inc(sy.dma_start(tapw_d.ap(), we_all[:, :]), 16)
                    sS.inc(sy.dma_start(tapd_d.ap()[:, 0:2], dn_sb[:, :], allow_slow_non_contiguous=True), 16)
                    sS.inc(sy.dma_start(tapd_d.ap()[:, 2:3], rc_sb[:, :], allow_slow_non_contiguous=True), 16)
                    sS.inc(sy.dma_start(tapg_d.ap(), diag_sb[:, :]), 16)
                    sS.inc(sy.dma_start(tapr_d.ap(), rt_sb[:, :]), 16)
                    sy.wait_ge(sS.h, sS.n)
            return nc

        with nc.Block() as blk:
            @blk.gpsimd
            def _(g):
                cnt['x_bf'] = sQ.inc(g.dma_start(x_bf.ap(), x_in.ap()), 16)
                cnt['sgw'] = sQ.inc(g.dma_start(sgw_bf.ap(), sgw_d.ap()), 16)
                cnt['sdw'] = sQ.inc(g.dma_start(sdw_bf.ap(), sdw_d.ap()), 16)
                for e in range(NE):
                    cnt[f'up{e}'] = sQ.inc(g.dma_start(up_bf.ap()[e], upw_d.ap()[e]), 16)
                    cnt[f'dw{e}'] = sQ.inc(g.dma_start(dw_bf.ap()[e], dww_d.ap()[e]), 16)

            @blk.sync
            def _(sy):
                sS.inc(sy.dma_start(cos_sb[:, :], cos_d.ap()), 16)
                sS.inc(sy.dma_start(sin_sb[:, :], sin_d.ap()), 16)
                sS.inc(sy.dma_start(eye_bf[:, :], eyeb_d.ap()), 16)
                sS.inc(sy.dma_start(eye_f[:, :], eyef_d.ap()), 16)
                sS.inc(sy.dma_start(ub_sb[:, :], ub_d.ap()), 16)
                sS.inc(sy.dma_start(sel8_sb[0:8, :], sel8_d.ap()), 16)
                sS.inc(sy.dma_start(bo_sb[:, :], bo_d.ap()), 16)
                sS.inc(sy.dma_start(fs[0:1, 2048:4096], bq_d.ap()), 16)
                sS.inc(sy.dma_start(fs[0:1, 0:KVW], bk_d.ap()), 16)
                cnt['consts'] = sS.inc(sy.dma_start(fs[0:1, 512:1024], bv_d.ap()), 16)

            @blk.vector
            def _(v):
                v.memset(ones_bf[:, :], 1.0)
                v.memset(ones_f[0:1, :], 1.0)
                v.memset(cpool[:, CP_EXPB:CP_EXPB + 1], -EXP_C)
                v.memset(cpool[:, CP_SCL:CP_SCL + 1], 0.125)
                v.memset(cpool[:, CP_NEG:CP_NEG + 1], -1e9)
                v.memset(cpool[:, CP_INV:CP_INV + 1], 1.0 / C)
                v.memset(cpool[:, CP_EPS:CP_EPS + 1], EPS)
                v.memset(cpool[:, CP_ONE:CP_ONE + 1], 1.0)

        if not phase_on("b1"):
            return emit_dump()
        with nc.Block() as blk:
            @blk.sync
            def _(sy):
                sy.wait_ge(sQ.h, cnt['x_bf'])
                for c in range(16):
                    sS.inc(sy.dma_start(sl(XFM[c]), x_bf.ap()[:, c * P:(c + 1) * P],
                                        transpose=True), 16)
                cnt['xfm'] = sS.n

            @blk.gpsimd
            def _(g):
                dst = arena[:, 40 * SLOT: 40 * SLOT + 16 * 512].rearrange(
                    "p (c s) -> p c s", s=512)
                src = wk_d.ap().rearrange("(c p) s -> p c s", p=P)
                cnt['wk'] = sQ.inc(g.dma_start(dst, src), 16)

            @blk.scalar
            def _(sc):
                sc.wait_ge(sS.h, cnt['consts'])
                cc = sc.copy(browt[0:1, 0:1024], fs[0:1, 0:1024])
                cnt['bias_kv_hi'] = sACT.inc(cc, 1)

            @blk.vector
            def _(v):
                v.wait_ge(sACT.h, cnt['bias_kv_hi'])
                vv = v.tensor_sub(fs[0:1, 1024:2048], fs[0:1, 0:1024],
                                  browt[0:1, 0:1024])
                cnt['bias_kv_lo_f'] = sDVE.inc(vv, 1)

        with nc.Block() as blk:
            @blk.scalar
            def _(sc):
                sc.wait_ge(sDVE.h, cnt['bias_kv_lo_f'])
                cc = sc.copy(browt[32:33, 0:1024], fs[0:1, 1024:2048])
                cnt['bias_kv_lo32'] = sACT.inc(cc, 1)

            @blk.gpsimd
            def _(g):
                g.wait_ge(sS.h, cnt['xfm'])
                g.wait_ge(sACT.h, cnt['bias_kv_lo32'])
                g.dma_start(browt[1:2, 0:1024], browt[32:33, 0:1024]).then_inc(sQ.h, 16)
                sQ.n += 16
                cnt['bias_kv_done'] = sQ.n

        def emit_prep_bq():
            with nc.Block() as blk:
                @blk.scalar
                def _(sc):
                    cc = sc.copy(browt[0:1, 0:C], fs[0:1, 2048:4096])
                    cnt['bias_q_hi'] = sACT.inc(cc, 1)

                @blk.vector
                def _(v):
                    v.wait_ge(sACT.h, cnt['bias_q_hi'])
                    vv = v.tensor_sub(fs[0:1, 0:2048], fs[0:1, 2048:4096],
                                      browt[0:1, 0:C])
                    cnt['bias_q_lo_f'] = sDVE.inc(vv, 1)

            with nc.Block() as blk:
                @blk.scalar
                def _(sc):
                    sc.wait_ge(sDVE.h, cnt['bias_q_lo_f'])
                    cc = sc.copy(browt[32:33, 0:C], fs[0:1, 0:2048])
                    cnt['bias_q_lo32'] = sACT.inc(cc, 1)

                @blk.gpsimd
                def _(g):
                    g.wait_ge(sACT.h, cnt['bias_q_lo32'])
                    g.dma_start(browt[1:2, 0:C], browt[32:33, 0:C]).then_inc(sQ.h, 16)
                    sQ.n += 16
                    cnt['bias_q_done'] = sQ.n

        def emit_rope(v, scr512, dst512, tb):
            sv = scr512.rearrange("p (h two i) -> p h two i", two=2, i=32)
            dv = dst512.rearrange("p (h two i) -> p h two i", two=2, i=32)
            cs = cos_sb[:, tb * 32:(tb + 1) * 32].rearrange("p (o i) -> p o i", o=1)
            sn = sin_sb[:, tb * 32:(tb + 1) * 32].rearrange("p (o i) -> p o i", o=1)
            csb = bass.AP(cs.tensor, cs.offset, [cs.ap[0], [0, 8], cs.ap[2]])
            snb = bass.AP(sn.tensor, sn.offset, [sn.ap[0], [0, 8], sn.ap[2]])
            ev, od = sv[:, :, 0, :], sv[:, :, 1, :]
            t1 = fs[:, 0:256].rearrange("p (h i) -> p h i", i=32)
            t2 = fs[:, 256:512].rearrange("p (h i) -> p h i", i=32)
            v.tensor_mul(t1, ev, csb)
            vedge(v, v.tensor_mul(t2, od, snb))
            vedge(v, v.tensor_sub(dv[:, :, 0, :], t1, t2))
            v.tensor_mul(t1, od, csb)
            vedge(v, v.tensor_mul(t2, ev, snb))
            return v.tensor_add(dv[:, :, 1, :], t1, t2)

        def emit_proj_pass(wkey, bias_rows, bias_key, dst_fn, do_rope, next_w_load):
            with nc.Block() as blk:
                pe0, ac0, dv0 = sPE.n, sACT.n, sDVE.n

                @blk.gpsimd
                def _(g):
                    if next_w_load is not None:
                        name, src_ap = next_w_load
                        g.wait_ge(sPE.h, pe0 + NT)
                        dst = arena[:, 40 * SLOT: 40 * SLOT + 16 * 512].rearrange(
                            "p (c s) -> p c s", s=512)
                        cnt[name] = sQ.inc(g.dma_start(dst, src_ap), 16)

                @blk.tensor
                def _(t):
                    t.wait_ge(sS.h, cnt['xfm'])
                    t.wait_ge(sQ.h, cnt[wkey])
                    t.wait_ge(sQ.h, cnt[bias_key])
                    for tb in range(NT):
                        q4 = tb % 4
                        ps = pS[:, q4 * 512:(q4 + 1) * 512]
                        if tb >= 4:
                            t.wait_ge(sACT.h, ac0 + (tb - 3))
                        for c in range(16):
                            t.matmul(ps, sl(XFM[c], tb * P, P), sl(*WST[c], ln=512),
                                     start=(c == 0), stop=False)
                        mm = t.matmul(ps, bias_rows[0], bias_rows[1],
                                      start=False, stop=True)
                        sPE.inc(mm, 1)

                @blk.scalar
                def _(sc):
                    for tb in range(NT):
                        q4 = tb % 4
                        ps = pS[:, q4 * 512:(q4 + 1) * 512]
                        sc.wait_ge(sPE.h, pe0 + tb + 1)
                        if do_rope:
                            if tb >= 2:
                                sc.wait_ge(sDVE.h, dv0 + 6 * (tb - 1))
                            cc = sc.copy(sl(SCR44, (tb % 2) * 512, 512), ps)
                        else:
                            cc = sc.copy(dst_fn(tb), ps)
                        sACT.inc(cc, 1)

                @blk.vector
                def _(v):
                    if not do_rope:
                        return
                    for tb in range(NT):
                        v.wait_ge(sACT.h, ac0 + tb + 1)
                        vv = emit_rope(v, sl(SCR44, (tb % 2) * 512, 512),
                                       dst_fn(tb), tb)
                        sDVE.inc(vv, 6)

        if not phase_on("pk"):
            return emit_dump()
        bias_k = (ones_bf[0:2, 0:P], browt[0:2, 0:512])
        bias_v = (ones_bf[0:2, 0:P], browt[0:2, 512:1024])

        emit_proj_pass('wk', bias_k, 'bias_kv_done',
                       lambda tb: sl(*KTM[tb], ln=512), True,
                       ('wv', wv_d.ap().rearrange("(c p) s -> p c s", p=P)))
        if not phase_on("pv"):
            return emit_dump()
        emit_proj_pass('wv', bias_v, 'bias_kv_done',
                       lambda tb: sl(*VTM[tb], ln=512), False,
                       ('wq0', wq_d.ap()[:, 0:512].rearrange("(c p) s -> p c s", p=P)))
        if not phase_on("pq1"):
            return emit_dump()
        emit_prep_bq()
        for qq in range(4):
            if qq >= 1 and not phase_on("proj"):
                break
            bias_q = (ones_bf[0:2, 0:P], browt[0:2, qq * 512:(qq + 1) * 512])
            nxt = None
            if qq < 3:
                nxt = (f'wq{qq + 1}',
                       wq_d.ap()[:, (qq + 1) * 512:(qq + 2) * 512].rearrange(
                           "(c p) s -> p c s", p=P))
            emit_proj_pass(f'wq{qq}', bias_q, 'bias_q_done',
                           lambda tb, qq=qq: sl(QTM[tb], qq * 512, 512), True, nxt)

        if not phase_on("ktr"):
            return emit_dump()
        with nc.Block() as blk:
            pe0, dv0 = sPE.n, sDVE.n

            @blk.tensor
            def _(t):
                for tb in range(NT):
                    if tb >= 1:
                        t.wait_ge(sDVE.h, dv0 + 8 * tb)
                    b1 = [pT, pT2][tb % 2]
                    b2 = [pT2, pT][tb % 2]
                    ktile = sl(*KTM[tb], ln=512)
                    mm = None
                    for g in range(HKV):
                        lhs = ktile[:, g * 64:(g + 1) * 64]
                        bank = b1 if g < 4 else b2
                        ps = bank[:, (g % 4) * P:(g % 4) * P + P]
                        t.matmul(ps[0:64, :], lhs, eye_bf[:, :], start=True, stop=True)
                        mm = t.matmul(ps[64:128, :], lhs, eye_bf[:, :], start=True, stop=True)
                    sPE.inc(mm, 1)

            @blk.vector
            def _(v):
                for tb in range(NT):
                    v.wait_ge(sPE.h, pe0 + tb + 1)
                    b1 = [pT, pT2][tb % 2]
                    b2 = [pT2, pT][tb % 2]
                    vv = None
                    for g in range(HKV):
                        bank = b1 if g < 4 else b2
                        ps = bank[:, (g % 4) * P:(g % 4) * P + P]
                        vv = v.tensor_copy(sl(KDUP[g], tb * P, P), ps)
                    sDVE.inc(vv, 8)

        if not phase_on("qtr"):
            return emit_dump()
        with nc.Block() as blk:
            pe0, ac0, dv0 = sPE.n, sACT.n, sDVE.n

            @blk.tensor
            def _(t):
                for tb in range(NT):
                    for grp in range(4):
                        bank = pS[:, grp * 512:(grp + 1) * 512]
                        if tb >= 1:
                            if grp % 2 == 0:
                                t.wait_ge(sDVE.h, dv0 + (tb - 1) * 2 + grp // 2 + 1)
                            else:
                                t.wait_ge(sACT.h, ac0 + (tb - 1) * 2 + grp // 2 + 1)
                        mm = None
                        for j in range(4):
                            d = grp * 4 + j
                            mm = t.matmul(bank[:, j * P:(j + 1) * P],
                                          sl(QTM[tb], d * P, P), eye_bf[:, :],
                                          start=True, stop=True)
                        sPE.inc(mm, 1)

            @blk.vector
            def _(v):
                for tb in range(NT):
                    for grp in (0, 2):
                        bank = pS[:, grp * 512:(grp + 1) * 512]
                        v.wait_ge(sPE.h, pe0 + tb * 4 + grp + 1)
                        vv = None
                        for j in range(4):
                            d = grp * 4 + j
                            vv = v.tensor_copy(sl(QFM[d], tb * P, P),
                                               bank[:, j * P:(j + 1) * P])
                        sDVE.inc(vv, 1)

            @blk.scalar
            def _(sc):
                for tb in range(NT):
                    for grp in (1, 3):
                        bank = pS[:, grp * 512:(grp + 1) * 512]
                        sc.wait_ge(sPE.h, pe0 + tb * 4 + grp + 1)
                        cc = None
                        for j in range(4):
                            d = grp * 4 + j
                            cc = sc.copy(sl(QFM[d], tb * P, P),
                                         bank[:, j * P:(j + 1) * P])
                        sACT.inc(cc, 1)

        if not phase_on("attn"):
            return emit_dump()
        pe0a, ac0a, dv0a = sPE.n, sACT.n, sDVE.n
        g_chunks = []
        pe_items_all = []
        act_items_all = []
        dve_items_all = []
        for h in range(HQ):
            for qt in range(NT):
                klen = (qt + 1) * P
                chs = [(0, min(1024, klen))]
                if klen > 1024:
                    chs.append((1, klen - 1024))
                for (ch, cw) in chs:
                    g_chunks.append((h, qt, ch, cw))
                    pe_items_all.append(("S", h, qt, ch, cw))
                    act_items_all.append(("E", h, qt, ch, cw))
                dve_items_all.append(("DG", h, qt, len(chs)))
                if qt == 9:
                    pe_items_all.append(("PV", h, 0))
                    act_items_all.append(("O", h, 0))
                if qt >= 1:
                    pe_items_all.append(("PT", h, qt - 1))
                    dve_items_all.append(("EV", h, qt - 1, (qt + 3) // 4))
            pe_items_all.append(("PT", h, NT - 1))
            dve_items_all.append(("EV", h, NT - 1, (NT + 3) // 4))
            pe_items_all.append(("PV", h, 1))
            act_items_all.append(("O", h, 1))

        s_mark = {}
        pt_gmarks = {}
        pv_mark = {}
        m = 0
        gidx = 0
        for it in pe_items_all:
            if it[0] == "S":
                m += 1
                s_mark[(it[1], it[2], it[3])] = pe0a + m
            elif it[0] == "PT":
                ng = (it[2] + 4) // 4
                mk = []
                for _ in range(ng):
                    m += 1
                    mk.append(pe0a + m)
                pt_gmarks[(it[1], it[2])] = mk
            else:
                m += 1
                pv_mark[(it[1], it[2])] = pe0a + m

        exp_mark = {}
        oev_mark = {}
        a = 0
        for it in act_items_all:
            a += 1
            if it[0] == "E":
                exp_mark[(it[1], it[2], it[3])] = ac0a + a
            else:
                oev_mark[(it[1], it[2])] = ac0a + a

        diag_mark = {}
        evac_gmark = []
        evac_last = {}
        d = 0
        for it in dve_items_all:
            if it[0] == "DG":
                d += (1 if it[3] == 2 else 0) + 2
                diag_mark[(it[1], it[2])] = dv0a + d
            else:
                for _ in range(it[3]):
                    d += 1
                    evac_gmark.append(dv0a + d)
                evac_last[(it[1], it[2])] = dv0a + d

        last_ch = {qt: (1 if (qt + 1) * P > 1024 else 0) for qt in range(NT)}

        s_gci = {}
        for i_, (h_, q_, c_, w_) in enumerate(g_chunks):
            s_gci[(h_, q_, c_)] = i_
        g_chunks = [(h_, q_, c_, w_) for (h_, q_, c_, w_) in g_chunks]

        ptm_prev_user = {}
        dn_prev_user = {}
        last_pt_mark = {0: None, 1: None}
        prev_diag = None
        diag_prev_user = {}
        prev_oev = {}
        last_oev = None
        for it in pe_items_all:
            pass
        for h in range(HQ):
            for qt in range(NT):
                for ch in range(2 if (qt + 1) * P > 1024 else 1):
                    ptm_prev_user[(h, qt, ch)] = last_pt_mark[qt % 2]
                    dn_prev_user[(h, qt, ch)] = prev_diag
                diag_prev_user[(h, qt)] = last_pt_mark[qt % 2]
                prev_diag = diag_mark[(h, qt)]
                if qt >= 1:
                    last_pt_mark[(qt - 1) % 2] = pt_gmarks[(h, qt - 1)][-1]
                if qt == 9:
                    prev_oev[(h, 0)] = last_oev
                    last_oev = oev_mark[(h, 0)]
            last_pt_mark[(NT - 1) % 2] = pt_gmarks[(h, NT - 1)][-1]
            prev_oev[(h, 1)] = last_oev
            last_oev = oev_mark[(h, 1)]

        with nc.Block() as blk:
            @blk.tensor
            def _(t):
                ggl = [0]
                for idx, it in enumerate(pe_items_all):
                    if it[0] == "S":
                        _, h, qt, ch, cw = it
                        hp, g_ = h // 2, h // 4
                        base = (h % 2) * 64
                        ci = g_chunks.index((h, qt, ch)) if False else None
                        gci = s_gci[(h, qt, ch)]
                        ps = pS[:, (gci % 2) * 1024: (gci % 2) * 1024 + cw]
                        if gci >= 2:
                            ph_, pq_, pc_, _ = g_chunks[gci - 2]
                            t.wait_ge(sACT.h, exp_mark[(ph_, pq_, pc_)])
                        lhs = arena[base:base + 64,
                                    QFM[hp] * SLOT + qt * P: QFM[hp] * SLOT + (qt + 1) * P]
                        dpos = qt * P - ch * 1024
                        mm = None
                        pos = 0
                        while pos < cw:
                            w = min(512, cw - pos)
                            kcol = ch * 1024 + pos
                            rhs = arena[base:base + 64,
                                        KDUP[g_] * SLOT + kcol: KDUP[g_] * SLOT + kcol + w]
                            has_diag = (pos <= dpos < pos + w)
                            mm = t.matmul(ps[:, pos:pos + w], lhs, rhs,
                                          start=True, stop=not has_diag)
                            pos += w
                        if 0 <= dpos < cw:
                            mm = t.matmul(ps[:, dpos:dpos + P], eye_bf[:, :],
                                          ub_sb[:, :], start=False, stop=True)
                        sPE.inc(mm, 1)
                    elif it[0] == "PT":
                        _, h, qtp = it
                        nkb = qtp + 1
                        t.wait_ge(sDVE.h, diag_mark[(h, qtp)])
                        t.wait_ge(sACT.h, exp_mark[(h, qtp, last_ch[qtp])])
                        for g0 in range(0, nkb, 4):
                            gw_ = min(4, nkb - g0)
                            gi = ggl[0]
                            bank = [pT, pT2][gi % 2]
                            if gi >= 2:
                                t.wait_ge(sDVE.h, evac_gmark[gi - 2])
                            mm = None
                            for j in range(gw_):
                                kb = g0 + j
                                mm = t.matmul(bank[:, j * P:(j + 1) * P],
                                              sl(PTM[qtp % 2], kb * P, P),
                                              diag_sb[:, (qtp % 2) * P:(qtp % 2 + 1) * P],
                                              start=True, stop=True)
                            sPE.inc(mm, 1)
                            ggl[0] += 1
                    else:
                        _, h, half = it
                        g_ = h // 4
                        hp = h // 2
                        base = (h % 2) * 64
                        kmax = 8 if half == 0 else 16
                        lastq = 7 if half == 0 else 15
                        t.wait_ge(sDVE.h, evac_last[(h, lastq)])
                        prevo = prev_oev[(h, half)]
                        if prevo is not None:
                            t.wait_ge(sACT.h, prevo)
                        mm = None
                        for kb in range(kmax):
                            qs = max(0, kb * P - half * 1024)
                            vt = sl(*VTM[kb], ln=512)
                            lhs = vt[:, g_ * 64:(g_ + 1) * 64]
                            pos = qs
                            while pos < 1024:
                                w = min(512 - (pos % 512), 1024 - pos)
                                mm = t.matmul(pV[0:64, pos:pos + w], lhs,
                                              arena[:, PTR * SLOT + kb * 1024 + pos:
                                                    PTR * SLOT + kb * 1024 + pos + w],
                                              start=(kb == 0),
                                              stop=(kb == kmax - 1))
                                pos += w
                        sPE.inc(mm, 1)

            @blk.scalar
            def _(sc):
                for it in act_items_all:
                    if it[0] == "E":
                        _, h, qt, ch, cw = it
                        gci = s_gci[(h, qt, ch)]
                        ps = pS[:, (gci % 2) * 1024:(gci % 2) * 1024 + cw]
                        sc.wait_ge(sPE.h, s_mark[(h, qt, ch)])
                        pm = ptm_prev_user[(h, qt, ch)]
                        if pm is not None:
                            sc.wait_ge(sPE.h, pm)
                        dm = dn_prev_user[(h, qt, ch)]
                        if dm is not None and ch == 0:
                            sc.wait_ge(sDVE.h, dm)
                        cc = sc.activation(sl(PTM[qt % 2], ch * 1024, cw), ps,
                                           AF.Exp,
                                           bias=cpool[:, CP_EXPB:CP_EXPB + 1],
                                           scale=cpool[:, CP_SCL:CP_SCL + 1],
                                           accum_out=dn_sb[:, ch:ch + 1])
                        sACT.inc(cc, 1)
                    else:
                        _, h, half = it
                        hp = h // 2
                        base = (h % 2) * 64
                        sc.wait_ge(sPE.h, pv_mark[(h, half)])
                        obase = QFM[hp] * SLOT + half * 1024
                        sc.copy(arena[base:base + 64, obase:obase + 512],
                                pV[0:64, 0:512])
                        cc = sc.copy(arena[base:base + 64, obase + 512:obase + 1024],
                                     pV[0:64, 512:1024])
                        sACT.inc(cc, 1)

            @blk.vector
            def _(v):
                gi2 = [0]
                for it in dve_items_all:
                    if it[0] == "DG":
                        _, h, qt, nch = it
                        v.wait_ge(sACT.h, exp_mark[(h, qt, last_ch[qt])])
                        pm = diag_prev_user[(h, qt)]
                        if pm is not None:
                            v.wait_ge(sPE.h, pm)
                        k = 0
                        if nch == 2:
                            aa = v.tensor_add(st_sb[:, 0:1], dn_sb[:, 0:1],
                                              dn_sb[:, 1:2])
                            sR.inc(aa, 1)
                            v.wait_ge(sR.h, sR.n)
                            srcv = st_sb[:, 0:1]
                            k = 1
                        else:
                            srcv = dn_sb[:, 0:1]
                        rr = v.reciprocal(rc_sb[:, 0:1], srcv)
                        sR.inc(rr, 1)
                        v.wait_ge(sR.h, sR.n)
                        vv = v.tensor_scalar(
                            diag_sb[:, (qt % 2) * P:(qt % 2 + 1) * P],
                            eye_bf[:, :], rc_sb[:, 0:1], None, ALU.mult)
                        sDVE.inc(vv, k + 2)
                    else:
                        _, h, qtp, ng = it
                        for gg in range(ng):
                            v.wait_ge(sPE.h, pt_gmarks[(h, qtp)][gg])
                            gi = gi2[0]
                            bank = [pT, pT2][gi % 2]
                            gw_ = min(4, qtp + 1 - gg * 4)
                            dstb = arena[:, PTR * SLOT + (gg * 4) * 1024 + (qtp % 8) * P:
                                         PTR * SLOT + (gg * 4) * 1024 + (qtp % 8) * P + P]
                            dst = strided(dstb, 1024, gw_, P)
                            vv = v.tensor_copy(dst, bank[:, 0:gw_ * P])
                            sDVE.inc(vv, 1)
                            gi2[0] += 1


        with nc.Block() as blk:
            pe0, ac0, dv0, q0 = sPE.n, sACT.n, sDVE.n, sQ.n

            @blk.gpsimd
            def _(g):
                for cb in range(16):
                    if cb >= 2:
                        g.wait_ge(sPE.h, pe0 + cb - 1)
                    b = WOS[cb % 2]
                    dst = arena[:, b * SLOT: b * SLOT + 16 * P].rearrange(
                        "p (c s) -> p c s", s=P)
                    src = wo_d.ap()[:, cb * P:(cb + 1) * P].rearrange("(c p) s -> p c s", p=P)
                    sQ.inc(g.dma_start(dst, src), 16)

            @blk.tensor
            def _(t):
                for cb in range(16):
                    t.wait_ge(sQ.h, q0 + 16 * (cb + 1))
                    if cb >= 1:
                        t.wait_ge(sDVE.h, dv0 + cb)
                        t.wait_ge(sACT.h, ac0 + cb)
                    b = WOS[cb % 2]
                    mm = None
                    for cin in range(16):
                        lhs = sl(b, cin * P, P)
                        for tch in range(4):
                            mm = t.matmul(pS[:, tch * 512:(tch + 1) * 512], lhs,
                                          sl(QFM[cin], tch * 512, 512),
                                          start=(cin == 0), stop=(cin == 15))
                    sPE.inc(mm, 1)

            @blk.vector
            def _(v):
                for cb in range(16):
                    v.wait_ge(sPE.h, pe0 + cb + 1)
                    vv = None
                    for tch in (0, 1):
                        vv = v.tensor_scalar(sl(ATP[cb], tch * 512, 512),
                                             pS[:, tch * 512:(tch + 1) * 512],
                                             bo_sb[:, cb:cb + 1], None, ALU.add)
                    sDVE.inc(vv, 1)

            @blk.scalar
            def _(sc):
                for cb in range(16):
                    sc.wait_ge(sPE.h, pe0 + cb + 1)
                    cc = None
                    for tch in (2, 3):
                        cc = sc.add(sl(ATP[cb], tch * 512, 512),
                                    pS[:, tch * 512:(tch + 1) * 512],
                                    bo_sb[:, cb:cb + 1])
                    sACT.inc(cc, 1)

        if not phase_on("ln1"):
            return emit_dump()
        with nc.Block() as blk:
            pe0, ac0, dv0, s0 = sPE.n, sACT.n, sDVE.n, sS.n
            NDV, NAC = 5, 3

            def dvm(tb, i):
                return dv0 + NDV * tb + i

            def acm(tb, i):
                return ac0 + NAC * tb + i

            @blk.sync
            def _(sy):
                for tb in range(NT):
                    if tb >= 1:
                        sy.wait_ge(sDVE.h, dvm(tb - 1, 1))
                    sS.inc(sy.dma_start(fs[:, 2048:4096],
                                        x_in.ap()[tb * P:(tb + 1) * P, :]), 16)
                    sy.wait_ge(sDVE.h, dvm(tb, 4))
                    sS.inc(sy.dma_start(ln1tm_d.ap()[tb], sl(LN1TMS[tb % 2])), 16)
                sy.wait_ge(sDVE.h, dv0 + NDV * NT)
                sy.wait_ge(sACT.h, ac0 + NAC * NT)
                for c in range(16):
                    sS.inc(sy.dma_start(ln1fm_d.ap()[c], sl(LN1FM[c])), 16)
                cnt['ln1fm_stored'] = sS.n

            @blk.tensor
            def _(t):
                for tb in range(NT):
                    if tb >= 1:
                        t.wait_ge(sACT.h, acm(tb - 1, 1))
                        t.wait_ge(sDVE.h, dvm(tb - 1, 5))
                        t.wait_ge(sACT.h, acm(tb - 1, 3))
                    mm = None
                    for c in range(16):
                        mm = t.matmul(pS[:, c * P:(c + 1) * P],
                                      sl(ATP[c], tb * P, P), eye_bf[:, :],
                                      start=True, stop=True)
                    sPE.inc(mm, 1)
                    t.wait_ge(sDVE.h, dvm(tb, 4))
                    t.wait_ge(sACT.h, acm(tb, 1))
                    mm = None
                    for c in range(16):
                        mm = t.matmul(pS[:, c * P:(c + 1) * P],
                                      sl(LN1TMS[tb % 2], c * P, P), eye_bf[:, :],
                                      start=True, stop=True)
                    sPE.inc(mm, 1)

            @blk.vector
            def _(v):
                for tb in range(NT):
                    v.wait_ge(sPE.h, pe0 + 2 * tb + 1)
                    v.wait_ge(sS.h, s0 + 32 * tb + 16)
                    vv = None
                    for q4 in range(4):
                        vv = v.tensor_add(fs[:, q4 * 512:(q4 + 1) * 512],
                                          pS[:, q4 * 512:(q4 + 1) * 512],
                                          fs[:, 2048 + q4 * 512:2048 + (q4 + 1) * 512])
                    sDVE.inc(vv, 1)
                    v.wait_ge(sDVE.h, sDVE.n)
                    vedge(v, v.reduce_sum(st_sb[:, 0:1], fs[:, 0:2048], axis=AX.X))
                    vedge(v, v.tensor_scalar(st_sb[:, 1:2], st_sb[:, 0:1],
                                             cpool[:, CP_INV:CP_INV + 1], None, ALU.mult))
                    vv = v.tensor_mul(st_sb[:, 2:3], st_sb[:, 0:1], st_sb[:, 1:2])
                    sDVE.inc(vv, 1)
                    v.wait_ge(sDVE.h, sDVE.n)
                    v.wait_ge(sACT.h, acm(tb, 1))
                    vedge(v, v.tensor_sub(st_sb[:, 4:5], st_sb[:, 3:4], st_sb[:, 2:3]))
                    vv = v.tensor_scalar(st_sb[:, 5:6], st_sb[:, 4:5],
                                         cpool[:, CP_INV:CP_INV + 1], None, ALU.mult)
                    sDVE.inc(vv, 1)
                    v.wait_ge(sDVE.h, sDVE.n)
                    v.wait_ge(sACT.h, acm(tb, 2))
                    rr = v.reciprocal(st_sb[:, 7:8], st_sb[:, 6:7])
                    sR.inc(rr, 1)
                    v.wait_ge(sR.h, sR.n)
                    if tb >= 2:
                        v.wait_ge(sS.h, s0 + 32 * (tb - 2) + 32)
                    vv = v.tensor_scalar(sl(LN1TMS[tb % 2]), fs[:, 0:2048],
                                         st_sb[:, 1:2], st_sb[:, 7:8],
                                         ALU.subtract, ALU.mult)
                    sDVE.inc(vv, 1)
                    v.wait_ge(sPE.h, pe0 + 2 * tb + 2)
                    vv = None
                    for grp in (0, 2):
                        c0 = grp * 4
                        dstb = arena[:, LN1FM[c0] * SLOT + tb * P:
                                     LN1FM[c0] * SLOT + tb * P + P]
                        vv = v.tensor_copy(strided(dstb, SLOT, 4, P),
                                           pS[:, c0 * P:c0 * P + 4 * P])
                    sDVE.inc(vv, 1)

            @blk.scalar
            def _(sc):
                for tb in range(NT):
                    sc.wait_ge(sDVE.h, dvm(tb, 1))
                    cc = sc.activation(pS[:, 0:2048], fs[:, 0:2048], AF.Square,
                                       accum_out=st_sb[:, 3:4])
                    sACT.inc(cc, 1)
                    sc.wait_ge(sDVE.h, dvm(tb, 3))
                    cc = sc.activation(st_sb[:, 6:7], st_sb[:, 5:6], AF.Sqrt,
                                       bias=cpool[:, CP_EPS:CP_EPS + 1],
                                       scale=cpool[:, CP_ONE:CP_ONE + 1])
                    sACT.inc(cc, 1)
                    sc.wait_ge(sPE.h, pe0 + 2 * tb + 2)
                    cc = None
                    for grp in (1, 3):
                        c0 = grp * 4
                        dstb = arena[:, LN1FM[c0] * SLOT + tb * P:
                                     LN1FM[c0] * SLOT + tb * P + P]
                        cc = sc.copy(strided(dstb, SLOT, 4, P),
                                     pS[:, c0 * P:c0 * P + 4 * P])
                    sACT.inc(cc, 1)

        if not phase_on("rout"):
            return emit_dump()
        with nc.Block() as blk:
            pe0, ac0, dv0 = sPE.n, sACT.n, sDVE.n

            @blk.gpsimd
            def _(g):
                cnt['gw'] = sQ.inc(g.dma_start(arena[0:8, GWSB * SLOT:GWSB * SLOT + C],
                                               gw_d.ap()), 16)

            @blk.sync
            def _(sy):
                cnt['gb'] = sS.inc(sy.dma_start(fs[0:1, 0:8], gb_d.ap()), 16)

            @blk.tensor
            def _(t):
                t.wait_ge(sQ.h, cnt['gw'])
                mm = None
                for c in range(16):
                    mm = t.matmul(pT[:, c * 8:(c + 1) * 8],
                                  arena[0:8, GWSB * SLOT + c * P:GWSB * SLOT + (c + 1) * P],
                                  eye_bf[0:8, 0:8], start=True, stop=True)
                sPE.inc(mm, 1)
                t.wait_ge(sS.h, cnt['gb'])
                mm = t.matmul(pT2[:, 0:8], ones_f[0:1, 0:P], fs[0:1, 0:8],
                              start=True, stop=True)
                sPE.inc(mm, 1)
                t.wait_ge(sDVE.h, dv0 + 1)
                mm = None
                for c in range(16):
                    for tch in range(4):
                        mm = t.matmul(pS[0:8, tch * 512:(tch + 1) * 512],
                                      gwt_sb[:, c * 8:(c + 1) * 8],
                                      sl(LN1FM[c], tch * 512, 512),
                                      start=(c == 0), stop=(c == 15))
                sPE.inc(mm, 1)
                for tb in range(NT):
                    t.wait_ge(sACT.h, ac0 + 1 + tb)
                    mm = t.matmul(pT[:, 0:8],
                                  arena[0:8, GLOG * SLOT + tb * P:GLOG * SLOT + (tb + 1) * P],
                                  eye_bf[0:8, 0:8], start=True, stop=True)
                    sPE.inc(mm, 1)
                    t.wait_ge(sDVE.h, dv0 + 1 + 10 * (tb + 1))
                    mm = t.matmul(pS[0:8, tb * P:(tb + 1) * P],
                                  we_all[:, tb * 8:(tb + 1) * 8], eye_f[:, :],
                                  start=True, stop=True)
                    sPE.inc(mm, 1)

            @blk.vector
            def _(v):
                v.wait_ge(sPE.h, pe0 + 1)
                vv = v.tensor_copy(gwt_sb[:, :], pT[:, 0:P])
                sDVE.inc(vv, 1)
                for tb in range(NT):
                    v.wait_ge(sACT.h, ac0 + 2 + tb)
                    sc_ = rt_sb[:, 0:8]
                    vedge(v, v.tensor_add(rt_sb[:, 8:16], sc_, pT2[:, 0:8]))
                    vedge(v, v.reduce_max(rt_sb[:, 16:17], rt_sb[:, 8:16], axis=AX.X))
                    vedge(v, v.tensor_scalar(rt_sb[:, 24:32], rt_sb[:, 8:16],
                                             rt_sb[:, 16:17], None, ALU.is_ge))
                    vedge(v, v.scalar_tensor_tensor(rt_sb[:, 32:40], rt_sb[:, 24:32],
                                                    cpool[:, CP_NEG:CP_NEG + 1],
                                                    rt_sb[:, 8:16], ALU.mult, ALU.add))
                    vedge(v, v.reduce_max(rt_sb[:, 17:18], rt_sb[:, 32:40], axis=AX.X))
                    vedge(v, v.tensor_scalar(rt_sb[:, 24:32], rt_sb[:, 8:16],
                                             rt_sb[:, 17:18], None, ALU.is_ge))
                    vedge(v, v.tensor_mul(rt_sb[:, 32:40], rt_sb[:, 24:32], sc_))
                    ss = v.reduce_sum(rt_sb[:, 18:19], rt_sb[:, 32:40], axis=AX.X)
                    sR.inc(ss, 1)
                    v.wait_ge(sR.h, sR.n)
                    rr = v.reciprocal(rt_sb[:, 19:20], rt_sb[:, 18:19])
                    sR.inc(rr, 1)
                    v.wait_ge(sR.h, sR.n)
                    vv = v.tensor_scalar(we_all[:, tb * 8:(tb + 1) * 8],
                                         rt_sb[:, 32:40], rt_sb[:, 19:20], None,
                                         ALU.mult)
                    sDVE.inc(vv, 10)

            @blk.scalar
            def _(sc):
                sc.wait_ge(sPE.h, pe0 + 3)
                cc = sc.copy(arena[0:8, GLOG * SLOT:GLOG * SLOT + T], pS[0:8, 0:T])
                sACT.inc(cc, 1)
                for tb in range(NT):
                    sc.wait_ge(sPE.h, pe0 + 4 + 2 * tb)
                    cc = sc.activation(rt_sb[:, 0:8], pT[:, 0:8], AF.Sigmoid)
                    sACT.inc(cc, 1)
                sc.wait_ge(sPE.h, pe0 + 5 + 2 * (NT - 1))
                cc = sc.copy(arena[0:8, WER8 * SLOT:WER8 * SLOT + T], pS[0:8, 0:T])
                cnt['we_r8'] = sACT.inc(cc, 1)

        if not phase_on("ffn"):
            return emit_dump()
        jobs = []
        jobs.append(dict(name="shA", up=sgw_bf.ap()[0:HSH, :],
                         dn=lambda kb: sdw_bf.ap()[:, kb * P:(kb + 1) * P],
                         wcnt='sgw', dcnt='sdw', expert=None, first=True))
        jobs.append(dict(name="shB", up=sgw_bf.ap()[HSH:2 * HSH, :],
                         dn=lambda kb: sdw_bf.ap()[:, HEX + kb * P:HEX + (kb + 1) * P],
                         wcnt='sgw', dcnt='sdw', expert=None, first=False))
        for e in range(NE):
            jobs.append(dict(name=f"e{e}", up=up_bf.ap()[e],
                             dn=lambda kb, e=e: dw_bf.ap()[e][:, kb * P:(kb + 1) * P],
                             wcnt=f'up{e}', dcnt=f'dw{e}', expert=e, first=False))

        RW = [512, 512, 512, 512, 512, 256]
        PAIRB = [
            ((pS[:, 0:512], pS[:, 512:1024]), (pS[:, 1024:1536], pS[:, 1536:2048])),
            ((pT[:, :], pT2[:, :]), (pV[:, 0:512], pV[:, 512:1024])),
        ]

        for ph in range(2):
            for ji, job in enumerate(jobs):
                with nc.Block() as blk:
                    pe0, ac0, dv0, s0 = sPE.n, sACT.n, sDVE.n, sS.n
                    exp = job['expert']
                    npair = 11
                    pe_webc = 1 if exp is not None else 0
                    n_ln1h = 16 if ji == 0 else 0
                    range_mark = {}

                    @blk.sync
                    def _(sy, job=job, ji=ji, ph=ph, pe0=pe0, s0=s0, n_ln1h=n_ln1h,
                          range_mark=range_mark, pe_webc=pe_webc):
                        if ji == 0:
                            sy.wait_ge(sS.h, cnt['ln1fm_stored'])
                            for c in range(16):
                                sS.inc(sy.dma_start(
                                    sl(*LN1H[c], ln=1024),
                                    ln1fm_d.ap()[c][:, ph * 1024:(ph + 1) * 1024]), 16)
                        sy.wait_ge(sQ.h, cnt[job['dcnt']])
                        for kb in range(11):
                            sS.inc(sy.dma_start(sl(DWT[kb]), job['dn'](kb),
                                                transpose=True), 16)
                        cnt['dwT_done'] = sS.n
                        sy.wait_ge(sQ.h, cnt[job['wcnt']])
                        for r in range(6):
                            if r >= 2:
                                lastpair = 2 * (r - 2) + 1
                                sy.wait_ge(sPE.h, pe0 + pe_webc + lastpair + 1)
                            rw = RW[r]
                            for c in range(16):
                                sS.inc(sy.dma_start(
                                    sl(*UPT[r % 2][c], ln=rw),
                                    job['up'][r * 512:r * 512 + rw, c * P:(c + 1) * P],
                                    transpose=True), 16)
                            range_mark[r] = sS.n

                    @blk.tensor
                    def _(t, job=job, ji=ji, ph=ph, pe0=pe0, ac0=ac0, dv0=dv0,
                          s0=s0, exp=exp, range_mark=range_mark, pe_webc=pe_webc,
                          n_ln1h=n_ln1h):
                        if exp is not None:
                            mm = None
                            for cq in range(2):
                                src = arena[0:8, WER8 * SLOT + ph * 1024 + cq * 512:
                                            WER8 * SLOT + ph * 1024 + (cq + 1) * 512]
                                mm = t.matmul(pS[:, cq * 512:(cq + 1) * 512],
                                              sel8_sb[0:8, exp * P:(exp + 1) * P],
                                              src, start=True, stop=True)
                            sPE.inc(mm, 1)
                        for i in range(npair):
                            r = i // 2
                            gb_, ub_ = PAIRB[i % 2]
                            t.wait_ge(sS.h, range_mark[r])
                            if ji == 0 and i == 0:
                                t.wait_ge(sS.h, s0 + 16 * n_ln1h)
                            if exp is not None and i == 0:
                                t.wait_ge(sACT.h, ac0 + 1)
                            if i >= 2:
                                t.wait_ge(sDVE.h, dv0 + i - 1)
                            mm = None
                            for blkrow, bks in ((2 * i, gb_), (2 * i + 1, ub_)):
                                sub = (blkrow % 4) * P
                                for c in range(16):
                                    lhs = arena[:, UPT[r % 2][c][0] * SLOT + UPT[r % 2][c][1] + sub:
                                              UPT[r % 2][c][0] * SLOT + UPT[r % 2][c][1] + sub + P]
                                    for ch in range(2):
                                        mm = t.matmul(
                                            bks[ch],
                                            lhs,
                                            arena[:, LN1H[c][0] * SLOT + LN1H[c][1] + ch * 512:
                                                  LN1H[c][0] * SLOT + LN1H[c][1] + (ch + 1) * 512],
                                            start=(c == 0), stop=(c == 15))
                            sPE.inc(mm, 1)
                        t.wait_ge(sDVE.h, dv0 + npair)
                        t.wait_ge(sS.h, cnt['dwT_done'])
                        for tbl in range(8):
                            banks = ([pS[:, 0:512], pS[:, 512:1024], pS[:, 1024:1536],
                                      pS[:, 1536:2048]] if tbl % 2 == 0 else
                                     [pT[:, :], pT2[:, :], pV[:, 0:512], pV[:, 512:1024]])
                            if tbl >= 2:
                                if job['first']:
                                    t.wait_ge(sACT.h, ac0 + npair + (tbl - 1))
                                else:
                                    t.wait_ge(sDVE.h, dv0 + npair + (tbl - 1))
                            if tbl < 2:
                                t.wait_ge(sACT.h, ac0 + (1 if exp is not None else 0) + npair)
                            mm = None
                            for kb in range(11):
                                lhs = arena[:, HSL[kb][0] * SLOT + HSL[kb][1] + tbl * P:
                                            HSL[kb][0] * SLOT + HSL[kb][1] + (tbl + 1) * P]
                                for cc4 in range(4):
                                    mm = t.matmul(banks[cc4],
                                                  lhs,
                                                  sl(DWT[kb], cc4 * 512, 512),
                                                  start=(kb == 0), stop=(kb == 10))
                            sPE.inc(mm, 1)

                    @blk.scalar
                    def _(sc, job=job, exp=exp, pe0=pe0, ac0=ac0, dv0=dv0,
                          pe_webc=pe_webc):
                        if exp is not None:
                            sc.wait_ge(sPE.h, pe0 + 1)
                            sc.copy(sl(*WEBC, ln=512), pS[:, 0:512])
                            cc = sc.copy(arena[:, WEBC[0] * SLOT + WEBC[1] + 512:
                                               WEBC[0] * SLOT + WEBC[1] + 1024],
                                         pS[:, 512:1024])
                            sACT.inc(cc, 1)
                        for i in range(11):
                            gb_, ub_ = PAIRB[i % 2]
                            sc.wait_ge(sPE.h, pe0 + pe_webc + i + 1)
                            if i % 2 == 0:
                                cc = sc.activation(sl(*SGS, ln=1024), pS[:, 0:1024],
                                                   AF.Silu)
                            else:
                                sc.activation(sl(*SGS, ln=512), pT[:, :], AF.Silu)
                                cc = sc.activation(
                                    arena[:, SGS[0] * SLOT + SGS[1] + 512:
                                          SGS[0] * SLOT + SGS[1] + 1024],
                                    pT2[:, :], AF.Silu)
                            sACT.inc(cc, 1)
                        if job['first']:
                            for tbl in range(8):
                                sc.wait_ge(sPE.h, pe0 + pe_webc + 11 + tbl + 1)
                                banks = ([pS[:, 0:512], pS[:, 512:1024], pS[:, 1024:1536],
                                          pS[:, 1536:2048]] if tbl % 2 == 0 else
                                         [pT[:, :], pT2[:, :], pV[:, 0:512], pV[:, 512:1024]])
                                cc = None
                                for cc4 in range(4):
                                    cc = sc.copy(sl(ACC[tbl], cc4 * 512, 512), banks[cc4])
                                sACT.inc(cc, 1)

                    @blk.vector
                    def _(v, job=job, exp=exp, pe0=pe0, ac0=ac0, dv0=dv0,
                          pe_webc=pe_webc):
                        for i in range(11):
                            gb_, ub_ = PAIRB[i % 2]
                            v.wait_ge(sACT.h, ac0 + (1 if exp is not None else 0) + i + 1)
                            v.wait_ge(sPE.h, pe0 + pe_webc + i + 1)
                            hdst = arena[:, HSL[i][0] * SLOT + HSL[i][1]:
                                         HSL[i][0] * SLOT + HSL[i][1] + 1024]
                            hdst2 = arena[:, HSL[i][0] * SLOT + HSL[i][1] + 512:
                                          HSL[i][0] * SLOT + HSL[i][1] + 1024]
                            if exp is not None:
                                vedge(v, v.tensor_mul(sl(*SWC, ln=1024),
                                                      sl(*SGS, ln=1024),
                                                      sl(*WEBC, ln=1024)))
                                if i % 2 == 0:
                                    vv = v.tensor_mul(hdst, sl(*SWC, ln=1024),
                                                      pS[:, 1024:2048])
                                else:
                                    v.tensor_mul(
                                        arena[:, HSL[i][0] * SLOT + HSL[i][1]:
                                              HSL[i][0] * SLOT + HSL[i][1] + 512],
                                        sl(*SWC, ln=512), pV[:, 0:512])
                                    vv = v.tensor_mul(hdst2,
                                                      arena[:, SWC[0] * SLOT + SWC[1] + 512:
                                                            SWC[0] * SLOT + SWC[1] + 1024],
                                                      pV[:, 512:1024])
                            else:
                                if i % 2 == 0:
                                    vv = v.tensor_mul(hdst, sl(*SGS, ln=1024),
                                                      pS[:, 1024:2048])
                                else:
                                    v.tensor_mul(
                                        arena[:, HSL[i][0] * SLOT + HSL[i][1]:
                                              HSL[i][0] * SLOT + HSL[i][1] + 512],
                                        sl(*SGS, ln=512), pV[:, 0:512])
                                    vv = v.tensor_mul(hdst2,
                                                      arena[:, SGS[0] * SLOT + SGS[1] + 512:
                                                            SGS[0] * SLOT + SGS[1] + 1024],
                                                      pV[:, 512:1024])
                            sDVE.inc(vv, 1)
                        if not job['first']:
                            for tbl in range(8):
                                v.wait_ge(sPE.h, pe0 + pe_webc + 11 + tbl + 1)
                                banks = ([pS[:, 0:512], pS[:, 512:1024], pS[:, 1024:1536],
                                          pS[:, 1536:2048]] if tbl % 2 == 0 else
                                         [pT[:, :], pT2[:, :], pV[:, 0:512], pV[:, 512:1024]])
                                vv = None
                                for cc4 in range(4):
                                    acs = sl(ACC[tbl], cc4 * 512, 512)
                                    vv = v.tensor_add(acs, acs, banks[cc4])
                                sDVE.inc(vv, 1)

            with nc.Block() as blk:
                pe0, ac0, dv0, s0 = sPE.n, sACT.n, sDVE.n, sS.n
                NDV2, NAC2 = 4, 2

                def dvm2(tbl, i):
                    return dv0 + NDV2 * tbl + i

                def acm2(tbl, i):
                    return ac0 + NAC2 * tbl + i

                @blk.sync
                def _(sy, ph=ph, s0=s0, dv0=dv0):
                    for tbl in range(8):
                        if tbl >= 1:
                            sy.wait_ge(sDVE.h, dvm2(tbl - 1, 1))
                        sS.inc(sy.dma_start(sl(LN2TM), ln1tm_d.ap()[ph * 8 + tbl]), 16)
                        sy.wait_ge(sDVE.h, dvm2(tbl, 4))
                        sS.inc(sy.dma_start(
                            out_d.ap()[(ph * 8 + tbl) * P:(ph * 8 + tbl + 1) * P, :],
                            fs[:, 2048:4096]), 16)
                    cnt[f'out_st{ph}'] = sS.n

                @blk.vector
                def _(v, ph=ph, s0=s0, ac0=ac0):
                    for tbl in range(8):
                        v.wait_ge(sS.h, s0 + 32 * tbl + 16)
                        if tbl >= 1:
                            v.wait_ge(sS.h, s0 + 32 * tbl)
                        vv = v.tensor_add(fs[:, 0:2048], sl(LN2TM), sl(ACC[tbl]))
                        sDVE.inc(vv, 1)
                        v.wait_ge(sDVE.h, sDVE.n)
                        vedge(v, v.reduce_sum(st_sb[:, 0:1], fs[:, 0:2048], axis=AX.X))
                        vedge(v, v.tensor_scalar(st_sb[:, 1:2], st_sb[:, 0:1],
                                                 cpool[:, CP_INV:CP_INV + 1], None,
                                                 ALU.mult))
                        vv = v.tensor_mul(st_sb[:, 2:3], st_sb[:, 0:1], st_sb[:, 1:2])
                        sDVE.inc(vv, 1)
                        v.wait_ge(sDVE.h, sDVE.n)
                        v.wait_ge(sACT.h, acm2(tbl, 1))
                        vedge(v, v.tensor_sub(st_sb[:, 4:5], st_sb[:, 3:4],
                                              st_sb[:, 2:3]))
                        vv = v.tensor_scalar(st_sb[:, 5:6], st_sb[:, 4:5],
                                             cpool[:, CP_INV:CP_INV + 1], None, ALU.mult)
                        sDVE.inc(vv, 1)
                        v.wait_ge(sDVE.h, sDVE.n)
                        v.wait_ge(sACT.h, acm2(tbl, 2))
                        rr = v.reciprocal(st_sb[:, 7:8], st_sb[:, 6:7])
                        sR.inc(rr, 1)
                        v.wait_ge(sR.h, sR.n)
                        vv = v.tensor_scalar(fs[:, 2048:4096], fs[:, 0:2048],
                                             st_sb[:, 1:2], st_sb[:, 7:8],
                                             ALU.subtract, ALU.mult)
                        sDVE.inc(vv, 1)

                @blk.scalar
                def _(sc, dv0=dv0):
                    for tbl in range(8):
                        sc.wait_ge(sDVE.h, dvm2(tbl, 1))
                        cc = sc.activation(pS[:, 0:2048], fs[:, 0:2048], AF.Square,
                                           accum_out=st_sb[:, 3:4])
                        sACT.inc(cc, 1)
                        sc.wait_ge(sDVE.h, dvm2(tbl, 3))
                        cc = sc.activation(st_sb[:, 6:7], st_sb[:, 5:6], AF.Sqrt,
                                           bias=cpool[:, CP_EPS:CP_EPS + 1],
                                           scale=cpool[:, CP_ONE:CP_ONE + 1])
                        sACT.inc(cc, 1)

        with nc.Block() as blk:
            @blk.sync
            def _(sy):
                sy.wait_ge(sS.h, cnt['out_st1'])
                sy.wait_ge(sS.h, cnt['out_st0'])

    return nc



_PROG_CACHE = {}


def _get_prog():
    if 'nc' not in _PROG_CACHE:
        _PROG_CACHE['nc'] = build_program()
    return _PROG_CACHE['nc']


def kernel(L0_attn_x, L0_attn_Wq, L0_attn_bq, L0_attn_Wk, L0_attn_bk,
           L0_attn_Wv, L0_attn_bv, L0_attn_Wo, L0_attn_bo,
           L0_ffn_gate_w, L0_ffn_gate_bias, L0_ffn_shared_gw, L0_ffn_shared_dw,
           L0_ffn_up_w, L0_ffn_down_w):
    x = np.asarray(L0_attn_x, dtype=np.float32)

    perm = np.concatenate([np.arange(0, D, 2), np.arange(1, D, 2)])
    qcols = np.concatenate([h * D + perm for h in range(HQ)])
    kcols = np.concatenate([h * D + perm for h in range(HKV)])
    wq = np.ascontiguousarray(np.asarray(L0_attn_Wq, np.float32)[:, qcols])
    bq = np.ascontiguousarray(np.asarray(L0_attn_bq, np.float32)[qcols])[None, :]
    wk = np.ascontiguousarray(np.asarray(L0_attn_Wk, np.float32)[:, kcols])
    bk = np.ascontiguousarray(np.asarray(L0_attn_bk, np.float32)[kcols])[None, :]
    wv = np.ascontiguousarray(np.asarray(L0_attn_Wv, np.float32))
    bv = np.ascontiguousarray(np.asarray(L0_attn_bv, np.float32))[None, :]
    wo = np.ascontiguousarray(np.asarray(L0_attn_Wo, np.float32))
    bo = np.ascontiguousarray(
        np.asarray(L0_attn_bo, np.float32).reshape(NT, P).T)

    gw = np.ascontiguousarray(np.asarray(L0_ffn_gate_w, np.float32))
    gb = np.ascontiguousarray(np.asarray(L0_ffn_gate_bias, np.float32))[None, :]

    sgw = np.asarray(L0_ffn_shared_gw, np.float32)
    yb = sgw[0:HSH].reshape(22, P, C)
    gbk = sgw[HSH:2 * HSH].reshape(22, P, C)
    sgw_p = np.ascontiguousarray(
        np.stack([gbk, yb], axis=1).reshape(2 * HSH, C))
    sdw = np.ascontiguousarray(np.asarray(L0_ffn_shared_dw, np.float32))

    upw = np.asarray(L0_ffn_up_w, np.float32)
    gpart = upw[:, 0:HEX].reshape(NE, 11, P, C)
    upart = upw[:, HEX:2 * HEX].reshape(NE, 11, P, C)
    upw_p = np.ascontiguousarray(
        np.stack([gpart, upart], axis=2).reshape(NE, 2 * HEX, C))
    dww = np.ascontiguousarray(np.asarray(L0_ffn_down_w, np.float32))

    inv = (1.0 / (10000.0 ** (np.arange(0, D, 2, dtype=np.float32) / D))).astype(np.float32)
    pos = np.arange(T, dtype=np.float32)
    fr = np.outer(pos, inv).astype(np.float32)
    cosp = np.ascontiguousarray(
        np.cos(fr).astype(np.float32).reshape(NT, P, 32).transpose(1, 0, 2
                                                                   ).reshape(P, NT * 32)).astype(bfloat16)
    sinp = np.ascontiguousarray(
        np.sin(fr).astype(np.float32).reshape(NT, P, 32).transpose(1, 0, 2
                                                                   ).reshape(P, NT * 32)).astype(bfloat16)

    eyeb = np.eye(P, dtype=np.float32).astype(bfloat16)
    eyef = np.eye(P, dtype=np.float32)
    ub = np.zeros((P, P), dtype=np.float32)
    ub[np.triu_indices(P, 1)] = -1e9
    ubb = ub.astype(bfloat16)
    sel8 = np.zeros((8, 8 * P), dtype=np.float32)
    for e in range(8):
        sel8[e, e * P:(e + 1) * P] = 1.0
    sel8 = sel8.astype(bfloat16)

    shared = dict(wq=wq, bq=bq, wk=wk, bk=bk, wv=wv, bv=bv, wo=wo, bo_c=bo,
                  gw=gw, gb=gb, sgw=sgw_p, sdw=sdw, upw=upw_p, dww=dww,
                  cosp=cosp, sinp=sinp, eyeb=eyeb, eyef=eyef, ubias=ubb,
                  sel8=sel8)
    in_maps = []
    for c in range(8):
        m = dict(shared)
        m['x'] = np.ascontiguousarray(x[c])
        in_maps.append(m)

    nc = _get_prog()
    res = run_bass_kernel_spmd(nc, in_maps, core_ids=list(range(8)))
    globals()['LAST'] = res
    return np.stack([res.results[i]['out'] for i in range(8)], axis=0)


# version 03
# speedup vs baseline: 1.0676x; 1.0595x over previous
import numpy as np
import ml_dtypes
from contextlib import ExitStack

import concourse.bass as bass
import concourse.mybir as mybir
from concourse.bass_utils import run_bass_kernel_spmd
import concourse.bass_utils as _bass_utils

if not getattr(_bass_utils, "_ldw_opt_patched", False):
    _orig_run_command = _bass_utils.run_command

    def _run_command_ldw(argv, **kwargs):
        argv = [a.replace("--enable-ldw-opt=false", "--enable-ldw-opt=true")
                if isinstance(a, str) else a for a in argv]
        return _orig_run_command(argv, **kwargs)

    _bass_utils.run_command = _run_command_ldw
    _bass_utils._ldw_opt_patched = True


F32 = mybir.dt.float32
BF16 = mybir.dt.bfloat16
AF = mybir.ActivationFunctionType
ALU = mybir.AluOpType
AX = mybir.AxisListType
bfloat16 = ml_dtypes.bfloat16

P = 128
T = 2048
C = 2048
NT = 16
HQ, HKV, D = 32, 8, 64
KVW = HKV * D
NE = 8
HEX = 1408
HSH = 2816
EXP_C = 9.0
EPS = 1e-5

NSLOT = 45
SLOT = 2048

XFM = list(range(16))
QTM = list(range(16, 32))
KTM = [(32 + tb // 4, (tb % 4) * 512) for tb in range(NT)]
VTM = [(36 + tb // 4, (tb % 4) * 512) for tb in range(NT)]
WST = [(40 + c // 4, (c % 4) * 512) for c in range(16)]
SCR44 = 44
KDUP = list(range(8))
QFM = [8, 9, 10, 11, 12, 13, 14, 15, 32, 33, 34, 35, 40, 41, 42, 43]
PTR = 16
PTM = [24, 25]
ATP = list(range(16, 32))
WOS = [0, 2]
LN1FM = [0, 1, 2, 3, 4, 5, 6, 7, 36, 37, 38, 39, 40, 41, 42, 43]
LN1TMS = [44, 34]
GWSB = 8
GLOG = 9
WER8 = 43
LN1H = [(c // 2, (c % 2) * 1024) for c in range(16)]
ACC = list(range(8, 16))
HSL = [(16 + kb // 2, (kb % 2) * 1024) for kb in range(11)]
DWT = list(range(22, 33))
UPT = [[(33 + c // 4, (c % 4) * 512) for c in range(16)],
       [(37 + c // 4, (c % 4) * 512) for c in range(16)]]
WEBC = (41, 0)
SGS = (41, 1024)
SWC = (42, 0)
LN2TM = 44

CP_EXPB, CP_SCL, CP_NEG, CP_INV, CP_EPS, CP_ONE = 0, 1, 2, 3, 4, 5


class Sem:
    def __init__(self, nc, es, name):
        self.h = es.enter_context(nc.semaphore(name))
        self.n = 0

    def inc(self, inst, k):
        inst.then_inc(self.h, k)
        self.n += k
        return self.n


def build_program(stop_phase=None):
    import os
    if stop_phase is None:
        stop_phase = os.environ.get("KRN_STOP_PHASE", "all")
    PHASES = ["b0", "b1", "pk", "pv", "pq1", "proj", "ktr", "qtr", "attn",
              "wo", "ln1", "rout", "ffn", "all"]
    plim = PHASES.index(stop_phase)

    def phase_on(name):
        return PHASES.index(name) <= plim

    nc = bass.Bass("TRN2", target_bir_lowering=False)

    debug_tap = plim < PHASES.index("all")

    x_in = nc.dram_tensor("x", [T, C], F32, kind="ExternalInput")
    wq_d = nc.dram_tensor("wq", [C, C], F32, kind="ExternalInput")
    bq_d = nc.dram_tensor("bq", [1, C], F32, kind="ExternalInput")
    wk_d = nc.dram_tensor("wk", [C, KVW], F32, kind="ExternalInput")
    bk_d = nc.dram_tensor("bk", [1, KVW], F32, kind="ExternalInput")
    wv_d = nc.dram_tensor("wv", [C, KVW], F32, kind="ExternalInput")
    bv_d = nc.dram_tensor("bv", [1, KVW], F32, kind="ExternalInput")
    wo_d = nc.dram_tensor("wo", [C, C], F32, kind="ExternalInput")
    bo_d = nc.dram_tensor("bo_c", [P, NT], F32, kind="ExternalInput")
    gw_d = nc.dram_tensor("gw", [NE, C], F32, kind="ExternalInput")
    gb_d = nc.dram_tensor("gb", [1, NE], F32, kind="ExternalInput")
    sgw_d = nc.dram_tensor("sgw", [2 * HSH, C], F32, kind="ExternalInput")
    sdw_d = nc.dram_tensor("sdw", [C, HSH], F32, kind="ExternalInput")
    upw_d = nc.dram_tensor("upw", [NE, 2 * HEX, C], F32, kind="ExternalInput")
    dww_d = nc.dram_tensor("dww", [NE, C, HEX], F32, kind="ExternalInput")
    cos_d = nc.dram_tensor("cosp", [P, NT * 32], BF16, kind="ExternalInput")
    sin_d = nc.dram_tensor("sinp", [P, NT * 32], BF16, kind="ExternalInput")
    eyeb_d = nc.dram_tensor("eyeb", [P, P], BF16, kind="ExternalInput")
    eyef_d = nc.dram_tensor("eyef", [P, P], F32, kind="ExternalInput")
    ub_d = nc.dram_tensor("ubias", [P, P], BF16, kind="ExternalInput")
    sel8_d = nc.dram_tensor("sel8", [8, 8 * P], BF16, kind="ExternalInput")
    out_d = nc.dram_tensor("out", [T, C], F32, kind="ExternalOutput")

    x_bf = nc.dram_tensor("x_bfs", [T, C], BF16, kind="Internal")
    sgw_bf = nc.dram_tensor("sgw_bfs", [2 * HSH, C], BF16, kind="Internal")
    sdw_bf = nc.dram_tensor("sdw_bfs", [C, HSH], BF16, kind="Internal")
    up_bf = nc.dram_tensor("up_bfs", [NE, 2 * HEX, C], BF16, kind="Internal")
    dw_bf = nc.dram_tensor("dw_bfs", [NE, C, HEX], BF16, kind="Internal")
    ln1tm_d = nc.dram_tensor("ln1tm_d", [NT, P, C], BF16, kind="Internal")
    ln1fm_d = nc.dram_tensor("ln1fm_d", [NT, P, T], BF16, kind="Internal")
    attn_tap_h = int(os.environ.get("KRN_ATTN_TAP_H", "-1"))
    if attn_tap_h >= 0:
        debug_tap = True
    if debug_tap:
        tap_d = nc.dram_tensor("tap", [P, NSLOT * SLOT], BF16, kind="ExternalOutput")
        tapf_d = nc.dram_tensor("tapf", [P, 4096], F32, kind="ExternalOutput")
        tapw_d = nc.dram_tensor("tapw", [P, NT * 8], F32, kind="ExternalOutput")
        tapd_d = nc.dram_tensor("tapd", [P, 8], F32, kind="ExternalOutput")
        tapr_d = nc.dram_tensor("tapr", [P, 48], F32, kind="ExternalOutput")
        tapg_d = nc.dram_tensor("tapg", [P, 2 * P], BF16, kind="ExternalOutput")

    es = ExitStack()
    with es:
        arena = es.enter_context(nc.sbuf_tensor("arena", [P, NSLOT * SLOT], BF16))
        fs = es.enter_context(nc.sbuf_tensor("fs", [P, 4096], F32))
        browt = es.enter_context(nc.sbuf_tensor("browt", [P, SLOT], BF16))
        cos_sb = es.enter_context(nc.sbuf_tensor("cos_sb", [P, NT * 32], BF16))
        sin_sb = es.enter_context(nc.sbuf_tensor("sin_sb", [P, NT * 32], BF16))
        eye_bf = es.enter_context(nc.sbuf_tensor("eye_bf", [P, P], BF16))
        eye_f = es.enter_context(nc.sbuf_tensor("eye_f", [P, P], F32))
        ub_sb = es.enter_context(nc.sbuf_tensor("ub_sb", [P, P], BF16))
        sel8_sb = es.enter_context(nc.sbuf_tensor("sel8_sb", [8, 8 * P], BF16))
        ones_bf = es.enter_context(nc.sbuf_tensor("ones_bf", [P, P], BF16))
        ones_f = es.enter_context(nc.sbuf_tensor("ones_f", [1, P], F32))
        cpool = es.enter_context(nc.sbuf_tensor("cpool", [P, 8], F32))
        bo_sb = es.enter_context(nc.sbuf_tensor("bo_sb", [P, NT], F32))
        gwt_sb = es.enter_context(nc.sbuf_tensor("gwt_sb", [P, P], BF16))
        dn_sb = es.enter_context(nc.sbuf_tensor("dn_sb", [P, 2], F32))
        rc_sb = es.enter_context(nc.sbuf_tensor("rc_sb", [P, 1], F32))
        diag_sb = es.enter_context(nc.sbuf_tensor("diag_sb", [P, 2 * P], BF16))
        st_sb = es.enter_context(nc.sbuf_tensor("st_sb", [P, 8], F32))
        rt_sb = es.enter_context(nc.sbuf_tensor("rt_sb", [P, 48], F32))
        we_all = es.enter_context(nc.sbuf_tensor("we_all", [P, NT * 8], F32))
        pS = es.enter_context(nc.psum_tensor("pS", [P, 2048], F32))
        pT = es.enter_context(nc.psum_tensor("pT", [P, 512], F32))
        pT2 = es.enter_context(nc.psum_tensor("pT2", [P, 512], F32))
        pV = es.enter_context(nc.psum_tensor("pV", [P, 1024], F32))

        sQ = Sem(nc, es, "sQ")
        sS = Sem(nc, es, "sS")
        sPE = Sem(nc, es, "sPE")
        sACT = Sem(nc, es, "sACT")
        sDVE = Sem(nc, es, "sDVE")
        sR = Sem(nc, es, "sR")

        def vedge(v, inst):
            sR.inc(inst, 1)
            v.wait_ge(sR.h, sR.n)

        def sl(i, off=0, ln=SLOT):
            return arena[:, i * SLOT + off: i * SLOT + off + ln]

        def strided(view, stride, count, width):
            return bass.AP(view.tensor, view.offset,
                           [view.ap[0], [stride, count], [1, width]])

        cnt = {}

        def emit_dump():
            with nc.Block() as blk:
                @blk.sync
                def _(sy):
                    sS.inc(sy.dma_start(tap_d.ap(), arena[:, :]), 16)
                    sS.inc(sy.dma_start(tapf_d.ap(), fs[:, :]), 16)
                    sS.inc(sy.dma_start(tapw_d.ap(), we_all[:, :]), 16)
                    sS.inc(sy.dma_start(tapd_d.ap()[:, 0:2], dn_sb[:, :], allow_slow_non_contiguous=True), 16)
                    sS.inc(sy.dma_start(tapd_d.ap()[:, 2:3], rc_sb[:, :], allow_slow_non_contiguous=True), 16)
                    sS.inc(sy.dma_start(tapg_d.ap(), diag_sb[:, :]), 16)
                    sS.inc(sy.dma_start(tapr_d.ap(), rt_sb[:, :]), 16)
                    sy.wait_ge(sS.h, sS.n)
            return nc

        with nc.Block() as blk:
            @blk.gpsimd
            def _(g):
                cnt['x_bf'] = sQ.inc(g.dma_start(x_bf.ap(), x_in.ap()), 16)
                cnt['sgw'] = sQ.inc(g.dma_start(sgw_bf.ap(), sgw_d.ap()), 16)
                cnt['sdw'] = sQ.inc(g.dma_start(sdw_bf.ap(), sdw_d.ap()), 16)
                for e in range(NE):
                    cnt[f'up{e}'] = sQ.inc(g.dma_start(up_bf.ap()[e], upw_d.ap()[e]), 16)
                    cnt[f'dw{e}'] = sQ.inc(g.dma_start(dw_bf.ap()[e], dww_d.ap()[e]), 16)

            @blk.sync
            def _(sy):
                sS.inc(sy.dma_start(cos_sb[:, :], cos_d.ap()), 16)
                sS.inc(sy.dma_start(sin_sb[:, :], sin_d.ap()), 16)
                sS.inc(sy.dma_start(eye_bf[:, :], eyeb_d.ap()), 16)
                sS.inc(sy.dma_start(eye_f[:, :], eyef_d.ap()), 16)
                sS.inc(sy.dma_start(ub_sb[:, :], ub_d.ap()), 16)
                sS.inc(sy.dma_start(sel8_sb[0:8, :], sel8_d.ap()), 16)
                sS.inc(sy.dma_start(bo_sb[:, :], bo_d.ap()), 16)
                sS.inc(sy.dma_start(fs[0:1, 2048:4096], bq_d.ap()), 16)
                sS.inc(sy.dma_start(fs[0:1, 0:KVW], bk_d.ap()), 16)
                cnt['consts'] = sS.inc(sy.dma_start(fs[0:1, 512:1024], bv_d.ap()), 16)

            @blk.vector
            def _(v):
                v.memset(ones_bf[:, :], 1.0)
                v.memset(ones_f[0:1, :], 1.0)
                v.memset(cpool[:, CP_EXPB:CP_EXPB + 1], -EXP_C)
                v.memset(cpool[:, CP_SCL:CP_SCL + 1], 0.125)
                v.memset(cpool[:, CP_NEG:CP_NEG + 1], -1e9)
                v.memset(cpool[:, CP_INV:CP_INV + 1], 1.0 / C)
                v.memset(cpool[:, CP_EPS:CP_EPS + 1], EPS)
                v.memset(cpool[:, CP_ONE:CP_ONE + 1], 1.0)

        if not phase_on("b1"):
            return emit_dump()
        with nc.Block() as blk:
            @blk.sync
            def _(sy):
                sy.wait_ge(sQ.h, cnt['x_bf'])
                for c in range(16):
                    sS.inc(sy.dma_start(sl(XFM[c]), x_bf.ap()[:, c * P:(c + 1) * P],
                                        transpose=True), 16)
                cnt['xfm'] = sS.n

            @blk.gpsimd
            def _(g):
                dst = arena[:, 40 * SLOT: 40 * SLOT + 16 * 512].rearrange(
                    "p (c s) -> p c s", s=512)
                src = wk_d.ap().rearrange("(c p) s -> p c s", p=P)
                cnt['wk'] = sQ.inc(g.dma_start(dst, src), 16)

            @blk.scalar
            def _(sc):
                sc.wait_ge(sS.h, cnt['consts'])
                cc = sc.copy(browt[0:1, 0:1024], fs[0:1, 0:1024])
                cnt['bias_kv_hi'] = sACT.inc(cc, 1)

            @blk.vector
            def _(v):
                v.wait_ge(sACT.h, cnt['bias_kv_hi'])
                vv = v.tensor_sub(fs[0:1, 1024:2048], fs[0:1, 0:1024],
                                  browt[0:1, 0:1024])
                cnt['bias_kv_lo_f'] = sDVE.inc(vv, 1)

        with nc.Block() as blk:
            @blk.scalar
            def _(sc):
                sc.wait_ge(sDVE.h, cnt['bias_kv_lo_f'])
                cc = sc.copy(browt[32:33, 0:1024], fs[0:1, 1024:2048])
                cnt['bias_kv_lo32'] = sACT.inc(cc, 1)

            @blk.gpsimd
            def _(g):
                g.wait_ge(sS.h, cnt['xfm'])
                g.wait_ge(sACT.h, cnt['bias_kv_lo32'])
                g.dma_start(browt[1:2, 0:1024], browt[32:33, 0:1024]).then_inc(sQ.h, 16)
                sQ.n += 16
                cnt['bias_kv_done'] = sQ.n

        def emit_prep_bq():
            with nc.Block() as blk:
                @blk.scalar
                def _(sc):
                    cc = sc.copy(browt[0:1, 0:C], fs[0:1, 2048:4096])
                    cnt['bias_q_hi'] = sACT.inc(cc, 1)

                @blk.vector
                def _(v):
                    v.wait_ge(sACT.h, cnt['bias_q_hi'])
                    vv = v.tensor_sub(fs[0:1, 0:2048], fs[0:1, 2048:4096],
                                      browt[0:1, 0:C])
                    cnt['bias_q_lo_f'] = sDVE.inc(vv, 1)

            with nc.Block() as blk:
                @blk.scalar
                def _(sc):
                    sc.wait_ge(sDVE.h, cnt['bias_q_lo_f'])
                    cc = sc.copy(browt[32:33, 0:C], fs[0:1, 0:2048])
                    cnt['bias_q_lo32'] = sACT.inc(cc, 1)

                @blk.gpsimd
                def _(g):
                    g.wait_ge(sACT.h, cnt['bias_q_lo32'])
                    g.dma_start(browt[1:2, 0:C], browt[32:33, 0:C]).then_inc(sQ.h, 16)
                    sQ.n += 16
                    cnt['bias_q_done'] = sQ.n

        def emit_rope(v, scr512, dst512, tb):
            sv = scr512.rearrange("p (h two i) -> p h two i", two=2, i=32)
            dv = dst512.rearrange("p (h two i) -> p h two i", two=2, i=32)
            cs = cos_sb[:, tb * 32:(tb + 1) * 32].rearrange("p (o i) -> p o i", o=1)
            sn = sin_sb[:, tb * 32:(tb + 1) * 32].rearrange("p (o i) -> p o i", o=1)
            csb = bass.AP(cs.tensor, cs.offset, [cs.ap[0], [0, 8], cs.ap[2]])
            snb = bass.AP(sn.tensor, sn.offset, [sn.ap[0], [0, 8], sn.ap[2]])
            ev, od = sv[:, :, 0, :], sv[:, :, 1, :]
            t1 = fs[:, 0:256].rearrange("p (h i) -> p h i", i=32)
            t2 = fs[:, 256:512].rearrange("p (h i) -> p h i", i=32)
            v.tensor_mul(t1, ev, csb)
            vedge(v, v.tensor_mul(t2, od, snb))
            vedge(v, v.tensor_sub(dv[:, :, 0, :], t1, t2))
            v.tensor_mul(t1, od, csb)
            vedge(v, v.tensor_mul(t2, ev, snb))
            return v.tensor_add(dv[:, :, 1, :], t1, t2)

        def emit_proj_pass(wkey, bias_rows, bias_key, dst_fn, do_rope, next_w_load):
            with nc.Block() as blk:
                pe0, ac0, dv0 = sPE.n, sACT.n, sDVE.n

                @blk.gpsimd
                def _(g):
                    if next_w_load is not None:
                        name, src_ap = next_w_load
                        g.wait_ge(sPE.h, pe0 + NT)
                        dst = arena[:, 40 * SLOT: 40 * SLOT + 16 * 512].rearrange(
                            "p (c s) -> p c s", s=512)
                        cnt[name] = sQ.inc(g.dma_start(dst, src_ap), 16)

                @blk.tensor
                def _(t):
                    t.wait_ge(sS.h, cnt['xfm'])
                    t.wait_ge(sQ.h, cnt[wkey])
                    t.wait_ge(sQ.h, cnt[bias_key])
                    for tb in range(NT):
                        q4 = tb % 4
                        ps = pS[:, q4 * 512:(q4 + 1) * 512]
                        if tb >= 4:
                            t.wait_ge(sACT.h, ac0 + (tb - 3))
                        for c in range(16):
                            t.matmul(ps, sl(XFM[c], tb * P, P), sl(*WST[c], ln=512),
                                     start=(c == 0), stop=False)
                        mm = t.matmul(ps, bias_rows[0], bias_rows[1],
                                      start=False, stop=True)
                        sPE.inc(mm, 1)

                @blk.scalar
                def _(sc):
                    for tb in range(NT):
                        q4 = tb % 4
                        ps = pS[:, q4 * 512:(q4 + 1) * 512]
                        sc.wait_ge(sPE.h, pe0 + tb + 1)
                        if do_rope:
                            if tb >= 2:
                                sc.wait_ge(sDVE.h, dv0 + 6 * (tb - 1))
                            cc = sc.copy(sl(SCR44, (tb % 2) * 512, 512), ps)
                        else:
                            cc = sc.copy(dst_fn(tb), ps)
                        sACT.inc(cc, 1)

                @blk.vector
                def _(v):
                    if not do_rope:
                        return
                    for tb in range(NT):
                        v.wait_ge(sACT.h, ac0 + tb + 1)
                        vv = emit_rope(v, sl(SCR44, (tb % 2) * 512, 512),
                                       dst_fn(tb), tb)
                        sDVE.inc(vv, 6)

        if not phase_on("pk"):
            return emit_dump()
        bias_k = (ones_bf[0:2, 0:P], browt[0:2, 0:512])
        bias_v = (ones_bf[0:2, 0:P], browt[0:2, 512:1024])

        emit_proj_pass('wk', bias_k, 'bias_kv_done',
                       lambda tb: sl(*KTM[tb], ln=512), True,
                       ('wv', wv_d.ap().rearrange("(c p) s -> p c s", p=P)))
        if not phase_on("pv"):
            return emit_dump()
        emit_proj_pass('wv', bias_v, 'bias_kv_done',
                       lambda tb: sl(*VTM[tb], ln=512), False,
                       ('wq0', wq_d.ap()[:, 0:512].rearrange("(c p) s -> p c s", p=P)))
        if not phase_on("pq1"):
            return emit_dump()
        emit_prep_bq()
        for qq in range(4):
            if qq >= 1 and not phase_on("proj"):
                break
            bias_q = (ones_bf[0:2, 0:P], browt[0:2, qq * 512:(qq + 1) * 512])
            nxt = None
            if qq < 3:
                nxt = (f'wq{qq + 1}',
                       wq_d.ap()[:, (qq + 1) * 512:(qq + 2) * 512].rearrange(
                           "(c p) s -> p c s", p=P))
            emit_proj_pass(f'wq{qq}', bias_q, 'bias_q_done',
                           lambda tb, qq=qq: sl(QTM[tb], qq * 512, 512), True, nxt)

        if not phase_on("ktr"):
            return emit_dump()
        with nc.Block() as blk:
            pe0, dv0 = sPE.n, sDVE.n

            @blk.tensor
            def _(t):
                for tb in range(NT):
                    if tb >= 1:
                        t.wait_ge(sDVE.h, dv0 + 8 * tb)
                    b1 = [pT, pT2][tb % 2]
                    b2 = [pT2, pT][tb % 2]
                    ktile = sl(*KTM[tb], ln=512)
                    mm = None
                    for g in range(HKV):
                        lhs = ktile[:, g * 64:(g + 1) * 64]
                        bank = b1 if g < 4 else b2
                        ps = bank[:, (g % 4) * P:(g % 4) * P + P]
                        t.matmul(ps[0:64, :], lhs, eye_bf[:, :], start=True, stop=True)
                        mm = t.matmul(ps[64:128, :], lhs, eye_bf[:, :], start=True, stop=True)
                    sPE.inc(mm, 1)

            @blk.vector
            def _(v):
                for tb in range(NT):
                    v.wait_ge(sPE.h, pe0 + tb + 1)
                    b1 = [pT, pT2][tb % 2]
                    b2 = [pT2, pT][tb % 2]
                    vv = None
                    for g in range(HKV):
                        bank = b1 if g < 4 else b2
                        ps = bank[:, (g % 4) * P:(g % 4) * P + P]
                        vv = v.tensor_copy(sl(KDUP[g], tb * P, P), ps)
                    sDVE.inc(vv, 8)

        if not phase_on("qtr"):
            return emit_dump()
        with nc.Block() as blk:
            pe0, ac0, dv0 = sPE.n, sACT.n, sDVE.n

            @blk.tensor
            def _(t):
                for tb in range(NT):
                    for grp in range(4):
                        bank = pS[:, grp * 512:(grp + 1) * 512]
                        if tb >= 1:
                            if grp % 2 == 0:
                                t.wait_ge(sDVE.h, dv0 + (tb - 1) * 2 + grp // 2 + 1)
                            else:
                                t.wait_ge(sACT.h, ac0 + (tb - 1) * 2 + grp // 2 + 1)
                        mm = None
                        for j in range(4):
                            d = grp * 4 + j
                            mm = t.matmul(bank[:, j * P:(j + 1) * P],
                                          sl(QTM[tb], d * P, P), eye_bf[:, :],
                                          start=True, stop=True)
                        sPE.inc(mm, 1)

            @blk.vector
            def _(v):
                for tb in range(NT):
                    for grp in (0, 2):
                        bank = pS[:, grp * 512:(grp + 1) * 512]
                        v.wait_ge(sPE.h, pe0 + tb * 4 + grp + 1)
                        vv = None
                        for j in range(4):
                            d = grp * 4 + j
                            vv = v.tensor_copy(sl(QFM[d], tb * P, P),
                                               bank[:, j * P:(j + 1) * P])
                        sDVE.inc(vv, 1)

            @blk.scalar
            def _(sc):
                for tb in range(NT):
                    for grp in (1, 3):
                        bank = pS[:, grp * 512:(grp + 1) * 512]
                        sc.wait_ge(sPE.h, pe0 + tb * 4 + grp + 1)
                        cc = None
                        for j in range(4):
                            d = grp * 4 + j
                            cc = sc.copy(sl(QFM[d], tb * P, P),
                                         bank[:, j * P:(j + 1) * P])
                        sACT.inc(cc, 1)

        if not phase_on("attn"):
            return emit_dump()
        pe0a, ac0a, dv0a = sPE.n, sACT.n, sDVE.n
        g_chunks = []
        pe_items_all = []
        act_items_all = []
        dve_items_all = []
        for h in range(HQ):
            for qt in range(NT):
                klen = (qt + 1) * P
                chs = [(0, min(1024, klen))]
                if klen > 1024:
                    chs.append((1, klen - 1024))
                for (ch, cw) in chs:
                    g_chunks.append((h, qt, ch, cw))
                    pe_items_all.append(("S", h, qt, ch, cw))
                    act_items_all.append(("E", h, qt, ch, cw))
                dve_items_all.append(("DG", h, qt, len(chs)))
                if qt == 9:
                    pe_items_all.append(("PV", h, 0))
                    act_items_all.append(("O", h, 0))
                if qt >= 1:
                    pe_items_all.append(("PT", h, qt - 1))
                    dve_items_all.append(("EV", h, qt - 1, (qt + 3) // 4))
            pe_items_all.append(("PT", h, NT - 1))
            dve_items_all.append(("EV", h, NT - 1, (NT + 3) // 4))
            pe_items_all.append(("PV", h, 1))
            act_items_all.append(("O", h, 1))

        s_mark = {}
        pt_gmarks = {}
        pv_mark = {}
        m = 0
        gidx = 0
        for it in pe_items_all:
            if it[0] == "S":
                m += 1
                s_mark[(it[1], it[2], it[3])] = pe0a + m
            elif it[0] == "PT":
                ng = (it[2] + 4) // 4
                mk = []
                for _ in range(ng):
                    m += 1
                    mk.append(pe0a + m)
                pt_gmarks[(it[1], it[2])] = mk
            else:
                m += 1
                pv_mark[(it[1], it[2])] = pe0a + m

        exp_mark = {}
        oev_mark = {}
        a = 0
        for it in act_items_all:
            a += 1
            if it[0] == "E":
                exp_mark[(it[1], it[2], it[3])] = ac0a + a
            else:
                oev_mark[(it[1], it[2])] = ac0a + a

        diag_mark = {}
        evac_gmark = []
        evac_last = {}
        d = 0
        for it in dve_items_all:
            if it[0] == "DG":
                d += (1 if it[3] == 2 else 0) + 2
                diag_mark[(it[1], it[2])] = dv0a + d
            else:
                for _ in range(it[3]):
                    d += 1
                    evac_gmark.append(dv0a + d)
                evac_last[(it[1], it[2])] = dv0a + d

        last_ch = {qt: (1 if (qt + 1) * P > 1024 else 0) for qt in range(NT)}

        s_gci = {}
        for i_, (h_, q_, c_, w_) in enumerate(g_chunks):
            s_gci[(h_, q_, c_)] = i_
        g_chunks = [(h_, q_, c_, w_) for (h_, q_, c_, w_) in g_chunks]

        ptm_prev_user = {}
        dn_prev_user = {}
        last_pt_mark = {0: None, 1: None}
        prev_diag = None
        diag_prev_user = {}
        prev_oev = {}
        last_oev = None
        for it in pe_items_all:
            pass
        for h in range(HQ):
            for qt in range(NT):
                for ch in range(2 if (qt + 1) * P > 1024 else 1):
                    ptm_prev_user[(h, qt, ch)] = last_pt_mark[qt % 2]
                    dn_prev_user[(h, qt, ch)] = prev_diag
                diag_prev_user[(h, qt)] = last_pt_mark[qt % 2]
                prev_diag = diag_mark[(h, qt)]
                if qt >= 1:
                    last_pt_mark[(qt - 1) % 2] = pt_gmarks[(h, qt - 1)][-1]
                if qt == 9:
                    prev_oev[(h, 0)] = last_oev
                    last_oev = oev_mark[(h, 0)]
            last_pt_mark[(NT - 1) % 2] = pt_gmarks[(h, NT - 1)][-1]
            prev_oev[(h, 1)] = last_oev
            last_oev = oev_mark[(h, 1)]

        with nc.Block() as blk:
            @blk.tensor
            def _(t):
                ggl = [0]
                for idx, it in enumerate(pe_items_all):
                    if it[0] == "S":
                        _, h, qt, ch, cw = it
                        hp, g_ = h // 2, h // 4
                        base = (h % 2) * 64
                        ci = g_chunks.index((h, qt, ch)) if False else None
                        gci = s_gci[(h, qt, ch)]
                        ps = pS[:, (gci % 2) * 1024: (gci % 2) * 1024 + cw]
                        if gci >= 2:
                            ph_, pq_, pc_, _ = g_chunks[gci - 2]
                            t.wait_ge(sACT.h, exp_mark[(ph_, pq_, pc_)])
                        lhs = arena[base:base + 64,
                                    QFM[hp] * SLOT + qt * P: QFM[hp] * SLOT + (qt + 1) * P]
                        dpos = qt * P - ch * 1024
                        mm = None
                        pos = 0
                        while pos < cw:
                            w = min(512, cw - pos)
                            kcol = ch * 1024 + pos
                            rhs = arena[base:base + 64,
                                        KDUP[g_] * SLOT + kcol: KDUP[g_] * SLOT + kcol + w]
                            has_diag = (pos <= dpos < pos + w)
                            mm = t.matmul(ps[:, pos:pos + w], lhs, rhs,
                                          start=True, stop=not has_diag)
                            pos += w
                        if 0 <= dpos < cw:
                            mm = t.matmul(ps[:, dpos:dpos + P], eye_bf[:, :],
                                          ub_sb[:, :], start=False, stop=True)
                        sPE.inc(mm, 1)
                    elif it[0] == "PT":
                        _, h, qtp = it
                        nkb = qtp + 1
                        t.wait_ge(sDVE.h, diag_mark[(h, qtp)])
                        t.wait_ge(sACT.h, exp_mark[(h, qtp, last_ch[qtp])])
                        for g0 in range(0, nkb, 4):
                            gw_ = min(4, nkb - g0)
                            gi = ggl[0]
                            bank = [pT, pT2][gi % 2]
                            if gi >= 2:
                                t.wait_ge(sDVE.h, evac_gmark[gi - 2])
                            mm = None
                            for j in range(gw_):
                                kb = g0 + j
                                mm = t.matmul(bank[:, j * P:(j + 1) * P],
                                              sl(PTM[qtp % 2], kb * P, P),
                                              diag_sb[:, (qtp % 2) * P:(qtp % 2 + 1) * P],
                                              start=True, stop=True)
                            sPE.inc(mm, 1)
                            ggl[0] += 1
                    else:
                        _, h, half = it
                        g_ = h // 4
                        hp = h // 2
                        base = (h % 2) * 64
                        kmax = 8 if half == 0 else 16
                        lastq = 7 if half == 0 else 15
                        t.wait_ge(sDVE.h, evac_last[(h, lastq)])
                        prevo = prev_oev[(h, half)]
                        if prevo is not None:
                            t.wait_ge(sACT.h, prevo)
                        mm = None
                        for kb in range(kmax):
                            qs = max(0, kb * P - half * 1024)
                            vt = sl(*VTM[kb], ln=512)
                            lhs = vt[:, g_ * 64:(g_ + 1) * 64]
                            pos = qs
                            while pos < 1024:
                                w = min(512 - (pos % 512), 1024 - pos)
                                mm = t.matmul(pV[0:64, pos:pos + w], lhs,
                                              arena[:, PTR * SLOT + kb * 1024 + pos:
                                                    PTR * SLOT + kb * 1024 + pos + w],
                                              start=(kb == 0),
                                              stop=(kb == kmax - 1))
                                pos += w
                        sPE.inc(mm, 1)

            @blk.scalar
            def _(sc):
                for it in act_items_all:
                    if it[0] == "E":
                        _, h, qt, ch, cw = it
                        gci = s_gci[(h, qt, ch)]
                        ps = pS[:, (gci % 2) * 1024:(gci % 2) * 1024 + cw]
                        sc.wait_ge(sPE.h, s_mark[(h, qt, ch)])
                        pm = ptm_prev_user[(h, qt, ch)]
                        if pm is not None:
                            sc.wait_ge(sPE.h, pm)
                        dm = dn_prev_user[(h, qt, ch)]
                        if dm is not None and ch == 0:
                            sc.wait_ge(sDVE.h, dm)
                        cc = sc.activation(sl(PTM[qt % 2], ch * 1024, cw), ps,
                                           AF.Exp,
                                           bias=cpool[:, CP_EXPB:CP_EXPB + 1],
                                           scale=cpool[:, CP_SCL:CP_SCL + 1],
                                           accum_out=dn_sb[:, ch:ch + 1])
                        sACT.inc(cc, 1)
                    else:
                        _, h, half = it
                        hp = h // 2
                        base = (h % 2) * 64
                        sc.wait_ge(sPE.h, pv_mark[(h, half)])
                        obase = QFM[hp] * SLOT + half * 1024
                        sc.copy(arena[base:base + 64, obase:obase + 512],
                                pV[0:64, 0:512])
                        cc = sc.copy(arena[base:base + 64, obase + 512:obase + 1024],
                                     pV[0:64, 512:1024])
                        sACT.inc(cc, 1)

            @blk.vector
            def _(v):
                gi2 = [0]
                for it in dve_items_all:
                    if it[0] == "DG":
                        _, h, qt, nch = it
                        v.wait_ge(sACT.h, exp_mark[(h, qt, last_ch[qt])])
                        pm = diag_prev_user[(h, qt)]
                        if pm is not None:
                            v.wait_ge(sPE.h, pm)
                        k = 0
                        if nch == 2:
                            aa = v.tensor_add(st_sb[:, 0:1], dn_sb[:, 0:1],
                                              dn_sb[:, 1:2])
                            sR.inc(aa, 1)
                            v.wait_ge(sR.h, sR.n)
                            srcv = st_sb[:, 0:1]
                            k = 1
                        else:
                            srcv = dn_sb[:, 0:1]
                        rr = v.reciprocal(rc_sb[:, 0:1], srcv)
                        sR.inc(rr, 1)
                        v.wait_ge(sR.h, sR.n)
                        vv = v.tensor_scalar(
                            diag_sb[:, (qt % 2) * P:(qt % 2 + 1) * P],
                            eye_bf[:, :], rc_sb[:, 0:1], None, ALU.mult)
                        sDVE.inc(vv, k + 2)
                    else:
                        _, h, qtp, ng = it
                        for gg in range(ng):
                            v.wait_ge(sPE.h, pt_gmarks[(h, qtp)][gg])
                            gi = gi2[0]
                            bank = [pT, pT2][gi % 2]
                            gw_ = min(4, qtp + 1 - gg * 4)
                            dstb = arena[:, PTR * SLOT + (gg * 4) * 1024 + (qtp % 8) * P:
                                         PTR * SLOT + (gg * 4) * 1024 + (qtp % 8) * P + P]
                            dst = strided(dstb, 1024, gw_, P)
                            vv = v.tensor_copy(dst, bank[:, 0:gw_ * P])
                            sDVE.inc(vv, 1)
                            gi2[0] += 1


        with nc.Block() as blk:
            pe0, ac0, dv0, q0 = sPE.n, sACT.n, sDVE.n, sQ.n

            @blk.gpsimd
            def _(g):
                for cb in range(16):
                    if cb >= 2:
                        g.wait_ge(sPE.h, pe0 + cb - 1)
                    b = WOS[cb % 2]
                    dst = arena[:, b * SLOT: b * SLOT + 16 * P].rearrange(
                        "p (c s) -> p c s", s=P)
                    src = wo_d.ap()[:, cb * P:(cb + 1) * P].rearrange("(c p) s -> p c s", p=P)
                    sQ.inc(g.dma_start(dst, src), 16)

            @blk.tensor
            def _(t):
                for cb in range(16):
                    t.wait_ge(sQ.h, q0 + 16 * (cb + 1))
                    if cb >= 1:
                        t.wait_ge(sDVE.h, dv0 + cb)
                        t.wait_ge(sACT.h, ac0 + cb)
                    b = WOS[cb % 2]
                    mm = None
                    for cin in range(16):
                        lhs = sl(b, cin * P, P)
                        for tch in range(4):
                            mm = t.matmul(pS[:, tch * 512:(tch + 1) * 512], lhs,
                                          sl(QFM[cin], tch * 512, 512),
                                          start=(cin == 0), stop=(cin == 15))
                    sPE.inc(mm, 1)

            @blk.vector
            def _(v):
                for cb in range(16):
                    v.wait_ge(sPE.h, pe0 + cb + 1)
                    vv = None
                    for tch in (0, 1):
                        vv = v.tensor_scalar(sl(ATP[cb], tch * 512, 512),
                                             pS[:, tch * 512:(tch + 1) * 512],
                                             bo_sb[:, cb:cb + 1], None, ALU.add)
                    sDVE.inc(vv, 1)

            @blk.scalar
            def _(sc):
                for cb in range(16):
                    sc.wait_ge(sPE.h, pe0 + cb + 1)
                    cc = None
                    for tch in (2, 3):
                        cc = sc.add(sl(ATP[cb], tch * 512, 512),
                                    pS[:, tch * 512:(tch + 1) * 512],
                                    bo_sb[:, cb:cb + 1])
                    sACT.inc(cc, 1)

        if not phase_on("ln1"):
            return emit_dump()
        with nc.Block() as blk:
            pe0, ac0, dv0, s0 = sPE.n, sACT.n, sDVE.n, sS.n
            NDV, NAC = 5, 3

            def dvm(tb, i):
                return dv0 + NDV * tb + i

            def acm(tb, i):
                return ac0 + NAC * tb + i

            @blk.sync
            def _(sy):
                for tb in range(NT):
                    if tb >= 1:
                        sy.wait_ge(sDVE.h, dvm(tb - 1, 1))
                    sS.inc(sy.dma_start(fs[:, 2048:4096],
                                        x_in.ap()[tb * P:(tb + 1) * P, :]), 16)
                    sy.wait_ge(sDVE.h, dvm(tb, 4))
                    sS.inc(sy.dma_start(ln1tm_d.ap()[tb], sl(LN1TMS[tb % 2])), 16)
                sy.wait_ge(sDVE.h, dv0 + NDV * NT)
                sy.wait_ge(sACT.h, ac0 + NAC * NT)
                for c in range(16):
                    sS.inc(sy.dma_start(ln1fm_d.ap()[c], sl(LN1FM[c])), 16)
                cnt['ln1fm_stored'] = sS.n

            @blk.tensor
            def _(t):
                for tb in range(NT):
                    if tb >= 1:
                        t.wait_ge(sACT.h, acm(tb - 1, 1))
                        t.wait_ge(sDVE.h, dvm(tb - 1, 5))
                        t.wait_ge(sACT.h, acm(tb - 1, 3))
                    mm = None
                    for c in range(16):
                        mm = t.matmul(pS[:, c * P:(c + 1) * P],
                                      sl(ATP[c], tb * P, P), eye_bf[:, :],
                                      start=True, stop=True)
                    sPE.inc(mm, 1)
                    t.wait_ge(sDVE.h, dvm(tb, 4))
                    t.wait_ge(sACT.h, acm(tb, 1))
                    mm = None
                    for c in range(16):
                        mm = t.matmul(pS[:, c * P:(c + 1) * P],
                                      sl(LN1TMS[tb % 2], c * P, P), eye_bf[:, :],
                                      start=True, stop=True)
                    sPE.inc(mm, 1)

            @blk.vector
            def _(v):
                for tb in range(NT):
                    v.wait_ge(sPE.h, pe0 + 2 * tb + 1)
                    v.wait_ge(sS.h, s0 + 32 * tb + 16)
                    vv = None
                    for q4 in range(4):
                        vv = v.tensor_add(fs[:, q4 * 512:(q4 + 1) * 512],
                                          pS[:, q4 * 512:(q4 + 1) * 512],
                                          fs[:, 2048 + q4 * 512:2048 + (q4 + 1) * 512])
                    sDVE.inc(vv, 1)
                    v.wait_ge(sDVE.h, sDVE.n)
                    vedge(v, v.reduce_sum(st_sb[:, 0:1], fs[:, 0:2048], axis=AX.X))
                    vedge(v, v.tensor_scalar(st_sb[:, 1:2], st_sb[:, 0:1],
                                             cpool[:, CP_INV:CP_INV + 1], None, ALU.mult))
                    vv = v.tensor_mul(st_sb[:, 2:3], st_sb[:, 0:1], st_sb[:, 1:2])
                    sDVE.inc(vv, 1)
                    v.wait_ge(sDVE.h, sDVE.n)
                    v.wait_ge(sACT.h, acm(tb, 1))
                    vedge(v, v.tensor_sub(st_sb[:, 4:5], st_sb[:, 3:4], st_sb[:, 2:3]))
                    vv = v.tensor_scalar(st_sb[:, 5:6], st_sb[:, 4:5],
                                         cpool[:, CP_INV:CP_INV + 1], None, ALU.mult)
                    sDVE.inc(vv, 1)
                    v.wait_ge(sDVE.h, sDVE.n)
                    v.wait_ge(sACT.h, acm(tb, 2))
                    rr = v.reciprocal(st_sb[:, 7:8], st_sb[:, 6:7])
                    sR.inc(rr, 1)
                    v.wait_ge(sR.h, sR.n)
                    if tb >= 2:
                        v.wait_ge(sS.h, s0 + 32 * (tb - 2) + 32)
                    vv = v.tensor_scalar(sl(LN1TMS[tb % 2]), fs[:, 0:2048],
                                         st_sb[:, 1:2], st_sb[:, 7:8],
                                         ALU.subtract, ALU.mult)
                    sDVE.inc(vv, 1)
                    v.wait_ge(sPE.h, pe0 + 2 * tb + 2)
                    vv = None
                    for grp in (0, 2):
                        c0 = grp * 4
                        dstb = arena[:, LN1FM[c0] * SLOT + tb * P:
                                     LN1FM[c0] * SLOT + tb * P + P]
                        vv = v.tensor_copy(strided(dstb, SLOT, 4, P),
                                           pS[:, c0 * P:c0 * P + 4 * P])
                    sDVE.inc(vv, 1)

            @blk.scalar
            def _(sc):
                for tb in range(NT):
                    sc.wait_ge(sDVE.h, dvm(tb, 1))
                    cc = sc.activation(pS[:, 0:2048], fs[:, 0:2048], AF.Square,
                                       accum_out=st_sb[:, 3:4])
                    sACT.inc(cc, 1)
                    sc.wait_ge(sDVE.h, dvm(tb, 3))
                    cc = sc.activation(st_sb[:, 6:7], st_sb[:, 5:6], AF.Sqrt,
                                       bias=cpool[:, CP_EPS:CP_EPS + 1],
                                       scale=cpool[:, CP_ONE:CP_ONE + 1])
                    sACT.inc(cc, 1)
                    sc.wait_ge(sPE.h, pe0 + 2 * tb + 2)
                    cc = None
                    for grp in (1, 3):
                        c0 = grp * 4
                        dstb = arena[:, LN1FM[c0] * SLOT + tb * P:
                                     LN1FM[c0] * SLOT + tb * P + P]
                        cc = sc.copy(strided(dstb, SLOT, 4, P),
                                     pS[:, c0 * P:c0 * P + 4 * P])
                    sACT.inc(cc, 1)

        if not phase_on("rout"):
            return emit_dump()
        with nc.Block() as blk:
            pe0, ac0, dv0 = sPE.n, sACT.n, sDVE.n

            @blk.gpsimd
            def _(g):
                cnt['gw'] = sQ.inc(g.dma_start(arena[0:8, GWSB * SLOT:GWSB * SLOT + C],
                                               gw_d.ap()), 16)

            @blk.sync
            def _(sy):
                cnt['gb'] = sS.inc(sy.dma_start(fs[0:1, 0:8], gb_d.ap()), 16)

            @blk.tensor
            def _(t):
                t.wait_ge(sQ.h, cnt['gw'])
                mm = None
                for c in range(16):
                    mm = t.matmul(pT[:, c * 8:(c + 1) * 8],
                                  arena[0:8, GWSB * SLOT + c * P:GWSB * SLOT + (c + 1) * P],
                                  eye_bf[0:8, 0:8], start=True, stop=True)
                sPE.inc(mm, 1)
                t.wait_ge(sS.h, cnt['gb'])
                mm = t.matmul(pT2[:, 0:8], ones_f[0:1, 0:P], fs[0:1, 0:8],
                              start=True, stop=True)
                sPE.inc(mm, 1)
                t.wait_ge(sDVE.h, dv0 + 1)
                mm = None
                for c in range(16):
                    for tch in range(4):
                        mm = t.matmul(pS[0:8, tch * 512:(tch + 1) * 512],
                                      gwt_sb[:, c * 8:(c + 1) * 8],
                                      sl(LN1FM[c], tch * 512, 512),
                                      start=(c == 0), stop=(c == 15))
                sPE.inc(mm, 1)
                for tb in range(NT):
                    t.wait_ge(sACT.h, ac0 + 1 + tb)
                    mm = t.matmul(pT[:, 0:8],
                                  arena[0:8, GLOG * SLOT + tb * P:GLOG * SLOT + (tb + 1) * P],
                                  eye_bf[0:8, 0:8], start=True, stop=True)
                    sPE.inc(mm, 1)
                    t.wait_ge(sDVE.h, dv0 + 1 + 10 * (tb + 1))
                    mm = t.matmul(pS[0:8, tb * P:(tb + 1) * P],
                                  we_all[:, tb * 8:(tb + 1) * 8], eye_f[:, :],
                                  start=True, stop=True)
                    sPE.inc(mm, 1)

            @blk.vector
            def _(v):
                v.wait_ge(sPE.h, pe0 + 1)
                vv = v.tensor_copy(gwt_sb[:, :], pT[:, 0:P])
                sDVE.inc(vv, 1)
                for tb in range(NT):
                    v.wait_ge(sACT.h, ac0 + 2 + tb)
                    sc_ = rt_sb[:, 0:8]
                    vedge(v, v.tensor_add(rt_sb[:, 8:16], sc_, pT2[:, 0:8]))
                    vedge(v, v.reduce_max(rt_sb[:, 16:17], rt_sb[:, 8:16], axis=AX.X))
                    vedge(v, v.tensor_scalar(rt_sb[:, 24:32], rt_sb[:, 8:16],
                                             rt_sb[:, 16:17], None, ALU.is_ge))
                    vedge(v, v.scalar_tensor_tensor(rt_sb[:, 32:40], rt_sb[:, 24:32],
                                                    cpool[:, CP_NEG:CP_NEG + 1],
                                                    rt_sb[:, 8:16], ALU.mult, ALU.add))
                    vedge(v, v.reduce_max(rt_sb[:, 17:18], rt_sb[:, 32:40], axis=AX.X))
                    vedge(v, v.tensor_scalar(rt_sb[:, 24:32], rt_sb[:, 8:16],
                                             rt_sb[:, 17:18], None, ALU.is_ge))
                    vedge(v, v.tensor_mul(rt_sb[:, 32:40], rt_sb[:, 24:32], sc_))
                    ss = v.reduce_sum(rt_sb[:, 18:19], rt_sb[:, 32:40], axis=AX.X)
                    sR.inc(ss, 1)
                    v.wait_ge(sR.h, sR.n)
                    rr = v.reciprocal(rt_sb[:, 19:20], rt_sb[:, 18:19])
                    sR.inc(rr, 1)
                    v.wait_ge(sR.h, sR.n)
                    vv = v.tensor_scalar(we_all[:, tb * 8:(tb + 1) * 8],
                                         rt_sb[:, 32:40], rt_sb[:, 19:20], None,
                                         ALU.mult)
                    sDVE.inc(vv, 10)

            @blk.scalar
            def _(sc):
                sc.wait_ge(sPE.h, pe0 + 3)
                cc = sc.copy(arena[0:8, GLOG * SLOT:GLOG * SLOT + T], pS[0:8, 0:T])
                sACT.inc(cc, 1)
                for tb in range(NT):
                    sc.wait_ge(sPE.h, pe0 + 4 + 2 * tb)
                    cc = sc.activation(rt_sb[:, 0:8], pT[:, 0:8], AF.Sigmoid)
                    sACT.inc(cc, 1)
                sc.wait_ge(sPE.h, pe0 + 5 + 2 * (NT - 1))
                cc = sc.copy(arena[0:8, WER8 * SLOT:WER8 * SLOT + T], pS[0:8, 0:T])
                cnt['we_r8'] = sACT.inc(cc, 1)

        if not phase_on("ffn"):
            return emit_dump()
        jobs = []
        jobs.append(dict(name="shA", up=sgw_bf.ap()[0:HSH, :],
                         dn=lambda kb: sdw_bf.ap()[:, kb * P:(kb + 1) * P],
                         wcnt='sgw', dcnt='sdw', expert=None, first=True))
        jobs.append(dict(name="shB", up=sgw_bf.ap()[HSH:2 * HSH, :],
                         dn=lambda kb: sdw_bf.ap()[:, HEX + kb * P:HEX + (kb + 1) * P],
                         wcnt='sgw', dcnt='sdw', expert=None, first=False))
        for e in range(NE):
            jobs.append(dict(name=f"e{e}", up=up_bf.ap()[e],
                             dn=lambda kb, e=e: dw_bf.ap()[e][:, kb * P:(kb + 1) * P],
                             wcnt=f'up{e}', dcnt=f'dw{e}', expert=e, first=False))

        RW = [512, 512, 512, 512, 512, 256]
        PAIRB = [
            ((pS[:, 0:512], pS[:, 512:1024]), (pS[:, 1024:1536], pS[:, 1536:2048])),
            ((pT[:, :], pT2[:, :]), (pV[:, 0:512], pV[:, 512:1024])),
        ]

        for ph in range(2):
            for ji, job in enumerate(jobs):
                with nc.Block() as blk:
                    pe0, ac0, dv0, s0 = sPE.n, sACT.n, sDVE.n, sS.n
                    exp = job['expert']
                    npair = 11
                    pe_webc = 1 if exp is not None else 0
                    n_ln1h = 16 if ji == 0 else 0
                    range_mark = {}

                    @blk.sync
                    def _(sy, job=job, ji=ji, ph=ph, pe0=pe0, s0=s0, n_ln1h=n_ln1h,
                          range_mark=range_mark, pe_webc=pe_webc):
                        if ji == 0:
                            sy.wait_ge(sS.h, cnt['ln1fm_stored'])
                            for c in range(16):
                                sS.inc(sy.dma_start(
                                    sl(*LN1H[c], ln=1024),
                                    ln1fm_d.ap()[c][:, ph * 1024:(ph + 1) * 1024]), 16)
                        sy.wait_ge(sQ.h, cnt[job['dcnt']])
                        for kb in range(11):
                            sS.inc(sy.dma_start(sl(DWT[kb]), job['dn'](kb),
                                                transpose=True), 16)
                        cnt['dwT_done'] = sS.n
                        sy.wait_ge(sQ.h, cnt[job['wcnt']])
                        for r in range(6):
                            if r >= 2:
                                lastpair = 2 * (r - 2) + 1
                                sy.wait_ge(sPE.h, pe0 + pe_webc + lastpair + 1)
                            rw = RW[r]
                            for c in range(16):
                                sS.inc(sy.dma_start(
                                    sl(*UPT[r % 2][c], ln=rw),
                                    job['up'][r * 512:r * 512 + rw, c * P:(c + 1) * P],
                                    transpose=True), 16)
                            range_mark[r] = sS.n

                    @blk.tensor
                    def _(t, job=job, ji=ji, ph=ph, pe0=pe0, ac0=ac0, dv0=dv0,
                          s0=s0, exp=exp, range_mark=range_mark, pe_webc=pe_webc,
                          n_ln1h=n_ln1h):
                        if exp is not None:
                            mm = None
                            for cq in range(2):
                                src = arena[0:8, WER8 * SLOT + ph * 1024 + cq * 512:
                                            WER8 * SLOT + ph * 1024 + (cq + 1) * 512]
                                mm = t.matmul(pS[:, cq * 512:(cq + 1) * 512],
                                              sel8_sb[0:8, exp * P:(exp + 1) * P],
                                              src, start=True, stop=True)
                            sPE.inc(mm, 1)
                        for i in range(npair):
                            r = i // 2
                            gb_, ub_ = PAIRB[i % 2]
                            t.wait_ge(sS.h, range_mark[r])
                            if ji == 0 and i == 0:
                                t.wait_ge(sS.h, s0 + 16 * n_ln1h)
                            if exp is not None and i == 0:
                                t.wait_ge(sACT.h, ac0 + 1)
                            if i >= 2:
                                t.wait_ge(sDVE.h, dv0 + i - 1)
                            mm = None
                            for blkrow, bks in ((2 * i, gb_), (2 * i + 1, ub_)):
                                sub = (blkrow % 4) * P
                                for c in range(16):
                                    lhs = arena[:, UPT[r % 2][c][0] * SLOT + UPT[r % 2][c][1] + sub:
                                              UPT[r % 2][c][0] * SLOT + UPT[r % 2][c][1] + sub + P]
                                    for ch in range(2):
                                        mm = t.matmul(
                                            bks[ch],
                                            lhs,
                                            arena[:, LN1H[c][0] * SLOT + LN1H[c][1] + ch * 512:
                                                  LN1H[c][0] * SLOT + LN1H[c][1] + (ch + 1) * 512],
                                            start=(c == 0), stop=(c == 15))
                            sPE.inc(mm, 1)
                        t.wait_ge(sDVE.h, dv0 + npair)
                        t.wait_ge(sS.h, cnt['dwT_done'])
                        for tbl in range(8):
                            banks = ([pS[:, 0:512], pS[:, 512:1024], pS[:, 1024:1536],
                                      pS[:, 1536:2048]] if tbl % 2 == 0 else
                                     [pT[:, :], pT2[:, :], pV[:, 0:512], pV[:, 512:1024]])
                            if tbl >= 2:
                                if job['first']:
                                    t.wait_ge(sACT.h, ac0 + npair + (tbl - 1))
                                else:
                                    t.wait_ge(sDVE.h, dv0 + npair + (tbl - 1))
                            if tbl < 2:
                                t.wait_ge(sACT.h, ac0 + (1 if exp is not None else 0) + npair)
                            mm = None
                            for kb in range(11):
                                lhs = arena[:, HSL[kb][0] * SLOT + HSL[kb][1] + tbl * P:
                                            HSL[kb][0] * SLOT + HSL[kb][1] + (tbl + 1) * P]
                                for cc4 in range(4):
                                    mm = t.matmul(banks[cc4],
                                                  lhs,
                                                  sl(DWT[kb], cc4 * 512, 512),
                                                  start=(kb == 0), stop=(kb == 10))
                            sPE.inc(mm, 1)

                    @blk.scalar
                    def _(sc, job=job, exp=exp, pe0=pe0, ac0=ac0, dv0=dv0,
                          pe_webc=pe_webc):
                        if exp is not None:
                            sc.wait_ge(sPE.h, pe0 + 1)
                            sc.copy(sl(*WEBC, ln=512), pS[:, 0:512])
                            cc = sc.copy(arena[:, WEBC[0] * SLOT + WEBC[1] + 512:
                                               WEBC[0] * SLOT + WEBC[1] + 1024],
                                         pS[:, 512:1024])
                            sACT.inc(cc, 1)
                        for i in range(11):
                            gb_, ub_ = PAIRB[i % 2]
                            sc.wait_ge(sPE.h, pe0 + pe_webc + i + 1)
                            if i % 2 == 0:
                                cc = sc.activation(sl(*SGS, ln=1024), pS[:, 0:1024],
                                                   AF.Silu)
                            else:
                                sc.activation(sl(*SGS, ln=512), pT[:, :], AF.Silu)
                                cc = sc.activation(
                                    arena[:, SGS[0] * SLOT + SGS[1] + 512:
                                          SGS[0] * SLOT + SGS[1] + 1024],
                                    pT2[:, :], AF.Silu)
                            sACT.inc(cc, 1)
                        if job['first']:
                            for tbl in range(8):
                                sc.wait_ge(sPE.h, pe0 + pe_webc + 11 + tbl + 1)
                                banks = ([pS[:, 0:512], pS[:, 512:1024], pS[:, 1024:1536],
                                          pS[:, 1536:2048]] if tbl % 2 == 0 else
                                         [pT[:, :], pT2[:, :], pV[:, 0:512], pV[:, 512:1024]])
                                cc = None
                                for cc4 in range(4):
                                    cc = sc.copy(sl(ACC[tbl], cc4 * 512, 512), banks[cc4])
                                sACT.inc(cc, 1)

                    @blk.vector
                    def _(v, job=job, exp=exp, pe0=pe0, ac0=ac0, dv0=dv0,
                          pe_webc=pe_webc):
                        for i in range(11):
                            gb_, ub_ = PAIRB[i % 2]
                            v.wait_ge(sACT.h, ac0 + (1 if exp is not None else 0) + i + 1)
                            v.wait_ge(sPE.h, pe0 + pe_webc + i + 1)
                            hdst = arena[:, HSL[i][0] * SLOT + HSL[i][1]:
                                         HSL[i][0] * SLOT + HSL[i][1] + 1024]
                            hdst2 = arena[:, HSL[i][0] * SLOT + HSL[i][1] + 512:
                                          HSL[i][0] * SLOT + HSL[i][1] + 1024]
                            if exp is not None:
                                vedge(v, v.tensor_mul(sl(*SWC, ln=1024),
                                                      sl(*SGS, ln=1024),
                                                      sl(*WEBC, ln=1024)))
                                if i % 2 == 0:
                                    vv = v.tensor_mul(hdst, sl(*SWC, ln=1024),
                                                      pS[:, 1024:2048])
                                else:
                                    v.tensor_mul(
                                        arena[:, HSL[i][0] * SLOT + HSL[i][1]:
                                              HSL[i][0] * SLOT + HSL[i][1] + 512],
                                        sl(*SWC, ln=512), pV[:, 0:512])
                                    vv = v.tensor_mul(hdst2,
                                                      arena[:, SWC[0] * SLOT + SWC[1] + 512:
                                                            SWC[0] * SLOT + SWC[1] + 1024],
                                                      pV[:, 512:1024])
                            else:
                                if i % 2 == 0:
                                    vv = v.tensor_mul(hdst, sl(*SGS, ln=1024),
                                                      pS[:, 1024:2048])
                                else:
                                    v.tensor_mul(
                                        arena[:, HSL[i][0] * SLOT + HSL[i][1]:
                                              HSL[i][0] * SLOT + HSL[i][1] + 512],
                                        sl(*SGS, ln=512), pV[:, 0:512])
                                    vv = v.tensor_mul(hdst2,
                                                      arena[:, SGS[0] * SLOT + SGS[1] + 512:
                                                            SGS[0] * SLOT + SGS[1] + 1024],
                                                      pV[:, 512:1024])
                            sDVE.inc(vv, 1)
                        if not job['first']:
                            for tbl in range(8):
                                v.wait_ge(sPE.h, pe0 + pe_webc + 11 + tbl + 1)
                                banks = ([pS[:, 0:512], pS[:, 512:1024], pS[:, 1024:1536],
                                          pS[:, 1536:2048]] if tbl % 2 == 0 else
                                         [pT[:, :], pT2[:, :], pV[:, 0:512], pV[:, 512:1024]])
                                vv = None
                                for cc4 in range(4):
                                    acs = sl(ACC[tbl], cc4 * 512, 512)
                                    vv = v.tensor_add(acs, acs, banks[cc4])
                                sDVE.inc(vv, 1)

            with nc.Block() as blk:
                pe0, ac0, dv0, s0 = sPE.n, sACT.n, sDVE.n, sS.n
                NDV2, NAC2 = 4, 2

                def dvm2(tbl, i):
                    return dv0 + NDV2 * tbl + i

                def acm2(tbl, i):
                    return ac0 + NAC2 * tbl + i

                @blk.sync
                def _(sy, ph=ph, s0=s0, dv0=dv0):
                    for tbl in range(8):
                        if tbl >= 1:
                            sy.wait_ge(sDVE.h, dvm2(tbl - 1, 1))
                        sS.inc(sy.dma_start(sl(LN2TM), ln1tm_d.ap()[ph * 8 + tbl]), 16)
                        sy.wait_ge(sDVE.h, dvm2(tbl, 4))
                        sS.inc(sy.dma_start(
                            out_d.ap()[(ph * 8 + tbl) * P:(ph * 8 + tbl + 1) * P, :],
                            fs[:, 2048:4096]), 16)
                    cnt[f'out_st{ph}'] = sS.n

                @blk.vector
                def _(v, ph=ph, s0=s0, ac0=ac0):
                    for tbl in range(8):
                        v.wait_ge(sS.h, s0 + 32 * tbl + 16)
                        if tbl >= 1:
                            v.wait_ge(sS.h, s0 + 32 * tbl)
                        vv = v.tensor_add(fs[:, 0:2048], sl(LN2TM), sl(ACC[tbl]))
                        sDVE.inc(vv, 1)
                        v.wait_ge(sDVE.h, sDVE.n)
                        vedge(v, v.reduce_sum(st_sb[:, 0:1], fs[:, 0:2048], axis=AX.X))
                        vedge(v, v.tensor_scalar(st_sb[:, 1:2], st_sb[:, 0:1],
                                                 cpool[:, CP_INV:CP_INV + 1], None,
                                                 ALU.mult))
                        vv = v.tensor_mul(st_sb[:, 2:3], st_sb[:, 0:1], st_sb[:, 1:2])
                        sDVE.inc(vv, 1)
                        v.wait_ge(sDVE.h, sDVE.n)
                        v.wait_ge(sACT.h, acm2(tbl, 1))
                        vedge(v, v.tensor_sub(st_sb[:, 4:5], st_sb[:, 3:4],
                                              st_sb[:, 2:3]))
                        vv = v.tensor_scalar(st_sb[:, 5:6], st_sb[:, 4:5],
                                             cpool[:, CP_INV:CP_INV + 1], None, ALU.mult)
                        sDVE.inc(vv, 1)
                        v.wait_ge(sDVE.h, sDVE.n)
                        v.wait_ge(sACT.h, acm2(tbl, 2))
                        rr = v.reciprocal(st_sb[:, 7:8], st_sb[:, 6:7])
                        sR.inc(rr, 1)
                        v.wait_ge(sR.h, sR.n)
                        vv = v.tensor_scalar(fs[:, 2048:4096], fs[:, 0:2048],
                                             st_sb[:, 1:2], st_sb[:, 7:8],
                                             ALU.subtract, ALU.mult)
                        sDVE.inc(vv, 1)

                @blk.scalar
                def _(sc, dv0=dv0):
                    for tbl in range(8):
                        sc.wait_ge(sDVE.h, dvm2(tbl, 1))
                        cc = sc.activation(pS[:, 0:2048], fs[:, 0:2048], AF.Square,
                                           accum_out=st_sb[:, 3:4])
                        sACT.inc(cc, 1)
                        sc.wait_ge(sDVE.h, dvm2(tbl, 3))
                        cc = sc.activation(st_sb[:, 6:7], st_sb[:, 5:6], AF.Sqrt,
                                           bias=cpool[:, CP_EPS:CP_EPS + 1],
                                           scale=cpool[:, CP_ONE:CP_ONE + 1])
                        sACT.inc(cc, 1)

        with nc.Block() as blk:
            @blk.sync
            def _(sy):
                sy.wait_ge(sS.h, cnt['out_st1'])
                sy.wait_ge(sS.h, cnt['out_st0'])

    return nc



_PROG_CACHE = {}


def _get_prog():
    if 'nc' not in _PROG_CACHE:
        _PROG_CACHE['nc'] = build_program()
    return _PROG_CACHE['nc']


def kernel(L0_attn_x, L0_attn_Wq, L0_attn_bq, L0_attn_Wk, L0_attn_bk,
           L0_attn_Wv, L0_attn_bv, L0_attn_Wo, L0_attn_bo,
           L0_ffn_gate_w, L0_ffn_gate_bias, L0_ffn_shared_gw, L0_ffn_shared_dw,
           L0_ffn_up_w, L0_ffn_down_w):
    x = np.asarray(L0_attn_x, dtype=np.float32)

    perm = np.concatenate([np.arange(0, D, 2), np.arange(1, D, 2)])
    qcols = np.concatenate([h * D + perm for h in range(HQ)])
    kcols = np.concatenate([h * D + perm for h in range(HKV)])
    wq = np.ascontiguousarray(np.asarray(L0_attn_Wq, np.float32)[:, qcols])
    bq = np.ascontiguousarray(np.asarray(L0_attn_bq, np.float32)[qcols])[None, :]
    wk = np.ascontiguousarray(np.asarray(L0_attn_Wk, np.float32)[:, kcols])
    bk = np.ascontiguousarray(np.asarray(L0_attn_bk, np.float32)[kcols])[None, :]
    wv = np.ascontiguousarray(np.asarray(L0_attn_Wv, np.float32))
    bv = np.ascontiguousarray(np.asarray(L0_attn_bv, np.float32))[None, :]
    wo = np.ascontiguousarray(np.asarray(L0_attn_Wo, np.float32))
    bo = np.ascontiguousarray(
        np.asarray(L0_attn_bo, np.float32).reshape(NT, P).T)

    gw = np.ascontiguousarray(np.asarray(L0_ffn_gate_w, np.float32))
    gb = np.ascontiguousarray(np.asarray(L0_ffn_gate_bias, np.float32))[None, :]

    sgw = np.asarray(L0_ffn_shared_gw, np.float32)
    yb = sgw[0:HSH].reshape(22, P, C)
    gbk = sgw[HSH:2 * HSH].reshape(22, P, C)
    sgw_p = np.ascontiguousarray(
        np.stack([gbk, yb], axis=1).reshape(2 * HSH, C))
    sdw = np.ascontiguousarray(np.asarray(L0_ffn_shared_dw, np.float32))

    upw = np.asarray(L0_ffn_up_w, np.float32)
    gpart = upw[:, 0:HEX].reshape(NE, 11, P, C)
    upart = upw[:, HEX:2 * HEX].reshape(NE, 11, P, C)
    upw_p = np.ascontiguousarray(
        np.stack([gpart, upart], axis=2).reshape(NE, 2 * HEX, C))
    dww = np.ascontiguousarray(np.asarray(L0_ffn_down_w, np.float32))

    inv = (1.0 / (10000.0 ** (np.arange(0, D, 2, dtype=np.float32) / D))).astype(np.float32)
    pos = np.arange(T, dtype=np.float32)
    fr = np.outer(pos, inv).astype(np.float32)
    cosp = np.ascontiguousarray(
        np.cos(fr).astype(np.float32).reshape(NT, P, 32).transpose(1, 0, 2
                                                                   ).reshape(P, NT * 32)).astype(bfloat16)
    sinp = np.ascontiguousarray(
        np.sin(fr).astype(np.float32).reshape(NT, P, 32).transpose(1, 0, 2
                                                                   ).reshape(P, NT * 32)).astype(bfloat16)

    eyeb = np.eye(P, dtype=np.float32).astype(bfloat16)
    eyef = np.eye(P, dtype=np.float32)
    ub = np.zeros((P, P), dtype=np.float32)
    ub[np.triu_indices(P, 1)] = -1e9
    ubb = ub.astype(bfloat16)
    sel8 = np.zeros((8, 8 * P), dtype=np.float32)
    for e in range(8):
        sel8[e, e * P:(e + 1) * P] = 1.0
    sel8 = sel8.astype(bfloat16)

    shared = dict(wq=wq, bq=bq, wk=wk, bk=bk, wv=wv, bv=bv, wo=wo, bo_c=bo,
                  gw=gw, gb=gb, sgw=sgw_p, sdw=sdw, upw=upw_p, dww=dww,
                  cosp=cosp, sinp=sinp, eyeb=eyeb, eyef=eyef, ubias=ubb,
                  sel8=sel8)
    in_maps = []
    for c in range(8):
        m = dict(shared)
        m['x'] = np.ascontiguousarray(x[c])
        in_maps.append(m)

    nc = _get_prog()
    res = run_bass_kernel_spmd(nc, in_maps, core_ids=list(range(8)))
    globals()['LAST'] = res
    return np.stack([res.results[i]['out'] for i in range(8)], axis=0)


# version 04 (final)
# speedup vs baseline: 1.1474x; 1.0747x over previous
import numpy as np
import ml_dtypes
from contextlib import ExitStack

import concourse.bass as bass
import concourse.mybir as mybir
from concourse.bass_utils import run_bass_kernel_spmd
import concourse.bass_utils as _bass_utils

if not getattr(_bass_utils, "_ldw_opt_patched", False):
    _orig_run_command = _bass_utils.run_command

    def _run_command_ldw(argv, **kwargs):
        argv = [a.replace("--enable-ldw-opt=false", "--enable-ldw-opt=true")
                if isinstance(a, str) else a for a in argv]
        return _orig_run_command(argv, **kwargs)

    _bass_utils.run_command = _run_command_ldw
    _bass_utils._ldw_opt_patched = True


F32 = mybir.dt.float32
BF16 = mybir.dt.bfloat16
AF = mybir.ActivationFunctionType
ALU = mybir.AluOpType
AX = mybir.AxisListType
bfloat16 = ml_dtypes.bfloat16

P = 128
T = 2048
C = 2048
NT = 16
HQ, HKV, D = 32, 8, 64
KVW = HKV * D
NE = 8
HEX = 1408
HSH = 2816
EXP_C = 9.0
EPS = 1e-5

NSLOT = 45
SLOT = 2048

XFM = list(range(16))
QTM = list(range(16, 32))
KTM = [(32 + tb // 4, (tb % 4) * 512) for tb in range(NT)]
VTM = [(36 + tb // 4, (tb % 4) * 512) for tb in range(NT)]
WST = [(40 + c // 4, (c % 4) * 512) for c in range(16)]
SCR44 = 44
KDUP = list(range(8))
QFM = [8, 9, 10, 11, 12, 13, 14, 15, 32, 33, 34, 35, 40, 41, 42, 43]
PTR = 16
PTM = [24, 25]
ATP = list(range(16, 32))
WOS = [0, 2]
LN1FM = [0, 1, 2, 3, 4, 5, 6, 7, 36, 37, 38, 39, 40, 41, 42, 43]
LN1TMS = [44, 34]
GWSB = 8
GLOG = 9
WER8 = 43
LN1H = [(c // 2, (c % 2) * 1024) for c in range(16)]
ACC = list(range(8, 16))
HSL = [(16 + kb // 2, (kb % 2) * 1024) for kb in range(11)]
DWT = list(range(22, 33))
UPT = [[(33 + c // 4, (c % 4) * 512) for c in range(16)],
       [(37 + c // 4, (c % 4) * 512) for c in range(16)]]
WEBC = (41, 0)
SGS = (41, 1024)
SWC = (42, 0)
LN2TM = 44

CP_EXPB, CP_SCL, CP_NEG, CP_INV, CP_EPS, CP_ONE = 0, 1, 2, 3, 4, 5


class Sem:
    def __init__(self, nc, es, name):
        self.h = es.enter_context(nc.semaphore(name))
        self.n = 0

    def inc(self, inst, k):
        inst.then_inc(self.h, k)
        self.n += k
        return self.n


def build_program(stop_phase=None):
    import os
    if stop_phase is None:
        stop_phase = os.environ.get("KRN_STOP_PHASE", "all")
    PHASES = ["b0", "b1", "pk", "pv", "pq1", "proj", "ktr", "qtr", "attn",
              "wo", "ln1", "rout", "ffn", "all"]
    plim = PHASES.index(stop_phase)

    def phase_on(name):
        return PHASES.index(name) <= plim

    nc = bass.Bass("TRN2", target_bir_lowering=False)

    debug_tap = plim < PHASES.index("all")

    x_in = nc.dram_tensor("x", [T, C], F32, kind="ExternalInput")
    wq_d = nc.dram_tensor("wq", [C, C], F32, kind="ExternalInput")
    bq_d = nc.dram_tensor("bq", [1, C], F32, kind="ExternalInput")
    wk_d = nc.dram_tensor("wk", [C, KVW], F32, kind="ExternalInput")
    bk_d = nc.dram_tensor("bk", [1, KVW], F32, kind="ExternalInput")
    wv_d = nc.dram_tensor("wv", [C, KVW], F32, kind="ExternalInput")
    bv_d = nc.dram_tensor("bv", [1, KVW], F32, kind="ExternalInput")
    wo_d = nc.dram_tensor("wo", [C, C], F32, kind="ExternalInput")
    bo_d = nc.dram_tensor("bo_c", [P, NT], F32, kind="ExternalInput")
    gw_d = nc.dram_tensor("gw", [NE, C], F32, kind="ExternalInput")
    gb_d = nc.dram_tensor("gb", [1, NE], F32, kind="ExternalInput")
    sgw_d = nc.dram_tensor("sgw", [2 * HSH, C], F32, kind="ExternalInput")
    sdw_d = nc.dram_tensor("sdw", [C, HSH], F32, kind="ExternalInput")
    upw_d = nc.dram_tensor("upw", [NE, 2 * HEX, C], F32, kind="ExternalInput")
    dww_d = nc.dram_tensor("dww", [NE, C, HEX], F32, kind="ExternalInput")
    cos_d = nc.dram_tensor("cosp", [P, NT * 32], BF16, kind="ExternalInput")
    sin_d = nc.dram_tensor("sinp", [P, NT * 32], BF16, kind="ExternalInput")
    eyeb_d = nc.dram_tensor("eyeb", [P, P], BF16, kind="ExternalInput")
    eyef_d = nc.dram_tensor("eyef", [P, P], F32, kind="ExternalInput")
    ub_d = nc.dram_tensor("ubias", [P, P], BF16, kind="ExternalInput")
    sel8_d = nc.dram_tensor("sel8", [8, 8 * P], BF16, kind="ExternalInput")
    out_d = nc.dram_tensor("out", [T, C], F32, kind="ExternalOutput")

    x_bf = nc.dram_tensor("x_bfs", [T, C], BF16, kind="Internal")
    sgw_bf = nc.dram_tensor("sgw_bfs", [2 * HSH, C], BF16, kind="Internal")
    sdw_bf = nc.dram_tensor("sdw_bfs", [C, HSH], BF16, kind="Internal")
    up_bf = nc.dram_tensor("up_bfs", [NE, 2 * HEX, C], BF16, kind="Internal")
    dw_bf = nc.dram_tensor("dw_bfs", [NE, C, HEX], BF16, kind="Internal")
    ln1tm_d = nc.dram_tensor("ln1tm_d", [NT, P, C], BF16, kind="Internal")
    ln1fm_d = nc.dram_tensor("ln1fm_d", [NT, P, T], BF16, kind="Internal")
    attn_tap_h = int(os.environ.get("KRN_ATTN_TAP_H", "-1"))
    if attn_tap_h >= 0:
        debug_tap = True
    if debug_tap:
        tap_d = nc.dram_tensor("tap", [P, NSLOT * SLOT], BF16, kind="ExternalOutput")
        tapf_d = nc.dram_tensor("tapf", [P, 4096], F32, kind="ExternalOutput")
        tapw_d = nc.dram_tensor("tapw", [P, NT * 8], F32, kind="ExternalOutput")
        tapd_d = nc.dram_tensor("tapd", [P, 8], F32, kind="ExternalOutput")
        tapr_d = nc.dram_tensor("tapr", [P, 48], F32, kind="ExternalOutput")
        tapg_d = nc.dram_tensor("tapg", [P, 2 * P], BF16, kind="ExternalOutput")

    es = ExitStack()
    with es:
        arena = es.enter_context(nc.sbuf_tensor("arena", [P, NSLOT * SLOT], BF16))
        fs = es.enter_context(nc.sbuf_tensor("fs", [P, 4096], F32))
        browt = es.enter_context(nc.sbuf_tensor("browt", [P, SLOT], BF16))
        cos_sb = es.enter_context(nc.sbuf_tensor("cos_sb", [P, NT * 32], BF16))
        sin_sb = es.enter_context(nc.sbuf_tensor("sin_sb", [P, NT * 32], BF16))
        eye_bf = es.enter_context(nc.sbuf_tensor("eye_bf", [P, P], BF16))
        eye_f = es.enter_context(nc.sbuf_tensor("eye_f", [P, P], F32))
        ub_sb = es.enter_context(nc.sbuf_tensor("ub_sb", [P, P], BF16))
        sel8_sb = es.enter_context(nc.sbuf_tensor("sel8_sb", [8, 8 * P], BF16))
        ones_bf = es.enter_context(nc.sbuf_tensor("ones_bf", [P, P], BF16))
        ones_f = es.enter_context(nc.sbuf_tensor("ones_f", [1, P], F32))
        cpool = es.enter_context(nc.sbuf_tensor("cpool", [P, 8], F32))
        bo_sb = es.enter_context(nc.sbuf_tensor("bo_sb", [P, NT], F32))
        gwt_sb = es.enter_context(nc.sbuf_tensor("gwt_sb", [P, P], BF16))
        dn_sb = es.enter_context(nc.sbuf_tensor("dn_sb", [P, 2], F32))
        rc_sb = es.enter_context(nc.sbuf_tensor("rc_sb", [P, 1], F32))
        diag_sb = es.enter_context(nc.sbuf_tensor("diag_sb", [P, 2 * P], BF16))
        st_sb = es.enter_context(nc.sbuf_tensor("st_sb", [P, 8], F32))
        rt_sb = es.enter_context(nc.sbuf_tensor("rt_sb", [P, 48], F32))
        we_all = es.enter_context(nc.sbuf_tensor("we_all", [P, NT * 8], F32))
        pS = es.enter_context(nc.psum_tensor("pS", [P, 2048], F32))
        pT = es.enter_context(nc.psum_tensor("pT", [P, 512], F32))
        pT2 = es.enter_context(nc.psum_tensor("pT2", [P, 512], F32))
        pV = es.enter_context(nc.psum_tensor("pV", [P, 1024], F32))

        sQ = Sem(nc, es, "sQ")
        sS = Sem(nc, es, "sS")
        sPE = Sem(nc, es, "sPE")
        sACT = Sem(nc, es, "sACT")
        sDVE = Sem(nc, es, "sDVE")
        sR = Sem(nc, es, "sR")

        def vedge(v, inst):
            sR.inc(inst, 1)
            v.wait_ge(sR.h, sR.n)

        def sl(i, off=0, ln=SLOT):
            return arena[:, i * SLOT + off: i * SLOT + off + ln]

        def strided(view, stride, count, width):
            return bass.AP(view.tensor, view.offset,
                           [view.ap[0], [stride, count], [1, width]])

        cnt = {}

        def emit_dump():
            with nc.Block() as blk:
                @blk.sync
                def _(sy):
                    sS.inc(sy.dma_start(tap_d.ap(), arena[:, :]), 16)
                    sS.inc(sy.dma_start(tapf_d.ap(), fs[:, :]), 16)
                    sS.inc(sy.dma_start(tapw_d.ap(), we_all[:, :]), 16)
                    sS.inc(sy.dma_start(tapd_d.ap()[:, 0:2], dn_sb[:, :], allow_slow_non_contiguous=True), 16)
                    sS.inc(sy.dma_start(tapd_d.ap()[:, 2:3], rc_sb[:, :], allow_slow_non_contiguous=True), 16)
                    sS.inc(sy.dma_start(tapg_d.ap(), diag_sb[:, :]), 16)
                    sS.inc(sy.dma_start(tapr_d.ap(), rt_sb[:, :]), 16)
                    sy.wait_ge(sS.h, sS.n)
            return nc

        with nc.Block() as blk:
            @blk.gpsimd
            def _(g):
                cnt['x_bf'] = sQ.inc(g.dma_start(x_bf.ap(), x_in.ap()), 16)
                cnt['sgw'] = sQ.inc(g.dma_start(sgw_bf.ap(), sgw_d.ap()), 16)
                cnt['sdw'] = sQ.inc(g.dma_start(sdw_bf.ap(), sdw_d.ap()), 16)
                for e in range(NE):
                    cnt[f'up{e}'] = sQ.inc(g.dma_start(up_bf.ap()[e], upw_d.ap()[e]), 16)
                    cnt[f'dw{e}'] = sQ.inc(g.dma_start(dw_bf.ap()[e], dww_d.ap()[e]), 16)

            @blk.sync
            def _(sy):
                sS.inc(sy.dma_start(cos_sb[:, :], cos_d.ap()), 16)
                sS.inc(sy.dma_start(sin_sb[:, :], sin_d.ap()), 16)
                sS.inc(sy.dma_start(eye_bf[:, :], eyeb_d.ap()), 16)
                sS.inc(sy.dma_start(eye_f[:, :], eyef_d.ap()), 16)
                sS.inc(sy.dma_start(ub_sb[:, :], ub_d.ap()), 16)
                sS.inc(sy.dma_start(sel8_sb[0:8, :], sel8_d.ap()), 16)
                sS.inc(sy.dma_start(bo_sb[:, :], bo_d.ap()), 16)
                sS.inc(sy.dma_start(fs[0:1, 2048:4096], bq_d.ap()), 16)
                sS.inc(sy.dma_start(fs[0:1, 0:KVW], bk_d.ap()), 16)
                cnt['consts'] = sS.inc(sy.dma_start(fs[0:1, 512:1024], bv_d.ap()), 16)

            @blk.vector
            def _(v):
                v.memset(ones_bf[:, :], 1.0)
                v.memset(ones_f[0:1, :], 1.0)
                v.memset(cpool[:, CP_EXPB:CP_EXPB + 1], -EXP_C)
                v.memset(cpool[:, CP_SCL:CP_SCL + 1], 0.125)
                v.memset(cpool[:, CP_NEG:CP_NEG + 1], -1e9)
                v.memset(cpool[:, CP_INV:CP_INV + 1], 1.0 / C)
                v.memset(cpool[:, CP_EPS:CP_EPS + 1], EPS)
                v.memset(cpool[:, CP_ONE:CP_ONE + 1], 1.0)

        if not phase_on("b1"):
            return emit_dump()
        with nc.Block() as blk:
            @blk.sync
            def _(sy):
                sy.wait_ge(sQ.h, cnt['x_bf'])
                for c in range(16):
                    sS.inc(sy.dma_start(sl(XFM[c]), x_bf.ap()[:, c * P:(c + 1) * P],
                                        transpose=True), 16)
                cnt['xfm'] = sS.n

            @blk.gpsimd
            def _(g):
                dst = arena[:, 40 * SLOT: 40 * SLOT + 16 * 512].rearrange(
                    "p (c s) -> p c s", s=512)
                src = wk_d.ap().rearrange("(c p) s -> p c s", p=P)
                cnt['wk'] = sQ.inc(g.dma_start(dst, src), 16)

            @blk.scalar
            def _(sc):
                sc.wait_ge(sS.h, cnt['consts'])
                cc = sc.copy(browt[0:1, 0:1024], fs[0:1, 0:1024])
                cnt['bias_kv_hi'] = sACT.inc(cc, 1)

            @blk.vector
            def _(v):
                v.wait_ge(sACT.h, cnt['bias_kv_hi'])
                vv = v.tensor_sub(fs[0:1, 1024:2048], fs[0:1, 0:1024],
                                  browt[0:1, 0:1024])
                cnt['bias_kv_lo_f'] = sDVE.inc(vv, 1)

        with nc.Block() as blk:
            @blk.scalar
            def _(sc):
                sc.wait_ge(sDVE.h, cnt['bias_kv_lo_f'])
                cc = sc.copy(browt[32:33, 0:1024], fs[0:1, 1024:2048])
                cnt['bias_kv_lo32'] = sACT.inc(cc, 1)

            @blk.gpsimd
            def _(g):
                g.wait_ge(sS.h, cnt['xfm'])
                g.wait_ge(sACT.h, cnt['bias_kv_lo32'])
                g.dma_start(browt[1:2, 0:1024], browt[32:33, 0:1024]).then_inc(sQ.h, 16)
                sQ.n += 16
                cnt['bias_kv_done'] = sQ.n

        def emit_prep_bq():
            with nc.Block() as blk:
                @blk.scalar
                def _(sc):
                    cc = sc.copy(browt[0:1, 0:C], fs[0:1, 2048:4096])
                    cnt['bias_q_hi'] = sACT.inc(cc, 1)

                @blk.vector
                def _(v):
                    v.wait_ge(sACT.h, cnt['bias_q_hi'])
                    vv = v.tensor_sub(fs[0:1, 0:2048], fs[0:1, 2048:4096],
                                      browt[0:1, 0:C])
                    cnt['bias_q_lo_f'] = sDVE.inc(vv, 1)

            with nc.Block() as blk:
                @blk.scalar
                def _(sc):
                    sc.wait_ge(sDVE.h, cnt['bias_q_lo_f'])
                    cc = sc.copy(browt[32:33, 0:C], fs[0:1, 0:2048])
                    cnt['bias_q_lo32'] = sACT.inc(cc, 1)

                @blk.gpsimd
                def _(g):
                    g.wait_ge(sACT.h, cnt['bias_q_lo32'])
                    g.dma_start(browt[1:2, 0:C], browt[32:33, 0:C]).then_inc(sQ.h, 16)
                    sQ.n += 16
                    cnt['bias_q_done'] = sQ.n

        def emit_rope(v, scr512, dst512, tb):
            sv = scr512.rearrange("p (h two i) -> p h two i", two=2, i=32)
            dv = dst512.rearrange("p (h two i) -> p h two i", two=2, i=32)
            cs = cos_sb[:, tb * 32:(tb + 1) * 32].rearrange("p (o i) -> p o i", o=1)
            sn = sin_sb[:, tb * 32:(tb + 1) * 32].rearrange("p (o i) -> p o i", o=1)
            csb = bass.AP(cs.tensor, cs.offset, [cs.ap[0], [0, 8], cs.ap[2]])
            snb = bass.AP(sn.tensor, sn.offset, [sn.ap[0], [0, 8], sn.ap[2]])
            ev, od = sv[:, :, 0, :], sv[:, :, 1, :]
            t1 = fs[:, 0:256].rearrange("p (h i) -> p h i", i=32)
            t2 = fs[:, 256:512].rearrange("p (h i) -> p h i", i=32)
            v.tensor_mul(t1, ev, csb)
            vedge(v, v.tensor_mul(t2, od, snb))
            vedge(v, v.tensor_sub(dv[:, :, 0, :], t1, t2))
            v.tensor_mul(t1, od, csb)
            vedge(v, v.tensor_mul(t2, ev, snb))
            return v.tensor_add(dv[:, :, 1, :], t1, t2)

        def emit_proj_pass(wkey, bias_rows, bias_key, dst_fn, do_rope, next_w_load):
            with nc.Block() as blk:
                pe0, ac0, dv0 = sPE.n, sACT.n, sDVE.n

                @blk.gpsimd
                def _(g):
                    if next_w_load is not None:
                        name, src_ap = next_w_load
                        g.wait_ge(sPE.h, pe0 + NT)
                        dst = arena[:, 40 * SLOT: 40 * SLOT + 16 * 512].rearrange(
                            "p (c s) -> p c s", s=512)
                        cnt[name] = sQ.inc(g.dma_start(dst, src_ap), 16)

                @blk.tensor
                def _(t):
                    t.wait_ge(sS.h, cnt['xfm'])
                    t.wait_ge(sQ.h, cnt[wkey])
                    t.wait_ge(sQ.h, cnt[bias_key])
                    for tb in range(NT):
                        q4 = tb % 4
                        ps = pS[:, q4 * 512:(q4 + 1) * 512]
                        if tb >= 4:
                            t.wait_ge(sACT.h, ac0 + (tb - 3))
                        for c in range(16):
                            t.matmul(ps, sl(XFM[c], tb * P, P), sl(*WST[c], ln=512),
                                     start=(c == 0), stop=False)
                        mm = t.matmul(ps, bias_rows[0], bias_rows[1],
                                      start=False, stop=True)
                        sPE.inc(mm, 1)

                @blk.scalar
                def _(sc):
                    for tb in range(NT):
                        q4 = tb % 4
                        ps = pS[:, q4 * 512:(q4 + 1) * 512]
                        sc.wait_ge(sPE.h, pe0 + tb + 1)
                        if do_rope:
                            if tb >= 2:
                                sc.wait_ge(sDVE.h, dv0 + 6 * (tb - 1))
                            cc = sc.copy(sl(SCR44, (tb % 2) * 512, 512), ps)
                        else:
                            cc = sc.copy(dst_fn(tb), ps)
                        sACT.inc(cc, 1)

                @blk.vector
                def _(v):
                    if not do_rope:
                        return
                    for tb in range(NT):
                        v.wait_ge(sACT.h, ac0 + tb + 1)
                        vv = emit_rope(v, sl(SCR44, (tb % 2) * 512, 512),
                                       dst_fn(tb), tb)
                        sDVE.inc(vv, 6)

        if not phase_on("pk"):
            return emit_dump()
        bias_k = (ones_bf[0:2, 0:P], browt[0:2, 0:512])
        bias_v = (ones_bf[0:2, 0:P], browt[0:2, 512:1024])

        emit_proj_pass('wk', bias_k, 'bias_kv_done',
                       lambda tb: sl(*KTM[tb], ln=512), True,
                       ('wv', wv_d.ap().rearrange("(c p) s -> p c s", p=P)))
        if not phase_on("pv"):
            return emit_dump()
        emit_proj_pass('wv', bias_v, 'bias_kv_done',
                       lambda tb: sl(*VTM[tb], ln=512), False,
                       ('wq0', wq_d.ap()[:, 0:512].rearrange("(c p) s -> p c s", p=P)))
        if not phase_on("pq1"):
            return emit_dump()
        emit_prep_bq()
        for qq in range(4):
            if qq >= 1 and not phase_on("proj"):
                break
            bias_q = (ones_bf[0:2, 0:P], browt[0:2, qq * 512:(qq + 1) * 512])
            nxt = None
            if qq < 3:
                nxt = (f'wq{qq + 1}',
                       wq_d.ap()[:, (qq + 1) * 512:(qq + 2) * 512].rearrange(
                           "(c p) s -> p c s", p=P))
            emit_proj_pass(f'wq{qq}', bias_q, 'bias_q_done',
                           lambda tb, qq=qq: sl(QTM[tb], qq * 512, 512), True, nxt)

        if not phase_on("ktr"):
            return emit_dump()
        with nc.Block() as blk:
            pe0, dv0 = sPE.n, sDVE.n

            @blk.tensor
            def _(t):
                for tb in range(NT):
                    if tb >= 1:
                        t.wait_ge(sDVE.h, dv0 + 8 * tb)
                    b1 = [pT, pT2][tb % 2]
                    b2 = [pT2, pT][tb % 2]
                    ktile = sl(*KTM[tb], ln=512)
                    mm = None
                    for g in range(HKV):
                        lhs = ktile[:, g * 64:(g + 1) * 64]
                        bank = b1 if g < 4 else b2
                        ps = bank[:, (g % 4) * P:(g % 4) * P + P]
                        t.matmul(ps[0:64, :], lhs, eye_bf[:, :], start=True, stop=True)
                        mm = t.matmul(ps[64:128, :], lhs, eye_bf[:, :], start=True, stop=True)
                    sPE.inc(mm, 1)

            @blk.vector
            def _(v):
                for tb in range(NT):
                    v.wait_ge(sPE.h, pe0 + tb + 1)
                    b1 = [pT, pT2][tb % 2]
                    b2 = [pT2, pT][tb % 2]
                    vv = None
                    for g in range(HKV):
                        bank = b1 if g < 4 else b2
                        ps = bank[:, (g % 4) * P:(g % 4) * P + P]
                        vv = v.tensor_copy(sl(KDUP[g], tb * P, P), ps)
                    sDVE.inc(vv, 8)

        if not phase_on("qtr"):
            return emit_dump()
        with nc.Block() as blk:
            pe0, ac0, dv0 = sPE.n, sACT.n, sDVE.n

            @blk.tensor
            def _(t):
                for tb in range(NT):
                    for grp in range(4):
                        bank = pS[:, grp * 512:(grp + 1) * 512]
                        if tb >= 1:
                            if grp % 2 == 0:
                                t.wait_ge(sDVE.h, dv0 + (tb - 1) * 2 + grp // 2 + 1)
                            else:
                                t.wait_ge(sACT.h, ac0 + (tb - 1) * 2 + grp // 2 + 1)
                        mm = None
                        for j in range(4):
                            d = grp * 4 + j
                            mm = t.matmul(bank[:, j * P:(j + 1) * P],
                                          sl(QTM[tb], d * P, P), eye_bf[:, :],
                                          start=True, stop=True)
                        sPE.inc(mm, 1)

            @blk.vector
            def _(v):
                for tb in range(NT):
                    for grp in (0, 2):
                        bank = pS[:, grp * 512:(grp + 1) * 512]
                        v.wait_ge(sPE.h, pe0 + tb * 4 + grp + 1)
                        vv = None
                        for j in range(4):
                            d = grp * 4 + j
                            vv = v.tensor_copy(sl(QFM[d], tb * P, P),
                                               bank[:, j * P:(j + 1) * P])
                        sDVE.inc(vv, 1)

            @blk.scalar
            def _(sc):
                for tb in range(NT):
                    for grp in (1, 3):
                        bank = pS[:, grp * 512:(grp + 1) * 512]
                        sc.wait_ge(sPE.h, pe0 + tb * 4 + grp + 1)
                        cc = None
                        for j in range(4):
                            d = grp * 4 + j
                            cc = sc.copy(sl(QFM[d], tb * P, P),
                                         bank[:, j * P:(j + 1) * P])
                        sACT.inc(cc, 1)

        if not phase_on("attn"):
            return emit_dump()
        pe0a, ac0a, dv0a = sPE.n, sACT.n, sDVE.n
        g_chunks = []
        pe_items_all = []
        act_items_all = []
        dve_items_all = []
        for h in range(HQ):
            for qt in range(NT):
                klen = (qt + 1) * P
                chs = [(0, min(1024, klen))]
                if klen > 1024:
                    chs.append((1, klen - 1024))
                for (ch, cw) in chs:
                    g_chunks.append((h, qt, ch, cw))
                    pe_items_all.append(("S", h, qt, ch, cw))
                    act_items_all.append(("E", h, qt, ch, cw))
                dve_items_all.append(("DG", h, qt, len(chs)))
                if qt == 9:
                    pe_items_all.append(("PV", h, 0))
                    act_items_all.append(("O", h, 0))
                if qt >= 1:
                    pe_items_all.append(("PT", h, qt - 1))
                    dve_items_all.append(("EV", h, qt - 1, (qt + 3) // 4))
            pe_items_all.append(("PT", h, NT - 1))
            dve_items_all.append(("EV", h, NT - 1, (NT + 3) // 4))
            pe_items_all.append(("PV", h, 1))
            act_items_all.append(("O", h, 1))

        s_mark = {}
        pt_gmarks = {}
        pv_mark = {}
        m = 0
        gidx = 0
        for it in pe_items_all:
            if it[0] == "S":
                m += 1
                s_mark[(it[1], it[2], it[3])] = pe0a + m
            elif it[0] == "PT":
                ng = (it[2] + 4) // 4
                mk = []
                for _ in range(ng):
                    m += 1
                    mk.append(pe0a + m)
                pt_gmarks[(it[1], it[2])] = mk
            else:
                m += 1
                pv_mark[(it[1], it[2])] = pe0a + m

        exp_mark = {}
        oev_mark = {}
        a = 0
        for it in act_items_all:
            a += 1
            if it[0] == "E":
                exp_mark[(it[1], it[2], it[3])] = ac0a + a
            else:
                oev_mark[(it[1], it[2])] = ac0a + a

        diag_mark = {}
        evac_gmark = []
        evac_last = {}
        d = 0
        for it in dve_items_all:
            if it[0] == "DG":
                d += (1 if it[3] == 2 else 0) + 2
                diag_mark[(it[1], it[2])] = dv0a + d
            else:
                for _ in range(it[3]):
                    d += 1
                    evac_gmark.append(dv0a + d)
                evac_last[(it[1], it[2])] = dv0a + d

        last_ch = {qt: (1 if (qt + 1) * P > 1024 else 0) for qt in range(NT)}

        s_gci = {}
        for i_, (h_, q_, c_, w_) in enumerate(g_chunks):
            s_gci[(h_, q_, c_)] = i_
        g_chunks = [(h_, q_, c_, w_) for (h_, q_, c_, w_) in g_chunks]

        ptm_prev_user = {}
        dn_prev_user = {}
        last_pt_mark = {0: None, 1: None}
        prev_diag = None
        diag_prev_user = {}
        prev_oev = {}
        last_oev = None
        for it in pe_items_all:
            pass
        for h in range(HQ):
            for qt in range(NT):
                for ch in range(2 if (qt + 1) * P > 1024 else 1):
                    ptm_prev_user[(h, qt, ch)] = last_pt_mark[qt % 2]
                    dn_prev_user[(h, qt, ch)] = prev_diag
                diag_prev_user[(h, qt)] = last_pt_mark[qt % 2]
                prev_diag = diag_mark[(h, qt)]
                if qt >= 1:
                    last_pt_mark[(qt - 1) % 2] = pt_gmarks[(h, qt - 1)][-1]
                if qt == 9:
                    prev_oev[(h, 0)] = last_oev
                    last_oev = oev_mark[(h, 0)]
            last_pt_mark[(NT - 1) % 2] = pt_gmarks[(h, NT - 1)][-1]
            prev_oev[(h, 1)] = last_oev
            last_oev = oev_mark[(h, 1)]

        with nc.Block() as blk:
            @blk.tensor
            def _(t):
                ggl = [0]
                for idx, it in enumerate(pe_items_all):
                    if it[0] == "S":
                        _, h, qt, ch, cw = it
                        hp, g_ = h // 2, h // 4
                        base = (h % 2) * 64
                        ci = g_chunks.index((h, qt, ch)) if False else None
                        gci = s_gci[(h, qt, ch)]
                        ps = pS[:, (gci % 2) * 1024: (gci % 2) * 1024 + cw]
                        if gci >= 2:
                            ph_, pq_, pc_, _ = g_chunks[gci - 2]
                            t.wait_ge(sACT.h, exp_mark[(ph_, pq_, pc_)])
                        lhs = arena[base:base + 64,
                                    QFM[hp] * SLOT + qt * P: QFM[hp] * SLOT + (qt + 1) * P]
                        dpos = qt * P - ch * 1024
                        mm = None
                        pos = 0
                        while pos < cw:
                            w = min(512, cw - pos)
                            kcol = ch * 1024 + pos
                            rhs = arena[base:base + 64,
                                        KDUP[g_] * SLOT + kcol: KDUP[g_] * SLOT + kcol + w]
                            has_diag = (pos <= dpos < pos + w)
                            mm = t.matmul(ps[:, pos:pos + w], lhs, rhs,
                                          start=True, stop=not has_diag)
                            pos += w
                        if 0 <= dpos < cw:
                            mm = t.matmul(ps[:, dpos:dpos + P], eye_bf[:, :],
                                          ub_sb[:, :], start=False, stop=True)
                        sPE.inc(mm, 1)
                    elif it[0] == "PT":
                        _, h, qtp = it
                        nkb = qtp + 1
                        t.wait_ge(sDVE.h, diag_mark[(h, qtp)])
                        t.wait_ge(sACT.h, exp_mark[(h, qtp, last_ch[qtp])])
                        for g0 in range(0, nkb, 4):
                            gw_ = min(4, nkb - g0)
                            gi = ggl[0]
                            bank = [pT, pT2][gi % 2]
                            if gi >= 2:
                                t.wait_ge(sDVE.h, evac_gmark[gi - 2])
                            mm = None
                            for j in range(gw_):
                                kb = g0 + j
                                mm = t.matmul(bank[:, j * P:(j + 1) * P],
                                              sl(PTM[qtp % 2], kb * P, P),
                                              diag_sb[:, (qtp % 2) * P:(qtp % 2 + 1) * P],
                                              start=True, stop=True)
                            sPE.inc(mm, 1)
                            ggl[0] += 1
                    else:
                        _, h, half = it
                        g_ = h // 4
                        hp = h // 2
                        base = (h % 2) * 64
                        kmax = 8 if half == 0 else 16
                        lastq = 7 if half == 0 else 15
                        t.wait_ge(sDVE.h, evac_last[(h, lastq)])
                        prevo = prev_oev[(h, half)]
                        if prevo is not None:
                            t.wait_ge(sACT.h, prevo)
                        mm = None
                        for kb in range(kmax):
                            qs = max(0, kb * P - half * 1024)
                            vt = sl(*VTM[kb], ln=512)
                            lhs = vt[:, g_ * 64:(g_ + 1) * 64]
                            pos = qs
                            while pos < 1024:
                                w = min(512 - (pos % 512), 1024 - pos)
                                mm = t.matmul(pV[0:64, pos:pos + w], lhs,
                                              arena[:, PTR * SLOT + kb * 1024 + pos:
                                                    PTR * SLOT + kb * 1024 + pos + w],
                                              start=(kb == 0),
                                              stop=(kb == kmax - 1))
                                pos += w
                        sPE.inc(mm, 1)

            @blk.scalar
            def _(sc):
                for it in act_items_all:
                    if it[0] == "E":
                        _, h, qt, ch, cw = it
                        gci = s_gci[(h, qt, ch)]
                        ps = pS[:, (gci % 2) * 1024:(gci % 2) * 1024 + cw]
                        sc.wait_ge(sPE.h, s_mark[(h, qt, ch)])
                        pm = ptm_prev_user[(h, qt, ch)]
                        if pm is not None:
                            sc.wait_ge(sPE.h, pm)
                        dm = dn_prev_user[(h, qt, ch)]
                        if dm is not None and ch == 0:
                            sc.wait_ge(sDVE.h, dm)
                        cc = sc.activation(sl(PTM[qt % 2], ch * 1024, cw), ps,
                                           AF.Exp,
                                           bias=cpool[:, CP_EXPB:CP_EXPB + 1],
                                           scale=cpool[:, CP_SCL:CP_SCL + 1],
                                           accum_out=dn_sb[:, ch:ch + 1])
                        sACT.inc(cc, 1)
                    else:
                        _, h, half = it
                        hp = h // 2
                        base = (h % 2) * 64
                        sc.wait_ge(sPE.h, pv_mark[(h, half)])
                        obase = QFM[hp] * SLOT + half * 1024
                        sc.copy(arena[base:base + 64, obase:obase + 512],
                                pV[0:64, 0:512])
                        cc = sc.copy(arena[base:base + 64, obase + 512:obase + 1024],
                                     pV[0:64, 512:1024])
                        sACT.inc(cc, 1)

            @blk.vector
            def _(v):
                gi2 = [0]
                for it in dve_items_all:
                    if it[0] == "DG":
                        _, h, qt, nch = it
                        v.wait_ge(sACT.h, exp_mark[(h, qt, last_ch[qt])])
                        pm = diag_prev_user[(h, qt)]
                        if pm is not None:
                            v.wait_ge(sPE.h, pm)
                        k = 0
                        if nch == 2:
                            aa = v.tensor_add(st_sb[:, 0:1], dn_sb[:, 0:1],
                                              dn_sb[:, 1:2])
                            sR.inc(aa, 1)
                            v.wait_ge(sR.h, sR.n)
                            srcv = st_sb[:, 0:1]
                            k = 1
                        else:
                            srcv = dn_sb[:, 0:1]
                        rr = v.reciprocal(rc_sb[:, 0:1], srcv)
                        sR.inc(rr, 1)
                        v.wait_ge(sR.h, sR.n)
                        vv = v.tensor_scalar(
                            diag_sb[:, (qt % 2) * P:(qt % 2 + 1) * P],
                            eye_bf[:, :], rc_sb[:, 0:1], None, ALU.mult)
                        sDVE.inc(vv, k + 2)
                    else:
                        _, h, qtp, ng = it
                        for gg in range(ng):
                            v.wait_ge(sPE.h, pt_gmarks[(h, qtp)][gg])
                            gi = gi2[0]
                            bank = [pT, pT2][gi % 2]
                            gw_ = min(4, qtp + 1 - gg * 4)
                            dstb = arena[:, PTR * SLOT + (gg * 4) * 1024 + (qtp % 8) * P:
                                         PTR * SLOT + (gg * 4) * 1024 + (qtp % 8) * P + P]
                            dst = strided(dstb, 1024, gw_, P)
                            vv = v.tensor_copy(dst, bank[:, 0:gw_ * P])
                            sDVE.inc(vv, 1)
                            gi2[0] += 1


        with nc.Block() as blk:
            pe0, ac0, dv0, q0 = sPE.n, sACT.n, sDVE.n, sQ.n

            @blk.gpsimd
            def _(g):
                for cb in range(16):
                    if cb >= 2:
                        g.wait_ge(sPE.h, pe0 + cb - 1)
                    b = WOS[cb % 2]
                    dst = arena[:, b * SLOT: b * SLOT + 16 * P].rearrange(
                        "p (c s) -> p c s", s=P)
                    src = wo_d.ap()[:, cb * P:(cb + 1) * P].rearrange("(c p) s -> p c s", p=P)
                    sQ.inc(g.dma_start(dst, src), 16)

            @blk.tensor
            def _(t):
                for cb in range(16):
                    t.wait_ge(sQ.h, q0 + 16 * (cb + 1))
                    if cb >= 1:
                        t.wait_ge(sDVE.h, dv0 + cb)
                        t.wait_ge(sACT.h, ac0 + cb)
                    b = WOS[cb % 2]
                    mm = None
                    for cin in range(16):
                        lhs = sl(b, cin * P, P)
                        for tch in range(4):
                            mm = t.matmul(pS[:, tch * 512:(tch + 1) * 512], lhs,
                                          sl(QFM[cin], tch * 512, 512),
                                          start=(cin == 0), stop=(cin == 15))
                    sPE.inc(mm, 1)

            @blk.vector
            def _(v):
                for cb in range(16):
                    v.wait_ge(sPE.h, pe0 + cb + 1)
                    vv = None
                    for tch in (0, 1):
                        vv = v.tensor_scalar(sl(ATP[cb], tch * 512, 512),
                                             pS[:, tch * 512:(tch + 1) * 512],
                                             bo_sb[:, cb:cb + 1], None, ALU.add)
                    sDVE.inc(vv, 1)

            @blk.scalar
            def _(sc):
                for cb in range(16):
                    sc.wait_ge(sPE.h, pe0 + cb + 1)
                    cc = None
                    for tch in (2, 3):
                        cc = sc.add(sl(ATP[cb], tch * 512, 512),
                                    pS[:, tch * 512:(tch + 1) * 512],
                                    bo_sb[:, cb:cb + 1])
                    sACT.inc(cc, 1)

        if not phase_on("ln1"):
            return emit_dump()
        with nc.Block() as blk:
            pe0, ac0, dv0, s0 = sPE.n, sACT.n, sDVE.n, sS.n
            NDV, NAC = 5, 3

            def dvm(tb, i):
                return dv0 + NDV * tb + i

            def acm(tb, i):
                return ac0 + NAC * tb + i

            @blk.sync
            def _(sy):
                for tb in range(NT):
                    if tb >= 1:
                        sy.wait_ge(sDVE.h, dvm(tb - 1, 1))
                    sS.inc(sy.dma_start(fs[:, 2048:4096],
                                        x_in.ap()[tb * P:(tb + 1) * P, :]), 16)
                    sy.wait_ge(sDVE.h, dvm(tb, 4))
                    sS.inc(sy.dma_start(ln1tm_d.ap()[tb], sl(LN1TMS[tb % 2])), 16)
                sy.wait_ge(sDVE.h, dv0 + NDV * NT)
                sy.wait_ge(sACT.h, ac0 + NAC * NT)
                for c in range(16):
                    sS.inc(sy.dma_start(ln1fm_d.ap()[c], sl(LN1FM[c])), 16)
                cnt['ln1fm_stored'] = sS.n

            @blk.tensor
            def _(t):
                for tb in range(NT):
                    if tb >= 1:
                        t.wait_ge(sDVE.h, dvm(tb - 1, 5))
                        t.wait_ge(sACT.h, acm(tb - 1, 3))
                    mm = None
                    for c in range(16):
                        mm = t.matmul(pS[:, c * P:(c + 1) * P],
                                      sl(ATP[c], tb * P, P), eye_bf[:, :],
                                      start=True, stop=True)
                    sPE.inc(mm, 1)
                    t.wait_ge(sDVE.h, dvm(tb, 4))
                    mm = None
                    for c in range(16):
                        mm = t.matmul(pS[:, c * P:(c + 1) * P],
                                      sl(LN1TMS[tb % 2], c * P, P), eye_bf[:, :],
                                      start=True, stop=True)
                    sPE.inc(mm, 1)

            @blk.vector
            def _(v):
                for tb in range(NT):
                    v.wait_ge(sPE.h, pe0 + 2 * tb + 1)
                    v.wait_ge(sS.h, s0 + 32 * tb + 16)
                    vv = None
                    for q4 in range(4):
                        vv = v.tensor_add(fs[:, q4 * 512:(q4 + 1) * 512],
                                          pS[:, q4 * 512:(q4 + 1) * 512],
                                          fs[:, 2048 + q4 * 512:2048 + (q4 + 1) * 512])
                    sDVE.inc(vv, 1)
                    v.wait_ge(sDVE.h, sDVE.n)
                    vedge(v, v.reduce_sum(st_sb[:, 0:1], fs[:, 0:2048], axis=AX.X))
                    vedge(v, v.tensor_scalar(st_sb[:, 1:2], st_sb[:, 0:1],
                                             cpool[:, CP_INV:CP_INV + 1], None, ALU.mult))
                    vv = v.tensor_mul(st_sb[:, 2:3], st_sb[:, 0:1], st_sb[:, 1:2])
                    sDVE.inc(vv, 1)
                    v.wait_ge(sDVE.h, sDVE.n)
                    v.wait_ge(sACT.h, acm(tb, 1))
                    vedge(v, v.tensor_sub(st_sb[:, 4:5], st_sb[:, 3:4], st_sb[:, 2:3]))
                    vv = v.tensor_scalar(st_sb[:, 5:6], st_sb[:, 4:5],
                                         cpool[:, CP_INV:CP_INV + 1], None, ALU.mult)
                    sDVE.inc(vv, 1)
                    v.wait_ge(sDVE.h, sDVE.n)
                    v.wait_ge(sACT.h, acm(tb, 2))
                    rr = v.reciprocal(st_sb[:, 7:8], st_sb[:, 6:7])
                    sR.inc(rr, 1)
                    v.wait_ge(sR.h, sR.n)
                    if tb >= 2:
                        v.wait_ge(sS.h, s0 + 32 * (tb - 2) + 32)
                    vv = v.tensor_scalar(sl(LN1TMS[tb % 2]), fs[:, 0:2048],
                                         st_sb[:, 1:2], st_sb[:, 7:8],
                                         ALU.subtract, ALU.mult)
                    sDVE.inc(vv, 1)
                    v.wait_ge(sPE.h, pe0 + 2 * tb + 2)
                    vv = None
                    for grp in (0, 2):
                        c0 = grp * 4
                        dstb = arena[:, LN1FM[c0] * SLOT + tb * P:
                                     LN1FM[c0] * SLOT + tb * P + P]
                        vv = v.tensor_copy(strided(dstb, SLOT, 4, P),
                                           pS[:, c0 * P:c0 * P + 4 * P])
                    sDVE.inc(vv, 1)

            @blk.scalar
            def _(sc):
                for tb in range(NT):
                    sc.wait_ge(sDVE.h, dvm(tb, 1))
                    if tb >= 2:
                        sc.wait_ge(sS.h, s0 + 32 * (tb - 2) + 32)
                    cc = sc.activation(sl(LN1TMS[tb % 2]), fs[:, 0:2048], AF.Square,
                                       accum_out=st_sb[:, 3:4])
                    sACT.inc(cc, 1)
                    sc.wait_ge(sDVE.h, dvm(tb, 3))
                    cc = sc.activation(st_sb[:, 6:7], st_sb[:, 5:6], AF.Sqrt,
                                       bias=cpool[:, CP_EPS:CP_EPS + 1],
                                       scale=cpool[:, CP_ONE:CP_ONE + 1])
                    sACT.inc(cc, 1)
                    sc.wait_ge(sPE.h, pe0 + 2 * tb + 2)
                    cc = None
                    for grp in (1, 3):
                        c0 = grp * 4
                        dstb = arena[:, LN1FM[c0] * SLOT + tb * P:
                                     LN1FM[c0] * SLOT + tb * P + P]
                        cc = sc.copy(strided(dstb, SLOT, 4, P),
                                     pS[:, c0 * P:c0 * P + 4 * P])
                    sACT.inc(cc, 1)

        if not phase_on("rout"):
            return emit_dump()
        with nc.Block() as blk:
            pe0, ac0, dv0 = sPE.n, sACT.n, sDVE.n

            @blk.gpsimd
            def _(g):
                cnt['gw'] = sQ.inc(g.dma_start(arena[0:8, GWSB * SLOT:GWSB * SLOT + C],
                                               gw_d.ap()), 16)

            @blk.sync
            def _(sy):
                cnt['gb'] = sS.inc(sy.dma_start(fs[0:1, 0:8], gb_d.ap()), 16)

            @blk.tensor
            def _(t):
                t.wait_ge(sQ.h, cnt['gw'])
                mm = None
                for c in range(16):
                    mm = t.matmul(pT[:, c * 8:(c + 1) * 8],
                                  arena[0:8, GWSB * SLOT + c * P:GWSB * SLOT + (c + 1) * P],
                                  eye_bf[0:8, 0:8], start=True, stop=True)
                sPE.inc(mm, 1)
                t.wait_ge(sS.h, cnt['gb'])
                mm = t.matmul(pT2[:, 0:8], ones_f[0:1, 0:P], fs[0:1, 0:8],
                              start=True, stop=True)
                sPE.inc(mm, 1)
                t.wait_ge(sDVE.h, dv0 + 1)
                mm = None
                for c in range(16):
                    for tch in range(4):
                        mm = t.matmul(pS[0:8, tch * 512:(tch + 1) * 512],
                                      gwt_sb[:, c * 8:(c + 1) * 8],
                                      sl(LN1FM[c], tch * 512, 512),
                                      start=(c == 0), stop=(c == 15))
                sPE.inc(mm, 1)
                for tb in range(NT):
                    t.wait_ge(sACT.h, ac0 + 1 + tb)
                    mm = t.matmul(pT[:, 0:8],
                                  arena[0:8, GLOG * SLOT + tb * P:GLOG * SLOT + (tb + 1) * P],
                                  eye_bf[0:8, 0:8], start=True, stop=True)
                    sPE.inc(mm, 1)
                    t.wait_ge(sDVE.h, dv0 + 1 + 10 * (tb + 1))
                    mm = t.matmul(pS[0:8, tb * P:(tb + 1) * P],
                                  we_all[:, tb * 8:(tb + 1) * 8], eye_f[:, :],
                                  start=True, stop=True)
                    sPE.inc(mm, 1)

            @blk.vector
            def _(v):
                v.wait_ge(sPE.h, pe0 + 1)
                vv = v.tensor_copy(gwt_sb[:, :], pT[:, 0:P])
                sDVE.inc(vv, 1)
                for tb in range(NT):
                    v.wait_ge(sACT.h, ac0 + 2 + tb)
                    sc_ = rt_sb[:, 0:8]
                    vedge(v, v.tensor_add(rt_sb[:, 8:16], sc_, pT2[:, 0:8]))
                    vedge(v, v.reduce_max(rt_sb[:, 16:17], rt_sb[:, 8:16], axis=AX.X))
                    vedge(v, v.tensor_scalar(rt_sb[:, 24:32], rt_sb[:, 8:16],
                                             rt_sb[:, 16:17], None, ALU.is_ge))
                    vedge(v, v.scalar_tensor_tensor(rt_sb[:, 32:40], rt_sb[:, 24:32],
                                                    cpool[:, CP_NEG:CP_NEG + 1],
                                                    rt_sb[:, 8:16], ALU.mult, ALU.add))
                    vedge(v, v.reduce_max(rt_sb[:, 17:18], rt_sb[:, 32:40], axis=AX.X))
                    vedge(v, v.tensor_scalar(rt_sb[:, 24:32], rt_sb[:, 8:16],
                                             rt_sb[:, 17:18], None, ALU.is_ge))
                    vedge(v, v.tensor_mul(rt_sb[:, 32:40], rt_sb[:, 24:32], sc_))
                    ss = v.reduce_sum(rt_sb[:, 18:19], rt_sb[:, 32:40], axis=AX.X)
                    sR.inc(ss, 1)
                    v.wait_ge(sR.h, sR.n)
                    rr = v.reciprocal(rt_sb[:, 19:20], rt_sb[:, 18:19])
                    sR.inc(rr, 1)
                    v.wait_ge(sR.h, sR.n)
                    vv = v.tensor_scalar(we_all[:, tb * 8:(tb + 1) * 8],
                                         rt_sb[:, 32:40], rt_sb[:, 19:20], None,
                                         ALU.mult)
                    sDVE.inc(vv, 10)

            @blk.scalar
            def _(sc):
                sc.wait_ge(sPE.h, pe0 + 3)
                cc = sc.copy(arena[0:8, GLOG * SLOT:GLOG * SLOT + T], pS[0:8, 0:T])
                sACT.inc(cc, 1)
                for tb in range(NT):
                    sc.wait_ge(sPE.h, pe0 + 4 + 2 * tb)
                    cc = sc.activation(rt_sb[:, 0:8], pT[:, 0:8], AF.Sigmoid)
                    sACT.inc(cc, 1)
                sc.wait_ge(sPE.h, pe0 + 5 + 2 * (NT - 1))
                cc = sc.copy(arena[0:8, WER8 * SLOT:WER8 * SLOT + T], pS[0:8, 0:T])
                cnt['we_r8'] = sACT.inc(cc, 1)

        if not phase_on("ffn"):
            return emit_dump()
        jobs = []
        jobs.append(dict(name="shA", up=sgw_bf.ap()[0:HSH, :],
                         dn=lambda kb: sdw_bf.ap()[:, kb * P:(kb + 1) * P],
                         wcnt='sgw', dcnt='sdw', expert=None, first=True))
        jobs.append(dict(name="shB", up=sgw_bf.ap()[HSH:2 * HSH, :],
                         dn=lambda kb: sdw_bf.ap()[:, HEX + kb * P:HEX + (kb + 1) * P],
                         wcnt='sgw', dcnt='sdw', expert=None, first=False))
        for e in range(NE):
            jobs.append(dict(name=f"e{e}", up=up_bf.ap()[e],
                             dn=lambda kb, e=e: dw_bf.ap()[e][:, kb * P:(kb + 1) * P],
                             wcnt=f'up{e}', dcnt=f'dw{e}', expert=e, first=False))

        RW = [512, 512, 512, 512, 512, 256]
        PAIRB = [
            ((pS[:, 0:512], pS[:, 512:1024]), (pS[:, 1024:1536], pS[:, 1536:2048])),
            ((pT[:, :], pT2[:, :]), (pV[:, 0:512], pV[:, 512:1024])),
        ]

        for ph in range(2):
            for ji, job in enumerate(jobs):
                with nc.Block() as blk:
                    pe0, ac0, dv0, s0 = sPE.n, sACT.n, sDVE.n, sS.n
                    exp = job['expert']
                    npair = 11
                    pe_webc = 1 if exp is not None else 0
                    n_ln1h = 16 if ji == 0 else 0
                    range_mark = {}

                    @blk.sync
                    def _(sy, job=job, ji=ji, ph=ph, pe0=pe0, s0=s0, n_ln1h=n_ln1h,
                          range_mark=range_mark, pe_webc=pe_webc):
                        if ji == 0:
                            sy.wait_ge(sS.h, cnt['ln1fm_stored'])
                            for c in range(16):
                                sS.inc(sy.dma_start(
                                    sl(*LN1H[c], ln=1024),
                                    ln1fm_d.ap()[c][:, ph * 1024:(ph + 1) * 1024]), 16)
                        sy.wait_ge(sQ.h, cnt[job['wcnt']])
                        for r in range(2):
                            rw = RW[r]
                            for c in range(16):
                                sS.inc(sy.dma_start(
                                    sl(*UPT[r % 2][c], ln=rw),
                                    job['up'][r * 512:r * 512 + rw, c * P:(c + 1) * P],
                                    transpose=True), 16)
                            range_mark[r] = sS.n
                        sy.wait_ge(sQ.h, cnt[job['dcnt']])
                        for kb in range(11):
                            sS.inc(sy.dma_start(sl(DWT[kb]), job['dn'](kb),
                                                transpose=True), 16)
                        cnt['dwT_done'] = sS.n
                        for r in range(2, 6):
                            lastpair = 2 * (r - 2) + 1
                            sy.wait_ge(sPE.h, pe0 + pe_webc + lastpair + 1)
                            rw = RW[r]
                            for c in range(16):
                                sS.inc(sy.dma_start(
                                    sl(*UPT[r % 2][c], ln=rw),
                                    job['up'][r * 512:r * 512 + rw, c * P:(c + 1) * P],
                                    transpose=True), 16)
                            range_mark[r] = sS.n

                    @blk.tensor
                    def _(t, job=job, ji=ji, ph=ph, pe0=pe0, ac0=ac0, dv0=dv0,
                          s0=s0, exp=exp, range_mark=range_mark, pe_webc=pe_webc,
                          n_ln1h=n_ln1h):
                        if exp is not None:
                            mm = None
                            for cq in range(2):
                                src = arena[0:8, WER8 * SLOT + ph * 1024 + cq * 512:
                                            WER8 * SLOT + ph * 1024 + (cq + 1) * 512]
                                mm = t.matmul(pS[:, cq * 512:(cq + 1) * 512],
                                              sel8_sb[0:8, exp * P:(exp + 1) * P],
                                              src, start=True, stop=True)
                            sPE.inc(mm, 1)
                        for i in range(npair):
                            r = i // 2
                            gb_, ub_ = PAIRB[i % 2]
                            t.wait_ge(sS.h, range_mark[r])
                            if ji == 0 and i == 0:
                                t.wait_ge(sS.h, s0 + 16 * n_ln1h)
                            if exp is not None and i == 0:
                                t.wait_ge(sACT.h, ac0 + 1)
                            if i >= 2:
                                t.wait_ge(sDVE.h, dv0 + i - 1)
                            mm = None
                            for blkrow, bks in ((2 * i, gb_), (2 * i + 1, ub_)):
                                sub = (blkrow % 4) * P
                                for c in range(16):
                                    lhs = arena[:, UPT[r % 2][c][0] * SLOT + UPT[r % 2][c][1] + sub:
                                              UPT[r % 2][c][0] * SLOT + UPT[r % 2][c][1] + sub + P]
                                    for ch in range(2):
                                        mm = t.matmul(
                                            bks[ch],
                                            lhs,
                                            arena[:, LN1H[c][0] * SLOT + LN1H[c][1] + ch * 512:
                                                  LN1H[c][0] * SLOT + LN1H[c][1] + (ch + 1) * 512],
                                            start=(c == 0), stop=(c == 15))
                            sPE.inc(mm, 1)
                        t.wait_ge(sDVE.h, dv0 + npair)
                        t.wait_ge(sS.h, cnt['dwT_done'])
                        for tbl in range(8):
                            banks = ([pS[:, 0:512], pS[:, 512:1024], pS[:, 1024:1536],
                                      pS[:, 1536:2048]] if tbl % 2 == 0 else
                                     [pT[:, :], pT2[:, :], pV[:, 0:512], pV[:, 512:1024]])
                            if tbl >= 2:
                                if job['first']:
                                    t.wait_ge(sACT.h, ac0 + npair + (tbl - 1))
                                else:
                                    t.wait_ge(sDVE.h, dv0 + npair + (tbl - 1))
                            if tbl < 2:
                                t.wait_ge(sACT.h, ac0 + (1 if exp is not None else 0) + npair)
                            mm = None
                            for kb in range(11):
                                lhs = arena[:, HSL[kb][0] * SLOT + HSL[kb][1] + tbl * P:
                                            HSL[kb][0] * SLOT + HSL[kb][1] + (tbl + 1) * P]
                                for cc4 in range(4):
                                    mm = t.matmul(banks[cc4],
                                                  lhs,
                                                  sl(DWT[kb], cc4 * 512, 512),
                                                  start=(kb == 0), stop=(kb == 10))
                            sPE.inc(mm, 1)

                    @blk.scalar
                    def _(sc, job=job, exp=exp, pe0=pe0, ac0=ac0, dv0=dv0,
                          pe_webc=pe_webc):
                        if exp is not None:
                            sc.wait_ge(sPE.h, pe0 + 1)
                            sc.copy(sl(*WEBC, ln=512), pS[:, 0:512])
                            cc = sc.copy(arena[:, WEBC[0] * SLOT + WEBC[1] + 512:
                                               WEBC[0] * SLOT + WEBC[1] + 1024],
                                         pS[:, 512:1024])
                            sACT.inc(cc, 1)
                        for i in range(11):
                            gb_, ub_ = PAIRB[i % 2]
                            sc.wait_ge(sPE.h, pe0 + pe_webc + i + 1)
                            if i % 2 == 0:
                                cc = sc.activation(sl(*SGS, ln=1024), pS[:, 0:1024],
                                                   AF.Silu)
                            else:
                                sc.activation(sl(*SGS, ln=512), pT[:, :], AF.Silu)
                                cc = sc.activation(
                                    arena[:, SGS[0] * SLOT + SGS[1] + 512:
                                          SGS[0] * SLOT + SGS[1] + 1024],
                                    pT2[:, :], AF.Silu)
                            sACT.inc(cc, 1)
                        if job['first']:
                            for tbl in range(8):
                                sc.wait_ge(sPE.h, pe0 + pe_webc + 11 + tbl + 1)
                                banks = ([pS[:, 0:512], pS[:, 512:1024], pS[:, 1024:1536],
                                          pS[:, 1536:2048]] if tbl % 2 == 0 else
                                         [pT[:, :], pT2[:, :], pV[:, 0:512], pV[:, 512:1024]])
                                cc = None
                                for cc4 in range(4):
                                    cc = sc.copy(sl(ACC[tbl], cc4 * 512, 512), banks[cc4])
                                sACT.inc(cc, 1)

                    @blk.vector
                    def _(v, job=job, exp=exp, pe0=pe0, ac0=ac0, dv0=dv0,
                          pe_webc=pe_webc):
                        for i in range(11):
                            gb_, ub_ = PAIRB[i % 2]
                            v.wait_ge(sACT.h, ac0 + (1 if exp is not None else 0) + i + 1)
                            v.wait_ge(sPE.h, pe0 + pe_webc + i + 1)
                            hdst = arena[:, HSL[i][0] * SLOT + HSL[i][1]:
                                         HSL[i][0] * SLOT + HSL[i][1] + 1024]
                            hdst2 = arena[:, HSL[i][0] * SLOT + HSL[i][1] + 512:
                                          HSL[i][0] * SLOT + HSL[i][1] + 1024]
                            if exp is not None:
                                vedge(v, v.tensor_mul(sl(*SWC, ln=1024),
                                                      sl(*SGS, ln=1024),
                                                      sl(*WEBC, ln=1024)))
                                if i % 2 == 0:
                                    vv = v.tensor_mul(hdst, sl(*SWC, ln=1024),
                                                      pS[:, 1024:2048])
                                else:
                                    v.tensor_mul(
                                        arena[:, HSL[i][0] * SLOT + HSL[i][1]:
                                              HSL[i][0] * SLOT + HSL[i][1] + 512],
                                        sl(*SWC, ln=512), pV[:, 0:512])
                                    vv = v.tensor_mul(hdst2,
                                                      arena[:, SWC[0] * SLOT + SWC[1] + 512:
                                                            SWC[0] * SLOT + SWC[1] + 1024],
                                                      pV[:, 512:1024])
                            else:
                                if i % 2 == 0:
                                    vv = v.tensor_mul(hdst, sl(*SGS, ln=1024),
                                                      pS[:, 1024:2048])
                                else:
                                    v.tensor_mul(
                                        arena[:, HSL[i][0] * SLOT + HSL[i][1]:
                                              HSL[i][0] * SLOT + HSL[i][1] + 512],
                                        sl(*SGS, ln=512), pV[:, 0:512])
                                    vv = v.tensor_mul(hdst2,
                                                      arena[:, SGS[0] * SLOT + SGS[1] + 512:
                                                            SGS[0] * SLOT + SGS[1] + 1024],
                                                      pV[:, 512:1024])
                            sDVE.inc(vv, 1)
                        if not job['first']:
                            for tbl in range(8):
                                v.wait_ge(sPE.h, pe0 + pe_webc + 11 + tbl + 1)
                                banks = ([pS[:, 0:512], pS[:, 512:1024], pS[:, 1024:1536],
                                          pS[:, 1536:2048]] if tbl % 2 == 0 else
                                         [pT[:, :], pT2[:, :], pV[:, 0:512], pV[:, 512:1024]])
                                vv = None
                                for cc4 in range(4):
                                    acs = sl(ACC[tbl], cc4 * 512, 512)
                                    vv = v.tensor_add(acs, acs, banks[cc4])
                                sDVE.inc(vv, 1)

            with nc.Block() as blk:
                pe0, ac0, dv0, s0 = sPE.n, sACT.n, sDVE.n, sS.n
                NDV2, NAC2 = 4, 2

                def dvm2(tbl, i):
                    return dv0 + NDV2 * tbl + i

                def acm2(tbl, i):
                    return ac0 + NAC2 * tbl + i

                @blk.sync
                def _(sy, ph=ph, s0=s0, dv0=dv0):
                    for tbl in range(8):
                        if tbl >= 1:
                            sy.wait_ge(sDVE.h, dvm2(tbl - 1, 1))
                        sS.inc(sy.dma_start(sl(LN2TM), ln1tm_d.ap()[ph * 8 + tbl]), 16)
                        sy.wait_ge(sDVE.h, dvm2(tbl, 4))
                        sS.inc(sy.dma_start(
                            out_d.ap()[(ph * 8 + tbl) * P:(ph * 8 + tbl + 1) * P, :],
                            fs[:, 2048:4096]), 16)
                    cnt[f'out_st{ph}'] = sS.n

                @blk.vector
                def _(v, ph=ph, s0=s0, ac0=ac0):
                    for tbl in range(8):
                        v.wait_ge(sS.h, s0 + 32 * tbl + 16)
                        if tbl >= 1:
                            v.wait_ge(sS.h, s0 + 32 * tbl)
                        vv = v.tensor_add(fs[:, 0:2048], sl(LN2TM), sl(ACC[tbl]))
                        sDVE.inc(vv, 1)
                        v.wait_ge(sDVE.h, sDVE.n)
                        vedge(v, v.reduce_sum(st_sb[:, 0:1], fs[:, 0:2048], axis=AX.X))
                        vedge(v, v.tensor_scalar(st_sb[:, 1:2], st_sb[:, 0:1],
                                                 cpool[:, CP_INV:CP_INV + 1], None,
                                                 ALU.mult))
                        vv = v.tensor_mul(st_sb[:, 2:3], st_sb[:, 0:1], st_sb[:, 1:2])
                        sDVE.inc(vv, 1)
                        v.wait_ge(sDVE.h, sDVE.n)
                        v.wait_ge(sACT.h, acm2(tbl, 1))
                        vedge(v, v.tensor_sub(st_sb[:, 4:5], st_sb[:, 3:4],
                                              st_sb[:, 2:3]))
                        vv = v.tensor_scalar(st_sb[:, 5:6], st_sb[:, 4:5],
                                             cpool[:, CP_INV:CP_INV + 1], None, ALU.mult)
                        sDVE.inc(vv, 1)
                        v.wait_ge(sDVE.h, sDVE.n)
                        v.wait_ge(sACT.h, acm2(tbl, 2))
                        rr = v.reciprocal(st_sb[:, 7:8], st_sb[:, 6:7])
                        sR.inc(rr, 1)
                        v.wait_ge(sR.h, sR.n)
                        vv = v.tensor_scalar(fs[:, 2048:4096], fs[:, 0:2048],
                                             st_sb[:, 1:2], st_sb[:, 7:8],
                                             ALU.subtract, ALU.mult)
                        sDVE.inc(vv, 1)

                @blk.scalar
                def _(sc, dv0=dv0):
                    for tbl in range(8):
                        sc.wait_ge(sDVE.h, dvm2(tbl, 1))
                        cc = sc.activation(pS[:, 0:2048], fs[:, 0:2048], AF.Square,
                                           accum_out=st_sb[:, 3:4])
                        sACT.inc(cc, 1)
                        sc.wait_ge(sDVE.h, dvm2(tbl, 3))
                        cc = sc.activation(st_sb[:, 6:7], st_sb[:, 5:6], AF.Sqrt,
                                           bias=cpool[:, CP_EPS:CP_EPS + 1],
                                           scale=cpool[:, CP_ONE:CP_ONE + 1])
                        sACT.inc(cc, 1)

        with nc.Block() as blk:
            @blk.sync
            def _(sy):
                sy.wait_ge(sS.h, cnt['out_st1'])
                sy.wait_ge(sS.h, cnt['out_st0'])

    return nc



_PROG_CACHE = {}


def _get_prog():
    if 'nc' not in _PROG_CACHE:
        _PROG_CACHE['nc'] = build_program()
    return _PROG_CACHE['nc']


def kernel(L0_attn_x, L0_attn_Wq, L0_attn_bq, L0_attn_Wk, L0_attn_bk,
           L0_attn_Wv, L0_attn_bv, L0_attn_Wo, L0_attn_bo,
           L0_ffn_gate_w, L0_ffn_gate_bias, L0_ffn_shared_gw, L0_ffn_shared_dw,
           L0_ffn_up_w, L0_ffn_down_w):
    x = np.asarray(L0_attn_x, dtype=np.float32)

    perm = np.concatenate([np.arange(0, D, 2), np.arange(1, D, 2)])
    qcols = np.concatenate([h * D + perm for h in range(HQ)])
    kcols = np.concatenate([h * D + perm for h in range(HKV)])
    wq = np.ascontiguousarray(np.asarray(L0_attn_Wq, np.float32)[:, qcols])
    bq = np.ascontiguousarray(np.asarray(L0_attn_bq, np.float32)[qcols])[None, :]
    wk = np.ascontiguousarray(np.asarray(L0_attn_Wk, np.float32)[:, kcols])
    bk = np.ascontiguousarray(np.asarray(L0_attn_bk, np.float32)[kcols])[None, :]
    wv = np.ascontiguousarray(np.asarray(L0_attn_Wv, np.float32))
    bv = np.ascontiguousarray(np.asarray(L0_attn_bv, np.float32))[None, :]
    wo = np.ascontiguousarray(np.asarray(L0_attn_Wo, np.float32))
    bo = np.ascontiguousarray(
        np.asarray(L0_attn_bo, np.float32).reshape(NT, P).T)

    gw = np.ascontiguousarray(np.asarray(L0_ffn_gate_w, np.float32))
    gb = np.ascontiguousarray(np.asarray(L0_ffn_gate_bias, np.float32))[None, :]

    sgw = np.asarray(L0_ffn_shared_gw, np.float32)
    yb = sgw[0:HSH].reshape(22, P, C)
    gbk = sgw[HSH:2 * HSH].reshape(22, P, C)
    sgw_p = np.ascontiguousarray(
        np.stack([gbk, yb], axis=1).reshape(2 * HSH, C))
    sdw = np.ascontiguousarray(np.asarray(L0_ffn_shared_dw, np.float32))

    upw = np.asarray(L0_ffn_up_w, np.float32)
    gpart = upw[:, 0:HEX].reshape(NE, 11, P, C)
    upart = upw[:, HEX:2 * HEX].reshape(NE, 11, P, C)
    upw_p = np.ascontiguousarray(
        np.stack([gpart, upart], axis=2).reshape(NE, 2 * HEX, C))
    dww = np.ascontiguousarray(np.asarray(L0_ffn_down_w, np.float32))

    inv = (1.0 / (10000.0 ** (np.arange(0, D, 2, dtype=np.float32) / D))).astype(np.float32)
    pos = np.arange(T, dtype=np.float32)
    fr = np.outer(pos, inv).astype(np.float32)
    cosp = np.ascontiguousarray(
        np.cos(fr).astype(np.float32).reshape(NT, P, 32).transpose(1, 0, 2
                                                                   ).reshape(P, NT * 32)).astype(bfloat16)
    sinp = np.ascontiguousarray(
        np.sin(fr).astype(np.float32).reshape(NT, P, 32).transpose(1, 0, 2
                                                                   ).reshape(P, NT * 32)).astype(bfloat16)

    eyeb = np.eye(P, dtype=np.float32).astype(bfloat16)
    eyef = np.eye(P, dtype=np.float32)
    ub = np.zeros((P, P), dtype=np.float32)
    ub[np.triu_indices(P, 1)] = -1e9
    ubb = ub.astype(bfloat16)
    sel8 = np.zeros((8, 8 * P), dtype=np.float32)
    for e in range(8):
        sel8[e, e * P:(e + 1) * P] = 1.0
    sel8 = sel8.astype(bfloat16)

    shared = dict(wq=wq, bq=bq, wk=wk, bk=bk, wv=wv, bv=bv, wo=wo, bo_c=bo,
                  gw=gw, gb=gb, sgw=sgw_p, sdw=sdw, upw=upw_p, dww=dww,
                  cosp=cosp, sinp=sinp, eyeb=eyeb, eyef=eyef, ubias=ubb,
                  sel8=sel8)
    in_maps = []
    for c in range(8):
        m = dict(shared)
        m['x'] = np.ascontiguousarray(x[c])
        in_maps.append(m)

    nc = _get_prog()
    res = run_bass_kernel_spmd(nc, in_maps, core_ids=list(range(8)))
    globals()['LAST'] = res
    return np.stack([res.results[i]['out'] for i in range(8)], axis=0)


# version 07
# speedup vs baseline: 1.1677x; 1.0177x over previous
import numpy as np
import ml_dtypes
from contextlib import ExitStack

import concourse.bass as bass
import concourse.mybir as mybir
from concourse.bass_utils import run_bass_kernel_spmd
import concourse.bass_utils as _bass_utils

if not getattr(_bass_utils, "_ldw_opt_patched", False):
    _orig_run_command = _bass_utils.run_command

    def _run_command_ldw(argv, **kwargs):
        argv = [a.replace("--enable-ldw-opt=false", "--enable-ldw-opt=true")
                if isinstance(a, str) else a for a in argv]
        return _orig_run_command(argv, **kwargs)

    _bass_utils.run_command = _run_command_ldw
    _bass_utils._ldw_opt_patched = True


F32 = mybir.dt.float32
BF16 = mybir.dt.bfloat16
AF = mybir.ActivationFunctionType
ALU = mybir.AluOpType
AX = mybir.AxisListType
bfloat16 = ml_dtypes.bfloat16

P = 128
T = 2048
C = 2048
NT = 16
HQ, HKV, D = 32, 8, 64
KVW = HKV * D
NE = 8
HEX = 1408
HSH = 2816
EXP_C = 9.0
EPS = 1e-5

NSLOT = 45
SLOT = 2048

XFM = list(range(16))
QTM = list(range(16, 32))
KTM = [(32 + tb // 4, (tb % 4) * 512) for tb in range(NT)]
VTM = [(36 + tb // 4, (tb % 4) * 512) for tb in range(NT)]
WST = [(40 + c // 4, (c % 4) * 512) for c in range(16)]
SCR44 = 44
KDUP = list(range(8))
QFM = [8, 9, 10, 11, 12, 13, 14, 15, 32, 33, 34, 35, 40, 41, 42, 43]
PTR = 16
PTM = [24, 25]
ATP = list(range(16, 32))
WOS = [0, 2]
LN1FM = [0, 1, 2, 3, 4, 5, 6, 7, 36, 37, 38, 39, 40, 41, 42, 43]
LN1TMS = [44, 34]
GWSB = 8
GLOG = 9
WER8 = 43
LN1H = [(c // 2, (c % 2) * 1024) for c in range(16)]
ACC = list(range(8, 16))
HSL = [(16 + kb // 2, (kb % 2) * 1024) for kb in range(11)]
DWT = list(range(22, 33))
UPT = [[(33 + c // 4, (c % 4) * 512) for c in range(16)],
       [(37 + c // 4, (c % 4) * 512) for c in range(16)]]
WEBC = (41, 0)
SGS = (41, 1024)
SWC = (42, 0)
LN2TM = 44

CP_EXPB, CP_SCL, CP_NEG, CP_INV, CP_EPS, CP_ONE = 0, 1, 2, 3, 4, 5


class Sem:
    def __init__(self, nc, es, name):
        self.h = es.enter_context(nc.semaphore(name))
        self.n = 0

    def inc(self, inst, k):
        inst.then_inc(self.h, k)
        self.n += k
        return self.n


def build_program(stop_phase=None):
    import os
    if stop_phase is None:
        stop_phase = os.environ.get("KRN_STOP_PHASE", "all")
    PHASES = ["b0", "b1", "pk", "pv", "pq1", "proj", "ktr", "qtr", "attn",
              "wo", "ln1", "rout", "ffn", "all"]
    plim = PHASES.index(stop_phase)

    def phase_on(name):
        return PHASES.index(name) <= plim

    nc = bass.Bass("TRN2", target_bir_lowering=False)

    debug_tap = plim < PHASES.index("all")

    x_in = nc.dram_tensor("x", [T, C], F32, kind="ExternalInput")
    wq_d = nc.dram_tensor("wq", [C, C], F32, kind="ExternalInput")
    bq_d = nc.dram_tensor("bq", [1, C], F32, kind="ExternalInput")
    wk_d = nc.dram_tensor("wk", [C, KVW], F32, kind="ExternalInput")
    bk_d = nc.dram_tensor("bk", [1, KVW], F32, kind="ExternalInput")
    wv_d = nc.dram_tensor("wv", [C, KVW], F32, kind="ExternalInput")
    bv_d = nc.dram_tensor("bv", [1, KVW], F32, kind="ExternalInput")
    wo_d = nc.dram_tensor("wo", [C, C], F32, kind="ExternalInput")
    bo_d = nc.dram_tensor("bo_c", [P, NT], F32, kind="ExternalInput")
    gw_d = nc.dram_tensor("gw", [NE, C], F32, kind="ExternalInput")
    gb_d = nc.dram_tensor("gb", [1, NE], F32, kind="ExternalInput")
    sgw_d = nc.dram_tensor("sgw", [2 * HSH, C], F32, kind="ExternalInput")
    sdw_d = nc.dram_tensor("sdw", [C, HSH], F32, kind="ExternalInput")
    upw_d = nc.dram_tensor("upw", [NE, 2 * HEX, C], F32, kind="ExternalInput")
    dww_d = nc.dram_tensor("dww", [NE, C, HEX], F32, kind="ExternalInput")
    cos_d = nc.dram_tensor("cosp", [P, NT * 32], BF16, kind="ExternalInput")
    sin_d = nc.dram_tensor("sinp", [P, NT * 32], BF16, kind="ExternalInput")
    eyeb_d = nc.dram_tensor("eyeb", [P, P], BF16, kind="ExternalInput")
    eyef_d = nc.dram_tensor("eyef", [P, P], F32, kind="ExternalInput")
    ub_d = nc.dram_tensor("ubias", [P, P], BF16, kind="ExternalInput")
    sel8_d = nc.dram_tensor("sel8", [8, 8 * P], BF16, kind="ExternalInput")
    out_d = nc.dram_tensor("out", [T, C], F32, kind="ExternalOutput")

    x_bf = nc.dram_tensor("x_bfs", [T, C], BF16, kind="Internal")
    sgw_bf = nc.dram_tensor("sgw_bfs", [2 * HSH, C], BF16, kind="Internal")
    sdw_bf = nc.dram_tensor("sdw_bfs", [C, HSH], BF16, kind="Internal")
    up_bf = nc.dram_tensor("up_bfs", [NE, 2 * HEX, C], BF16, kind="Internal")
    dw_bf = nc.dram_tensor("dw_bfs", [NE, C, HEX], BF16, kind="Internal")
    ln1tm_d = nc.dram_tensor("ln1tm_d", [NT, P, C], BF16, kind="Internal")
    ln1fm_d = nc.dram_tensor("ln1fm_d", [NT, P, T], BF16, kind="Internal")
    attn_tap_h = int(os.environ.get("KRN_ATTN_TAP_H", "-1"))
    if attn_tap_h >= 0:
        debug_tap = True
    if debug_tap:
        tap_d = nc.dram_tensor("tap", [P, NSLOT * SLOT], BF16, kind="ExternalOutput")
        tapf_d = nc.dram_tensor("tapf", [P, 4096], F32, kind="ExternalOutput")
        tapw_d = nc.dram_tensor("tapw", [P, NT * 8], F32, kind="ExternalOutput")
        tapd_d = nc.dram_tensor("tapd", [P, 8], F32, kind="ExternalOutput")
        tapr_d = nc.dram_tensor("tapr", [P, 48], F32, kind="ExternalOutput")
        tapg_d = nc.dram_tensor("tapg", [P, 2 * P], BF16, kind="ExternalOutput")

    es = ExitStack()
    with es:
        arena = es.enter_context(nc.sbuf_tensor("arena", [P, NSLOT * SLOT], BF16))
        fs = es.enter_context(nc.sbuf_tensor("fs", [P, 4096], F32))
        browt = es.enter_context(nc.sbuf_tensor("browt", [P, SLOT], BF16))
        cos_sb = es.enter_context(nc.sbuf_tensor("cos_sb", [P, NT * 32], BF16))
        sin_sb = es.enter_context(nc.sbuf_tensor("sin_sb", [P, NT * 32], BF16))
        eye_bf = es.enter_context(nc.sbuf_tensor("eye_bf", [P, P], BF16))
        eye_f = es.enter_context(nc.sbuf_tensor("eye_f", [P, P], F32))
        ub_sb = es.enter_context(nc.sbuf_tensor("ub_sb", [P, P], BF16))
        sel8_sb = es.enter_context(nc.sbuf_tensor("sel8_sb", [8, 8 * P], BF16))
        ones_bf = es.enter_context(nc.sbuf_tensor("ones_bf", [P, P], BF16))
        ones_f = es.enter_context(nc.sbuf_tensor("ones_f", [1, P], F32))
        cpool = es.enter_context(nc.sbuf_tensor("cpool", [P, 8], F32))
        bo_sb = es.enter_context(nc.sbuf_tensor("bo_sb", [P, NT], F32))
        gwt_sb = es.enter_context(nc.sbuf_tensor("gwt_sb", [P, P], BF16))
        dn_sb = es.enter_context(nc.sbuf_tensor("dn_sb", [P, 2], F32))
        rc_sb = es.enter_context(nc.sbuf_tensor("rc_sb", [P, 1], F32))
        diag_sb = es.enter_context(nc.sbuf_tensor("diag_sb", [P, 2 * P], BF16))
        st_sb = es.enter_context(nc.sbuf_tensor("st_sb", [P, 8], F32))
        rt_sb = es.enter_context(nc.sbuf_tensor("rt_sb", [P, 48], F32))
        we_all = es.enter_context(nc.sbuf_tensor("we_all", [P, NT * 8], F32))
        pS = es.enter_context(nc.psum_tensor("pS", [P, 2048], F32))
        pT = es.enter_context(nc.psum_tensor("pT", [P, 512], F32))
        pT2 = es.enter_context(nc.psum_tensor("pT2", [P, 512], F32))
        pV = es.enter_context(nc.psum_tensor("pV", [P, 1024], F32))

        sQ = Sem(nc, es, "sQ")
        sS = Sem(nc, es, "sS")
        sPE = Sem(nc, es, "sPE")
        sACT = Sem(nc, es, "sACT")
        sDVE = Sem(nc, es, "sDVE")
        sR = Sem(nc, es, "sR")

        def vedge(v, inst):
            sR.inc(inst, 1)
            v.wait_ge(sR.h, sR.n)

        def sl(i, off=0, ln=SLOT):
            return arena[:, i * SLOT + off: i * SLOT + off + ln]

        def strided(view, stride, count, width):
            return bass.AP(view.tensor, view.offset,
                           [view.ap[0], [stride, count], [1, width]])

        cnt = {}

        def emit_dump():
            with nc.Block() as blk:
                @blk.sync
                def _(sy):
                    sS.inc(sy.dma_start(tap_d.ap(), arena[:, :]), 16)
                    sS.inc(sy.dma_start(tapf_d.ap(), fs[:, :]), 16)
                    sS.inc(sy.dma_start(tapw_d.ap(), we_all[:, :]), 16)
                    sS.inc(sy.dma_start(tapd_d.ap()[:, 0:2], dn_sb[:, :], allow_slow_non_contiguous=True), 16)
                    sS.inc(sy.dma_start(tapd_d.ap()[:, 2:3], rc_sb[:, :], allow_slow_non_contiguous=True), 16)
                    sS.inc(sy.dma_start(tapg_d.ap(), diag_sb[:, :]), 16)
                    sS.inc(sy.dma_start(tapr_d.ap(), rt_sb[:, :]), 16)
                    sy.wait_ge(sS.h, sS.n)
            return nc

        with nc.Block() as blk:
            @blk.gpsimd
            def _(g):
                for xc in range(4):
                    cnt[f'x_bf{xc}'] = sQ.inc(
                        g.dma_start(x_bf.ap()[:, xc * 512:(xc + 1) * 512],
                                    x_in.ap()[:, xc * 512:(xc + 1) * 512]), 16)
                cnt['sgw'] = sQ.inc(g.dma_start(sgw_bf.ap(), sgw_d.ap()), 16)
                cnt['sdw'] = sQ.inc(g.dma_start(sdw_bf.ap(), sdw_d.ap()), 16)
                for e in range(NE):
                    cnt[f'up{e}'] = sQ.inc(g.dma_start(up_bf.ap()[e], upw_d.ap()[e]), 16)
                    cnt[f'dw{e}'] = sQ.inc(g.dma_start(dw_bf.ap()[e], dww_d.ap()[e]), 16)

            @blk.sync
            def _(sy):
                sS.inc(sy.dma_start(cos_sb[:, :], cos_d.ap()), 16)
                sS.inc(sy.dma_start(sin_sb[:, :], sin_d.ap()), 16)
                sS.inc(sy.dma_start(eye_bf[:, :], eyeb_d.ap()), 16)
                sS.inc(sy.dma_start(eye_f[:, :], eyef_d.ap()), 16)
                sS.inc(sy.dma_start(ub_sb[:, :], ub_d.ap()), 16)
                sS.inc(sy.dma_start(sel8_sb[0:8, :], sel8_d.ap()), 16)
                sS.inc(sy.dma_start(bo_sb[:, :], bo_d.ap()), 16)
                sS.inc(sy.dma_start(fs[0:1, 2048:4096], bq_d.ap()), 16)
                sS.inc(sy.dma_start(fs[0:1, 0:KVW], bk_d.ap()), 16)
                cnt['consts'] = sS.inc(sy.dma_start(fs[0:1, 512:1024], bv_d.ap()), 16)

            @blk.vector
            def _(v):
                v.memset(ones_bf[:, :], 1.0)
                v.memset(ones_f[0:1, :], 1.0)
                v.memset(cpool[:, CP_EXPB:CP_EXPB + 1], -EXP_C)
                v.memset(cpool[:, CP_SCL:CP_SCL + 1], 0.125)
                v.memset(cpool[:, CP_NEG:CP_NEG + 1], -1e9)
                v.memset(cpool[:, CP_INV:CP_INV + 1], 1.0 / C)
                v.memset(cpool[:, CP_EPS:CP_EPS + 1], EPS)
                v.memset(cpool[:, CP_ONE:CP_ONE + 1], 1.0)

        if not phase_on("b1"):
            return emit_dump()
        with nc.Block() as blk:
            @blk.sync
            def _(sy):
                for c in range(16):
                    sy.wait_ge(sQ.h, cnt[f'x_bf{c // 4}'])
                    sS.inc(sy.dma_start(sl(XFM[c]), x_bf.ap()[:, c * P:(c + 1) * P],
                                        transpose=True), 16)
                cnt['xfm'] = sS.n

            @blk.gpsimd
            def _(g):
                dst = arena[:, 40 * SLOT: 40 * SLOT + 16 * 512].rearrange(
                    "p (c s) -> p c s", s=512)
                src = wk_d.ap().rearrange("(c p) s -> p c s", p=P)
                cnt['wk'] = sQ.inc(g.dma_start(dst, src), 16)

            @blk.scalar
            def _(sc):
                sc.wait_ge(sS.h, cnt['consts'])
                cc = sc.copy(browt[0:1, 0:1024], fs[0:1, 0:1024])
                cnt['bias_kv_hi'] = sACT.inc(cc, 1)

            @blk.vector
            def _(v):
                v.wait_ge(sACT.h, cnt['bias_kv_hi'])
                vv = v.tensor_sub(fs[0:1, 1024:2048], fs[0:1, 0:1024],
                                  browt[0:1, 0:1024])
                cnt['bias_kv_lo_f'] = sDVE.inc(vv, 1)

        with nc.Block() as blk:
            @blk.scalar
            def _(sc):
                sc.wait_ge(sDVE.h, cnt['bias_kv_lo_f'])
                cc = sc.copy(browt[32:33, 0:1024], fs[0:1, 1024:2048])
                cnt['bias_kv_lo32'] = sACT.inc(cc, 1)

            @blk.gpsimd
            def _(g):
                g.wait_ge(sS.h, cnt['xfm'])
                g.wait_ge(sACT.h, cnt['bias_kv_lo32'])
                g.dma_start(browt[1:2, 0:1024], browt[32:33, 0:1024]).then_inc(sQ.h, 16)
                sQ.n += 16
                cnt['bias_kv_done'] = sQ.n

        def emit_prep_bq():
            with nc.Block() as blk:
                @blk.scalar
                def _(sc):
                    cc = sc.copy(browt[0:1, 0:C], fs[0:1, 2048:4096])
                    cnt['bias_q_hi'] = sACT.inc(cc, 1)

                @blk.vector
                def _(v):
                    v.wait_ge(sACT.h, cnt['bias_q_hi'])
                    vv = v.tensor_sub(fs[0:1, 0:2048], fs[0:1, 2048:4096],
                                      browt[0:1, 0:C])
                    cnt['bias_q_lo_f'] = sDVE.inc(vv, 1)

            with nc.Block() as blk:
                @blk.scalar
                def _(sc):
                    sc.wait_ge(sDVE.h, cnt['bias_q_lo_f'])
                    cc = sc.copy(browt[32:33, 0:C], fs[0:1, 0:2048])
                    cnt['bias_q_lo32'] = sACT.inc(cc, 1)

                @blk.gpsimd
                def _(g):
                    g.wait_ge(sACT.h, cnt['bias_q_lo32'])
                    g.dma_start(browt[1:2, 0:C], browt[32:33, 0:C]).then_inc(sQ.h, 16)
                    sQ.n += 16
                    cnt['bias_q_done'] = sQ.n

        def emit_rope(v, scr512, dst512, tb):
            sv = scr512.rearrange("p (h two i) -> p h two i", two=2, i=32)
            dv = dst512.rearrange("p (h two i) -> p h two i", two=2, i=32)
            cs = cos_sb[:, tb * 32:(tb + 1) * 32].rearrange("p (o i) -> p o i", o=1)
            sn = sin_sb[:, tb * 32:(tb + 1) * 32].rearrange("p (o i) -> p o i", o=1)
            csb = bass.AP(cs.tensor, cs.offset, [cs.ap[0], [0, 8], cs.ap[2]])
            snb = bass.AP(sn.tensor, sn.offset, [sn.ap[0], [0, 8], sn.ap[2]])
            ev, od = sv[:, :, 0, :], sv[:, :, 1, :]
            t1 = fs[:, 0:256].rearrange("p (h i) -> p h i", i=32)
            t2 = fs[:, 256:512].rearrange("p (h i) -> p h i", i=32)
            v.tensor_mul(t1, ev, csb)
            vedge(v, v.tensor_mul(t2, od, snb))
            vedge(v, v.tensor_sub(dv[:, :, 0, :], t1, t2))
            v.tensor_mul(t1, od, csb)
            vedge(v, v.tensor_mul(t2, ev, snb))
            return v.tensor_add(dv[:, :, 1, :], t1, t2)

        def emit_proj_pass(wkey, bias_rows, bias_key, dst_fn, do_rope, next_w_load):
            with nc.Block() as blk:
                pe0, ac0, dv0 = sPE.n, sACT.n, sDVE.n

                @blk.gpsimd
                def _(g):
                    if next_w_load is not None:
                        name, src_ap = next_w_load
                        g.wait_ge(sPE.h, pe0 + NT)
                        dst = arena[:, 40 * SLOT: 40 * SLOT + 16 * 512].rearrange(
                            "p (c s) -> p c s", s=512)
                        cnt[name] = sQ.inc(g.dma_start(dst, src_ap), 16)

                @blk.tensor
                def _(t):
                    t.wait_ge(sS.h, cnt['xfm'])
                    t.wait_ge(sQ.h, cnt[wkey])
                    t.wait_ge(sQ.h, cnt[bias_key])
                    for tb in range(NT):
                        q4 = tb % 4
                        ps = pS[:, q4 * 512:(q4 + 1) * 512]
                        if tb >= 4:
                            t.wait_ge(sACT.h, ac0 + (tb - 3))
                        for c in range(16):
                            t.matmul(ps, sl(XFM[c], tb * P, P), sl(*WST[c], ln=512),
                                     start=(c == 0), stop=False)
                        mm = t.matmul(ps, bias_rows[0], bias_rows[1],
                                      start=False, stop=True)
                        sPE.inc(mm, 1)

                @blk.scalar
                def _(sc):
                    for tb in range(NT):
                        q4 = tb % 4
                        ps = pS[:, q4 * 512:(q4 + 1) * 512]
                        sc.wait_ge(sPE.h, pe0 + tb + 1)
                        if do_rope:
                            if tb >= 2:
                                sc.wait_ge(sDVE.h, dv0 + 6 * (tb - 1))
                            cc = sc.copy(sl(SCR44, (tb % 2) * 512, 512), ps)
                        else:
                            cc = sc.copy(dst_fn(tb), ps)
                        sACT.inc(cc, 1)

                @blk.vector
                def _(v):
                    if not do_rope:
                        return
                    for tb in range(NT):
                        v.wait_ge(sACT.h, ac0 + tb + 1)
                        vv = emit_rope(v, sl(SCR44, (tb % 2) * 512, 512),
                                       dst_fn(tb), tb)
                        sDVE.inc(vv, 6)

        if not phase_on("pk"):
            return emit_dump()
        bias_k = (ones_bf[0:2, 0:P], browt[0:2, 0:512])
        bias_v = (ones_bf[0:2, 0:P], browt[0:2, 512:1024])

        emit_proj_pass('wk', bias_k, 'bias_kv_done',
                       lambda tb: sl(*KTM[tb], ln=512), True,
                       ('wv', wv_d.ap().rearrange("(c p) s -> p c s", p=P)))
        if not phase_on("pv"):
            return emit_dump()
        emit_proj_pass('wv', bias_v, 'bias_kv_done',
                       lambda tb: sl(*VTM[tb], ln=512), False,
                       ('wq0', wq_d.ap()[:, 0:512].rearrange("(c p) s -> p c s", p=P)))
        if not phase_on("pq1"):
            return emit_dump()
        emit_prep_bq()
        for qq in range(4):
            if qq >= 1 and not phase_on("proj"):
                break
            bias_q = (ones_bf[0:2, 0:P], browt[0:2, qq * 512:(qq + 1) * 512])
            nxt = None
            if qq < 3:
                nxt = (f'wq{qq + 1}',
                       wq_d.ap()[:, (qq + 1) * 512:(qq + 2) * 512].rearrange(
                           "(c p) s -> p c s", p=P))
            emit_proj_pass(f'wq{qq}', bias_q, 'bias_q_done',
                           lambda tb, qq=qq: sl(QTM[tb], qq * 512, 512), True, nxt)

        if not phase_on("ktr"):
            return emit_dump()
        with nc.Block() as blk:
            pe0, dv0 = sPE.n, sDVE.n

            @blk.tensor
            def _(t):
                for tb in range(NT):
                    if tb >= 1:
                        t.wait_ge(sDVE.h, dv0 + 8 * tb)
                    b1 = [pT, pT2][tb % 2]
                    b2 = [pT2, pT][tb % 2]
                    ktile = sl(*KTM[tb], ln=512)
                    mm = None
                    for g in range(HKV):
                        lhs = ktile[:, g * 64:(g + 1) * 64]
                        bank = b1 if g < 4 else b2
                        ps = bank[:, (g % 4) * P:(g % 4) * P + P]
                        t.matmul(ps[0:64, :], lhs, eye_bf[:, :], start=True, stop=True)
                        mm = t.matmul(ps[64:128, :], lhs, eye_bf[:, :], start=True, stop=True)
                    sPE.inc(mm, 1)

            @blk.vector
            def _(v):
                for tb in range(NT):
                    v.wait_ge(sPE.h, pe0 + tb + 1)
                    b1 = [pT, pT2][tb % 2]
                    b2 = [pT2, pT][tb % 2]
                    vv = None
                    for g in range(HKV):
                        bank = b1 if g < 4 else b2
                        ps = bank[:, (g % 4) * P:(g % 4) * P + P]
                        vv = v.tensor_copy(sl(KDUP[g], tb * P, P), ps)
                    sDVE.inc(vv, 8)

        if not phase_on("qtr"):
            return emit_dump()
        with nc.Block() as blk:
            pe0, ac0, dv0 = sPE.n, sACT.n, sDVE.n

            @blk.tensor
            def _(t):
                for tb in range(NT):
                    for grp in range(4):
                        bank = pS[:, grp * 512:(grp + 1) * 512]
                        if tb >= 1:
                            if grp % 2 == 0:
                                t.wait_ge(sDVE.h, dv0 + (tb - 1) * 2 + grp // 2 + 1)
                            else:
                                t.wait_ge(sACT.h, ac0 + (tb - 1) * 2 + grp // 2 + 1)
                        mm = None
                        for j in range(4):
                            d = grp * 4 + j
                            mm = t.matmul(bank[:, j * P:(j + 1) * P],
                                          sl(QTM[tb], d * P, P), eye_bf[:, :],
                                          start=True, stop=True)
                        sPE.inc(mm, 1)

            @blk.vector
            def _(v):
                for tb in range(NT):
                    for grp in (0, 2):
                        bank = pS[:, grp * 512:(grp + 1) * 512]
                        v.wait_ge(sPE.h, pe0 + tb * 4 + grp + 1)
                        vv = None
                        for j in range(4):
                            d = grp * 4 + j
                            vv = v.tensor_copy(sl(QFM[d], tb * P, P),
                                               bank[:, j * P:(j + 1) * P])
                        sDVE.inc(vv, 1)

            @blk.scalar
            def _(sc):
                for tb in range(NT):
                    for grp in (1, 3):
                        bank = pS[:, grp * 512:(grp + 1) * 512]
                        sc.wait_ge(sPE.h, pe0 + tb * 4 + grp + 1)
                        cc = None
                        for j in range(4):
                            d = grp * 4 + j
                            cc = sc.copy(sl(QFM[d], tb * P, P),
                                         bank[:, j * P:(j + 1) * P])
                        sACT.inc(cc, 1)

        if not phase_on("attn"):
            return emit_dump()
        pe0a, ac0a, dv0a = sPE.n, sACT.n, sDVE.n
        g_chunks = []
        pe_items_all = []
        act_items_all = []
        dve_items_all = []
        for h in range(HQ):
            for qt in range(NT):
                klen = (qt + 1) * P
                chs = [(0, min(1024, klen))]
                if klen > 1024:
                    chs.append((1, klen - 1024))
                for (ch, cw) in chs:
                    g_chunks.append((h, qt, ch, cw))
                    pe_items_all.append(("S", h, qt, ch, cw))
                    act_items_all.append(("E", h, qt, ch, cw))
                dve_items_all.append(("DG", h, qt, len(chs)))
                if qt == 9:
                    pe_items_all.append(("PV", h, 0))
                    act_items_all.append(("O", h, 0))
                if qt >= 1:
                    pe_items_all.append(("PT", h, qt - 1))
                    dve_items_all.append(("EV", h, qt - 1, (qt + 3) // 4))
            pe_items_all.append(("PT", h, NT - 1))
            dve_items_all.append(("EV", h, NT - 1, (NT + 3) // 4))
            pe_items_all.append(("PV", h, 1))
            act_items_all.append(("O", h, 1))

        s_mark = {}
        pt_gmarks = {}
        pv_mark = {}
        m = 0
        gidx = 0
        for it in pe_items_all:
            if it[0] == "S":
                m += 1
                s_mark[(it[1], it[2], it[3])] = pe0a + m
            elif it[0] == "PT":
                ng = (it[2] + 4) // 4
                mk = []
                for _ in range(ng):
                    m += 1
                    mk.append(pe0a + m)
                pt_gmarks[(it[1], it[2])] = mk
            else:
                m += 1
                pv_mark[(it[1], it[2])] = pe0a + m

        exp_mark = {}
        oev_mark = {}
        a = 0
        for it in act_items_all:
            a += 1
            if it[0] == "E":
                exp_mark[(it[1], it[2], it[3])] = ac0a + a
            else:
                oev_mark[(it[1], it[2])] = ac0a + a

        diag_mark = {}
        evac_gmark = []
        evac_last = {}
        d = 0
        for it in dve_items_all:
            if it[0] == "DG":
                d += (1 if it[3] == 2 else 0) + 2
                diag_mark[(it[1], it[2])] = dv0a + d
            else:
                for _ in range(it[3]):
                    d += 1
                    evac_gmark.append(dv0a + d)
                evac_last[(it[1], it[2])] = dv0a + d

        last_ch = {qt: (1 if (qt + 1) * P > 1024 else 0) for qt in range(NT)}

        s_gci = {}
        for i_, (h_, q_, c_, w_) in enumerate(g_chunks):
            s_gci[(h_, q_, c_)] = i_
        g_chunks = [(h_, q_, c_, w_) for (h_, q_, c_, w_) in g_chunks]

        ptm_prev_user = {}
        dn_prev_user = {}
        last_pt_mark = {0: None, 1: None}
        prev_diag = None
        diag_prev_user = {}
        prev_oev = {}
        last_oev = None
        for it in pe_items_all:
            pass
        for h in range(HQ):
            for qt in range(NT):
                for ch in range(2 if (qt + 1) * P > 1024 else 1):
                    ptm_prev_user[(h, qt, ch)] = last_pt_mark[qt % 2]
                    dn_prev_user[(h, qt, ch)] = prev_diag
                diag_prev_user[(h, qt)] = last_pt_mark[qt % 2]
                prev_diag = diag_mark[(h, qt)]
                if qt >= 1:
                    last_pt_mark[(qt - 1) % 2] = pt_gmarks[(h, qt - 1)][-1]
                if qt == 9:
                    prev_oev[(h, 0)] = last_oev
                    last_oev = oev_mark[(h, 0)]
            last_pt_mark[(NT - 1) % 2] = pt_gmarks[(h, NT - 1)][-1]
            prev_oev[(h, 1)] = last_oev
            last_oev = oev_mark[(h, 1)]

        with nc.Block() as blk:
            @blk.tensor
            def _(t):
                ggl = [0]
                for idx, it in enumerate(pe_items_all):
                    if it[0] == "S":
                        _, h, qt, ch, cw = it
                        hp, g_ = h // 2, h // 4
                        base = (h % 2) * 64
                        ci = g_chunks.index((h, qt, ch)) if False else None
                        gci = s_gci[(h, qt, ch)]
                        ps = pS[:, (gci % 2) * 1024: (gci % 2) * 1024 + cw]
                        if gci >= 2:
                            ph_, pq_, pc_, _ = g_chunks[gci - 2]
                            t.wait_ge(sACT.h, exp_mark[(ph_, pq_, pc_)])
                        lhs = arena[base:base + 64,
                                    QFM[hp] * SLOT + qt * P: QFM[hp] * SLOT + (qt + 1) * P]
                        dpos = qt * P - ch * 1024
                        mm = None
                        pos = 0
                        while pos < cw:
                            w = min(512, cw - pos)
                            kcol = ch * 1024 + pos
                            rhs = arena[base:base + 64,
                                        KDUP[g_] * SLOT + kcol: KDUP[g_] * SLOT + kcol + w]
                            has_diag = (pos <= dpos < pos + w)
                            mm = t.matmul(ps[:, pos:pos + w], lhs, rhs,
                                          start=True, stop=not has_diag)
                            pos += w
                        if 0 <= dpos < cw:
                            mm = t.matmul(ps[:, dpos:dpos + P], eye_bf[:, :],
                                          ub_sb[:, :], start=False, stop=True)
                        sPE.inc(mm, 1)
                    elif it[0] == "PT":
                        _, h, qtp = it
                        nkb = qtp + 1
                        t.wait_ge(sDVE.h, diag_mark[(h, qtp)])
                        t.wait_ge(sACT.h, exp_mark[(h, qtp, last_ch[qtp])])
                        for g0 in range(0, nkb, 4):
                            gw_ = min(4, nkb - g0)
                            gi = ggl[0]
                            bank = [pT, pT2][gi % 2]
                            if gi >= 2:
                                t.wait_ge(sDVE.h, evac_gmark[gi - 2])
                            mm = None
                            for j in range(gw_):
                                kb = g0 + j
                                mm = t.matmul(bank[:, j * P:(j + 1) * P],
                                              sl(PTM[qtp % 2], kb * P, P),
                                              diag_sb[:, (qtp % 2) * P:(qtp % 2 + 1) * P],
                                              start=True, stop=True)
                            sPE.inc(mm, 1)
                            ggl[0] += 1
                    else:
                        _, h, half = it
                        g_ = h // 4
                        hp = h // 2
                        base = (h % 2) * 64
                        kmax = 8 if half == 0 else 16
                        lastq = 7 if half == 0 else 15
                        t.wait_ge(sDVE.h, evac_last[(h, lastq)])
                        prevo = prev_oev[(h, half)]
                        if prevo is not None:
                            t.wait_ge(sACT.h, prevo)
                        mm = None
                        for kb in range(kmax):
                            qs = max(0, kb * P - half * 1024)
                            vt = sl(*VTM[kb], ln=512)
                            lhs = vt[:, g_ * 64:(g_ + 1) * 64]
                            pos = qs
                            while pos < 1024:
                                w = min(512 - (pos % 512), 1024 - pos)
                                mm = t.matmul(pV[0:64, pos:pos + w], lhs,
                                              arena[:, PTR * SLOT + kb * 1024 + pos:
                                                    PTR * SLOT + kb * 1024 + pos + w],
                                              start=(kb == 0),
                                              stop=(kb == kmax - 1))
                                pos += w
                        sPE.inc(mm, 1)

            @blk.scalar
            def _(sc):
                for it in act_items_all:
                    if it[0] == "E":
                        _, h, qt, ch, cw = it
                        gci = s_gci[(h, qt, ch)]
                        ps = pS[:, (gci % 2) * 1024:(gci % 2) * 1024 + cw]
                        sc.wait_ge(sPE.h, s_mark[(h, qt, ch)])
                        pm = ptm_prev_user[(h, qt, ch)]
                        if pm is not None:
                            sc.wait_ge(sPE.h, pm)
                        dm = dn_prev_user[(h, qt, ch)]
                        if dm is not None and ch == 0:
                            sc.wait_ge(sDVE.h, dm)
                        cc = sc.activation(sl(PTM[qt % 2], ch * 1024, cw), ps,
                                           AF.Exp,
                                           bias=cpool[:, CP_EXPB:CP_EXPB + 1],
                                           scale=cpool[:, CP_SCL:CP_SCL + 1],
                                           accum_out=dn_sb[:, ch:ch + 1])
                        sACT.inc(cc, 1)
                    else:
                        _, h, half = it
                        hp = h // 2
                        base = (h % 2) * 64
                        sc.wait_ge(sPE.h, pv_mark[(h, half)])
                        obase = QFM[hp] * SLOT + half * 1024
                        sc.copy(arena[base:base + 64, obase:obase + 512],
                                pV[0:64, 0:512])
                        cc = sc.copy(arena[base:base + 64, obase + 512:obase + 1024],
                                     pV[0:64, 512:1024])
                        sACT.inc(cc, 1)

            @blk.vector
            def _(v):
                gi2 = [0]
                for it in dve_items_all:
                    if it[0] == "DG":
                        _, h, qt, nch = it
                        v.wait_ge(sACT.h, exp_mark[(h, qt, last_ch[qt])])
                        pm = diag_prev_user[(h, qt)]
                        if pm is not None:
                            v.wait_ge(sPE.h, pm)
                        k = 0
                        if nch == 2:
                            aa = v.tensor_add(st_sb[:, 0:1], dn_sb[:, 0:1],
                                              dn_sb[:, 1:2])
                            sR.inc(aa, 1)
                            v.wait_ge(sR.h, sR.n)
                            srcv = st_sb[:, 0:1]
                            k = 1
                        else:
                            srcv = dn_sb[:, 0:1]
                        rr = v.reciprocal(rc_sb[:, 0:1], srcv)
                        sR.inc(rr, 1)
                        v.wait_ge(sR.h, sR.n)
                        vv = v.tensor_scalar(
                            diag_sb[:, (qt % 2) * P:(qt % 2 + 1) * P],
                            eye_bf[:, :], rc_sb[:, 0:1], None, ALU.mult)
                        sDVE.inc(vv, k + 2)
                    else:
                        _, h, qtp, ng = it
                        for gg in range(ng):
                            v.wait_ge(sPE.h, pt_gmarks[(h, qtp)][gg])
                            gi = gi2[0]
                            bank = [pT, pT2][gi % 2]
                            gw_ = min(4, qtp + 1 - gg * 4)
                            dstb = arena[:, PTR * SLOT + (gg * 4) * 1024 + (qtp % 8) * P:
                                         PTR * SLOT + (gg * 4) * 1024 + (qtp % 8) * P + P]
                            dst = strided(dstb, 1024, gw_, P)
                            vv = v.tensor_copy(dst, bank[:, 0:gw_ * P])
                            sDVE.inc(vv, 1)
                            gi2[0] += 1


        with nc.Block() as blk:
            pe0, ac0, dv0, q0 = sPE.n, sACT.n, sDVE.n, sQ.n

            @blk.gpsimd
            def _(g):
                for cb in range(16):
                    if cb >= 2:
                        g.wait_ge(sPE.h, pe0 + cb - 1)
                    b = WOS[cb % 2]
                    dst = arena[:, b * SLOT: b * SLOT + 16 * P].rearrange(
                        "p (c s) -> p c s", s=P)
                    src = wo_d.ap()[:, cb * P:(cb + 1) * P].rearrange("(c p) s -> p c s", p=P)
                    sQ.inc(g.dma_start(dst, src), 16)

            @blk.tensor
            def _(t):
                for cb in range(16):
                    t.wait_ge(sQ.h, q0 + 16 * (cb + 1))
                    if cb >= 1:
                        t.wait_ge(sDVE.h, dv0 + cb)
                        t.wait_ge(sACT.h, ac0 + cb)
                    b = WOS[cb % 2]
                    mm = None
                    for cin in range(16):
                        lhs = sl(b, cin * P, P)
                        for tch in range(4):
                            mm = t.matmul(pS[:, tch * 512:(tch + 1) * 512], lhs,
                                          sl(QFM[cin], tch * 512, 512),
                                          start=(cin == 0), stop=(cin == 15))
                    sPE.inc(mm, 1)

            @blk.vector
            def _(v):
                for cb in range(16):
                    v.wait_ge(sPE.h, pe0 + cb + 1)
                    vv = None
                    for tch in (0, 1):
                        vv = v.tensor_scalar(sl(ATP[cb], tch * 512, 512),
                                             pS[:, tch * 512:(tch + 1) * 512],
                                             bo_sb[:, cb:cb + 1], None, ALU.add)
                    sDVE.inc(vv, 1)

            @blk.scalar
            def _(sc):
                for cb in range(16):
                    sc.wait_ge(sPE.h, pe0 + cb + 1)
                    cc = None
                    for tch in (2, 3):
                        cc = sc.add(sl(ATP[cb], tch * 512, 512),
                                    pS[:, tch * 512:(tch + 1) * 512],
                                    bo_sb[:, cb:cb + 1])
                    sACT.inc(cc, 1)

        if not phase_on("ln1"):
            return emit_dump()
        with nc.Block() as blk:
            pe0, ac0, dv0, s0 = sPE.n, sACT.n, sDVE.n, sS.n
            NDV, NAC = 5, 3

            def dvm(tb, i):
                return dv0 + NDV * tb + i

            def acm(tb, i):
                return ac0 + NAC * tb + i

            @blk.sync
            def _(sy):
                for tb in range(NT):
                    if tb >= 1:
                        sy.wait_ge(sDVE.h, dvm(tb - 1, 1))
                    sS.inc(sy.dma_start(fs[:, 2048:4096],
                                        x_in.ap()[tb * P:(tb + 1) * P, :]), 16)
                    sy.wait_ge(sDVE.h, dvm(tb, 4))
                    sS.inc(sy.dma_start(ln1tm_d.ap()[tb], sl(LN1TMS[tb % 2])), 16)
                sy.wait_ge(sDVE.h, dv0 + NDV * NT)
                sy.wait_ge(sACT.h, ac0 + NAC * NT)
                for c in range(16):
                    sS.inc(sy.dma_start(ln1fm_d.ap()[c], sl(LN1FM[c])), 16)
                cnt['ln1fm_stored'] = sS.n

            @blk.tensor
            def _(t):
                for tb in range(NT):
                    if tb >= 1:
                        t.wait_ge(sDVE.h, dvm(tb - 1, 5))
                        t.wait_ge(sACT.h, acm(tb - 1, 3))
                    mm = None
                    for c in range(16):
                        mm = t.matmul(pS[:, c * P:(c + 1) * P],
                                      sl(ATP[c], tb * P, P), eye_bf[:, :],
                                      start=True, stop=True)
                    sPE.inc(mm, 1)
                    t.wait_ge(sDVE.h, dvm(tb, 4))
                    mm = None
                    for c in range(16):
                        mm = t.matmul(pS[:, c * P:(c + 1) * P],
                                      sl(LN1TMS[tb % 2], c * P, P), eye_bf[:, :],
                                      start=True, stop=True)
                    sPE.inc(mm, 1)

            @blk.vector
            def _(v):
                for tb in range(NT):
                    v.wait_ge(sPE.h, pe0 + 2 * tb + 1)
                    v.wait_ge(sS.h, s0 + 32 * tb + 16)
                    vv = None
                    for q4 in range(4):
                        vv = v.tensor_add(fs[:, q4 * 512:(q4 + 1) * 512],
                                          pS[:, q4 * 512:(q4 + 1) * 512],
                                          fs[:, 2048 + q4 * 512:2048 + (q4 + 1) * 512])
                    sDVE.inc(vv, 1)
                    v.wait_ge(sDVE.h, sDVE.n)
                    vedge(v, v.reduce_sum(st_sb[:, 0:1], fs[:, 0:2048], axis=AX.X))
                    vedge(v, v.tensor_scalar(st_sb[:, 1:2], st_sb[:, 0:1],
                                             cpool[:, CP_INV:CP_INV + 1], None, ALU.mult))
                    vv = v.tensor_mul(st_sb[:, 2:3], st_sb[:, 0:1], st_sb[:, 1:2])
                    sDVE.inc(vv, 1)
                    v.wait_ge(sDVE.h, sDVE.n)
                    v.wait_ge(sACT.h, acm(tb, 1))
                    vedge(v, v.tensor_sub(st_sb[:, 4:5], st_sb[:, 3:4], st_sb[:, 2:3]))
                    vv = v.tensor_scalar(st_sb[:, 5:6], st_sb[:, 4:5],
                                         cpool[:, CP_INV:CP_INV + 1], None, ALU.mult)
                    sDVE.inc(vv, 1)
                    v.wait_ge(sDVE.h, sDVE.n)
                    v.wait_ge(sACT.h, acm(tb, 2))
                    rr = v.reciprocal(st_sb[:, 7:8], st_sb[:, 6:7])
                    sR.inc(rr, 1)
                    v.wait_ge(sR.h, sR.n)
                    if tb >= 2:
                        v.wait_ge(sS.h, s0 + 32 * (tb - 2) + 32)
                    vv = v.tensor_scalar(sl(LN1TMS[tb % 2]), fs[:, 0:2048],
                                         st_sb[:, 1:2], st_sb[:, 7:8],
                                         ALU.subtract, ALU.mult)
                    sDVE.inc(vv, 1)
                    v.wait_ge(sPE.h, pe0 + 2 * tb + 2)
                    vv = None
                    for grp in (0, 2):
                        c0 = grp * 4
                        dstb = arena[:, LN1FM[c0] * SLOT + tb * P:
                                     LN1FM[c0] * SLOT + tb * P + P]
                        vv = v.tensor_copy(strided(dstb, SLOT, 4, P),
                                           pS[:, c0 * P:c0 * P + 4 * P])
                    sDVE.inc(vv, 1)

            @blk.scalar
            def _(sc):
                for tb in range(NT):
                    sc.wait_ge(sDVE.h, dvm(tb, 1))
                    if tb >= 2:
                        sc.wait_ge(sS.h, s0 + 32 * (tb - 2) + 32)
                    cc = sc.activation(sl(LN1TMS[tb % 2]), fs[:, 0:2048], AF.Square,
                                       accum_out=st_sb[:, 3:4])
                    sACT.inc(cc, 1)
                    sc.wait_ge(sDVE.h, dvm(tb, 3))
                    cc = sc.activation(st_sb[:, 6:7], st_sb[:, 5:6], AF.Sqrt,
                                       bias=cpool[:, CP_EPS:CP_EPS + 1],
                                       scale=cpool[:, CP_ONE:CP_ONE + 1])
                    sACT.inc(cc, 1)
                    sc.wait_ge(sPE.h, pe0 + 2 * tb + 2)
                    cc = None
                    for grp in (1, 3):
                        c0 = grp * 4
                        dstb = arena[:, LN1FM[c0] * SLOT + tb * P:
                                     LN1FM[c0] * SLOT + tb * P + P]
                        cc = sc.copy(strided(dstb, SLOT, 4, P),
                                     pS[:, c0 * P:c0 * P + 4 * P])
                    sACT.inc(cc, 1)

        if not phase_on("rout"):
            return emit_dump()
        with nc.Block() as blk:
            pe0, ac0, dv0 = sPE.n, sACT.n, sDVE.n

            @blk.gpsimd
            def _(g):
                cnt['gw'] = sQ.inc(g.dma_start(arena[0:8, GWSB * SLOT:GWSB * SLOT + C],
                                               gw_d.ap()), 16)

            @blk.sync
            def _(sy):
                cnt['gb'] = sS.inc(sy.dma_start(fs[0:1, 0:8], gb_d.ap()), 16)
                sy.wait_ge(sPE.h, sPE.n + 3)
                sy.wait_ge(sS.h, cnt['ln1fm_stored'])
                for c in range(16):
                    sS.inc(sy.dma_start(sl(*LN1H[c], ln=1024),
                                        ln1fm_d.ap()[c][:, 0:1024]), 16)
                cnt['ln1h_pre'] = sS.n

            @blk.tensor
            def _(t):
                t.wait_ge(sQ.h, cnt['gw'])
                mm = None
                for c in range(16):
                    mm = t.matmul(pT[:, c * 8:(c + 1) * 8],
                                  arena[0:8, GWSB * SLOT + c * P:GWSB * SLOT + (c + 1) * P],
                                  eye_bf[0:8, 0:8], start=True, stop=True)
                sPE.inc(mm, 1)
                t.wait_ge(sS.h, cnt['gb'])
                mm = t.matmul(pT2[:, 0:8], ones_f[0:1, 0:P], fs[0:1, 0:8],
                              start=True, stop=True)
                sPE.inc(mm, 1)
                t.wait_ge(sDVE.h, dv0 + 1)
                mm = None
                for c in range(16):
                    for tch in range(4):
                        mm = t.matmul(pS[0:8, tch * 512:(tch + 1) * 512],
                                      gwt_sb[:, c * 8:(c + 1) * 8],
                                      sl(LN1FM[c], tch * 512, 512),
                                      start=(c == 0), stop=(c == 15))
                sPE.inc(mm, 1)
                for tb in range(NT):
                    t.wait_ge(sACT.h, ac0 + 1 + tb)
                    mm = t.matmul(pT[:, 0:8],
                                  arena[0:8, GLOG * SLOT + tb * P:GLOG * SLOT + (tb + 1) * P],
                                  eye_bf[0:8, 0:8], start=True, stop=True)
                    sPE.inc(mm, 1)
                    t.wait_ge(sDVE.h, dv0 + 1 + 10 * (tb + 1))
                    mm = t.matmul(pS[0:8, tb * P:(tb + 1) * P],
                                  we_all[:, tb * 8:(tb + 1) * 8], eye_f[:, :],
                                  start=True, stop=True)
                    sPE.inc(mm, 1)

            @blk.vector
            def _(v):
                v.wait_ge(sPE.h, pe0 + 1)
                vv = v.tensor_copy(gwt_sb[:, :], pT[:, 0:P])
                sDVE.inc(vv, 1)
                for tb in range(NT):
                    v.wait_ge(sACT.h, ac0 + 2 + tb)
                    sc_ = rt_sb[:, 0:8]
                    vedge(v, v.tensor_add(rt_sb[:, 8:16], sc_, pT2[:, 0:8]))
                    vedge(v, v.reduce_max(rt_sb[:, 16:17], rt_sb[:, 8:16], axis=AX.X))
                    vedge(v, v.tensor_scalar(rt_sb[:, 24:32], rt_sb[:, 8:16],
                                             rt_sb[:, 16:17], None, ALU.is_ge))
                    vedge(v, v.scalar_tensor_tensor(rt_sb[:, 32:40], rt_sb[:, 24:32],
                                                    cpool[:, CP_NEG:CP_NEG + 1],
                                                    rt_sb[:, 8:16], ALU.mult, ALU.add))
                    vedge(v, v.reduce_max(rt_sb[:, 17:18], rt_sb[:, 32:40], axis=AX.X))
                    vedge(v, v.tensor_scalar(rt_sb[:, 24:32], rt_sb[:, 8:16],
                                             rt_sb[:, 17:18], None, ALU.is_ge))
                    vedge(v, v.tensor_mul(rt_sb[:, 32:40], rt_sb[:, 24:32], sc_))
                    ss = v.reduce_sum(rt_sb[:, 18:19], rt_sb[:, 32:40], axis=AX.X)
                    sR.inc(ss, 1)
                    v.wait_ge(sR.h, sR.n)
                    rr = v.reciprocal(rt_sb[:, 19:20], rt_sb[:, 18:19])
                    sR.inc(rr, 1)
                    v.wait_ge(sR.h, sR.n)
                    vv = v.tensor_scalar(we_all[:, tb * 8:(tb + 1) * 8],
                                         rt_sb[:, 32:40], rt_sb[:, 19:20], None,
                                         ALU.mult)
                    sDVE.inc(vv, 10)

            @blk.scalar
            def _(sc):
                sc.wait_ge(sPE.h, pe0 + 3)
                cc = sc.copy(arena[0:8, GLOG * SLOT:GLOG * SLOT + T], pS[0:8, 0:T])
                sACT.inc(cc, 1)
                for tb in range(NT):
                    sc.wait_ge(sPE.h, pe0 + 4 + 2 * tb)
                    cc = sc.activation(rt_sb[:, 0:8], pT[:, 0:8], AF.Sigmoid)
                    sACT.inc(cc, 1)
                sc.wait_ge(sPE.h, pe0 + 5 + 2 * (NT - 1))
                cc = sc.copy(arena[0:8, WER8 * SLOT:WER8 * SLOT + T], pS[0:8, 0:T])
                cnt['we_r8'] = sACT.inc(cc, 1)

        if not phase_on("ffn"):
            return emit_dump()
        jobs = []
        jobs.append(dict(name="shA", up=sgw_bf.ap()[0:HSH, :],
                         dn=lambda kb: sdw_bf.ap()[:, kb * P:(kb + 1) * P],
                         wcnt='sgw', dcnt='sdw', expert=None, first=True))
        jobs.append(dict(name="shB", up=sgw_bf.ap()[HSH:2 * HSH, :],
                         dn=lambda kb: sdw_bf.ap()[:, HEX + kb * P:HEX + (kb + 1) * P],
                         wcnt='sgw', dcnt='sdw', expert=None, first=False))
        for e in range(NE):
            jobs.append(dict(name=f"e{e}", up=up_bf.ap()[e],
                             dn=lambda kb, e=e: dw_bf.ap()[e][:, kb * P:(kb + 1) * P],
                             wcnt=f'up{e}', dcnt=f'dw{e}', expert=e, first=False))

        RW = [512, 512, 512, 512, 512, 256]
        PAIRB = [
            ((pS[:, 0:512], pS[:, 512:1024]), (pS[:, 1024:1536], pS[:, 1536:2048])),
            ((pT[:, :], pT2[:, :]), (pV[:, 0:512], pV[:, 512:1024])),
        ]

        for ph in range(2):
            for ji, job in enumerate(jobs):
                with nc.Block() as blk:
                    pe0, ac0, dv0, s0 = sPE.n, sACT.n, sDVE.n, sS.n
                    exp = job['expert']
                    npair = 11
                    pe_webc = 1 if exp is not None else 0
                    n_ln1h = 16 if (ji == 0 and ph == 1) else 0
                    range_mark = {}

                    @blk.sync
                    def _(sy, job=job, ji=ji, ph=ph, pe0=pe0, s0=s0, n_ln1h=n_ln1h,
                          range_mark=range_mark, pe_webc=pe_webc):
                        if ji == 0 and ph == 1:
                            sy.wait_ge(sS.h, cnt['ln1fm_stored'])
                            for c in range(16):
                                sS.inc(sy.dma_start(
                                    sl(*LN1H[c], ln=1024),
                                    ln1fm_d.ap()[c][:, ph * 1024:(ph + 1) * 1024]), 16)
                        sy.wait_ge(sQ.h, cnt[job['wcnt']])
                        for r in range(2):
                            rw = RW[r]
                            for c in range(16):
                                sS.inc(sy.dma_start(
                                    sl(*UPT[r % 2][c], ln=rw),
                                    job['up'][r * 512:r * 512 + rw, c * P:(c + 1) * P],
                                    transpose=True), 16)
                            range_mark[r] = sS.n
                        sy.wait_ge(sQ.h, cnt[job['dcnt']])
                        for kb in range(11):
                            sS.inc(sy.dma_start(sl(DWT[kb]), job['dn'](kb),
                                                transpose=True), 16)
                        cnt['dwT_done'] = sS.n
                        for r in range(2, 6):
                            lastpair = 2 * (r - 2) + 1
                            sy.wait_ge(sPE.h, pe0 + pe_webc + lastpair + 1)
                            rw = RW[r]
                            for c in range(16):
                                sS.inc(sy.dma_start(
                                    sl(*UPT[r % 2][c], ln=rw),
                                    job['up'][r * 512:r * 512 + rw, c * P:(c + 1) * P],
                                    transpose=True), 16)
                            range_mark[r] = sS.n

                    @blk.tensor
                    def _(t, job=job, ji=ji, ph=ph, pe0=pe0, ac0=ac0, dv0=dv0,
                          s0=s0, exp=exp, range_mark=range_mark, pe_webc=pe_webc,
                          n_ln1h=n_ln1h):
                        if exp is not None:
                            mm = None
                            for cq in range(2):
                                src = arena[0:8, WER8 * SLOT + ph * 1024 + cq * 512:
                                            WER8 * SLOT + ph * 1024 + (cq + 1) * 512]
                                mm = t.matmul(pS[:, cq * 512:(cq + 1) * 512],
                                              sel8_sb[0:8, exp * P:(exp + 1) * P],
                                              src, start=True, stop=True)
                            sPE.inc(mm, 1)
                        for i in range(npair):
                            r = i // 2
                            gb_, ub_ = PAIRB[i % 2]
                            t.wait_ge(sS.h, range_mark[r])
                            if ji == 0 and i == 0:
                                t.wait_ge(sS.h, cnt['ln1h_pre'] if ph == 0
                                          else s0 + 16 * n_ln1h)
                            if exp is not None and i == 0:
                                t.wait_ge(sACT.h, ac0 + 1)
                            if i >= 2:
                                t.wait_ge(sDVE.h, dv0 + i - 1)
                            mm = None
                            for blkrow, bks in ((2 * i, gb_), (2 * i + 1, ub_)):
                                sub = (blkrow % 4) * P
                                for c in range(16):
                                    lhs = arena[:, UPT[r % 2][c][0] * SLOT + UPT[r % 2][c][1] + sub:
                                              UPT[r % 2][c][0] * SLOT + UPT[r % 2][c][1] + sub + P]
                                    for ch in range(2):
                                        mm = t.matmul(
                                            bks[ch],
                                            lhs,
                                            arena[:, LN1H[c][0] * SLOT + LN1H[c][1] + ch * 512:
                                                  LN1H[c][0] * SLOT + LN1H[c][1] + (ch + 1) * 512],
                                            start=(c == 0), stop=(c == 15))
                            sPE.inc(mm, 1)
                        t.wait_ge(sDVE.h, dv0 + npair)
                        t.wait_ge(sS.h, cnt['dwT_done'])
                        for tbl in range(8):
                            banks = ([pS[:, 0:512], pS[:, 512:1024], pS[:, 1024:1536],
                                      pS[:, 1536:2048]] if tbl % 2 == 0 else
                                     [pT[:, :], pT2[:, :], pV[:, 0:512], pV[:, 512:1024]])
                            if tbl >= 2:
                                if job['first']:
                                    t.wait_ge(sACT.h, ac0 + npair + (tbl - 1))
                                else:
                                    t.wait_ge(sDVE.h, dv0 + npair + (tbl - 1))
                            if tbl < 2:
                                t.wait_ge(sACT.h, ac0 + (1 if exp is not None else 0) + npair)
                            mm = None
                            for kb in range(11):
                                lhs = arena[:, HSL[kb][0] * SLOT + HSL[kb][1] + tbl * P:
                                            HSL[kb][0] * SLOT + HSL[kb][1] + (tbl + 1) * P]
                                for cc4 in range(4):
                                    mm = t.matmul(banks[cc4],
                                                  lhs,
                                                  sl(DWT[kb], cc4 * 512, 512),
                                                  start=(kb == 0), stop=(kb == 10))
                            sPE.inc(mm, 1)

                    @blk.scalar
                    def _(sc, job=job, exp=exp, pe0=pe0, ac0=ac0, dv0=dv0,
                          pe_webc=pe_webc):
                        if exp is not None:
                            sc.wait_ge(sPE.h, pe0 + 1)
                            sc.copy(sl(*WEBC, ln=512), pS[:, 0:512])
                            cc = sc.copy(arena[:, WEBC[0] * SLOT + WEBC[1] + 512:
                                               WEBC[0] * SLOT + WEBC[1] + 1024],
                                         pS[:, 512:1024])
                            sACT.inc(cc, 1)
                        for i in range(11):
                            gb_, ub_ = PAIRB[i % 2]
                            sc.wait_ge(sPE.h, pe0 + pe_webc + i + 1)
                            if i % 2 == 0:
                                cc = sc.activation(sl(*SGS, ln=1024), pS[:, 0:1024],
                                                   AF.Silu)
                            else:
                                sc.activation(sl(*SGS, ln=512), pT[:, :], AF.Silu)
                                cc = sc.activation(
                                    arena[:, SGS[0] * SLOT + SGS[1] + 512:
                                          SGS[0] * SLOT + SGS[1] + 1024],
                                    pT2[:, :], AF.Silu)
                            sACT.inc(cc, 1)
                        if job['first']:
                            for tbl in range(8):
                                sc.wait_ge(sPE.h, pe0 + pe_webc + 11 + tbl + 1)
                                banks = ([pS[:, 0:512], pS[:, 512:1024], pS[:, 1024:1536],
                                          pS[:, 1536:2048]] if tbl % 2 == 0 else
                                         [pT[:, :], pT2[:, :], pV[:, 0:512], pV[:, 512:1024]])
                                cc = None
                                for cc4 in range(4):
                                    cc = sc.copy(sl(ACC[tbl], cc4 * 512, 512), banks[cc4])
                                sACT.inc(cc, 1)

                    @blk.vector
                    def _(v, job=job, exp=exp, pe0=pe0, ac0=ac0, dv0=dv0,
                          pe_webc=pe_webc):
                        for i in range(11):
                            gb_, ub_ = PAIRB[i % 2]
                            v.wait_ge(sACT.h, ac0 + (1 if exp is not None else 0) + i + 1)
                            v.wait_ge(sPE.h, pe0 + pe_webc + i + 1)
                            hdst = arena[:, HSL[i][0] * SLOT + HSL[i][1]:
                                         HSL[i][0] * SLOT + HSL[i][1] + 1024]
                            hdst2 = arena[:, HSL[i][0] * SLOT + HSL[i][1] + 512:
                                          HSL[i][0] * SLOT + HSL[i][1] + 1024]
                            if exp is not None:
                                vedge(v, v.tensor_mul(sl(*SWC, ln=1024),
                                                      sl(*SGS, ln=1024),
                                                      sl(*WEBC, ln=1024)))
                                if i % 2 == 0:
                                    vv = v.tensor_mul(hdst, sl(*SWC, ln=1024),
                                                      pS[:, 1024:2048])
                                else:
                                    v.tensor_mul(
                                        arena[:, HSL[i][0] * SLOT + HSL[i][1]:
                                              HSL[i][0] * SLOT + HSL[i][1] + 512],
                                        sl(*SWC, ln=512), pV[:, 0:512])
                                    vv = v.tensor_mul(hdst2,
                                                      arena[:, SWC[0] * SLOT + SWC[1] + 512:
                                                            SWC[0] * SLOT + SWC[1] + 1024],
                                                      pV[:, 512:1024])
                            else:
                                if i % 2 == 0:
                                    vv = v.tensor_mul(hdst, sl(*SGS, ln=1024),
                                                      pS[:, 1024:2048])
                                else:
                                    v.tensor_mul(
                                        arena[:, HSL[i][0] * SLOT + HSL[i][1]:
                                              HSL[i][0] * SLOT + HSL[i][1] + 512],
                                        sl(*SGS, ln=512), pV[:, 0:512])
                                    vv = v.tensor_mul(hdst2,
                                                      arena[:, SGS[0] * SLOT + SGS[1] + 512:
                                                            SGS[0] * SLOT + SGS[1] + 1024],
                                                      pV[:, 512:1024])
                            sDVE.inc(vv, 1)
                        if not job['first']:
                            for tbl in range(8):
                                v.wait_ge(sPE.h, pe0 + pe_webc + 11 + tbl + 1)
                                banks = ([pS[:, 0:512], pS[:, 512:1024], pS[:, 1024:1536],
                                          pS[:, 1536:2048]] if tbl % 2 == 0 else
                                         [pT[:, :], pT2[:, :], pV[:, 0:512], pV[:, 512:1024]])
                                vv = None
                                for cc4 in range(4):
                                    acs = sl(ACC[tbl], cc4 * 512, 512)
                                    vv = v.tensor_add(acs, acs, banks[cc4])
                                sDVE.inc(vv, 1)

            with nc.Block() as blk:
                pe0, ac0, dv0, s0 = sPE.n, sACT.n, sDVE.n, sS.n
                NDV2, NAC2 = 4, 2

                def dvm2(tbl, i):
                    return dv0 + NDV2 * tbl + i

                def acm2(tbl, i):
                    return ac0 + NAC2 * tbl + i

                @blk.sync
                def _(sy, ph=ph, s0=s0, dv0=dv0):
                    for tbl in range(8):
                        if tbl >= 1:
                            sy.wait_ge(sDVE.h, dvm2(tbl - 1, 1))
                        sS.inc(sy.dma_start(sl(LN2TM), ln1tm_d.ap()[ph * 8 + tbl]), 16)
                        sy.wait_ge(sDVE.h, dvm2(tbl, 4))
                        sS.inc(sy.dma_start(
                            out_d.ap()[(ph * 8 + tbl) * P:(ph * 8 + tbl + 1) * P, :],
                            fs[:, 2048:4096]), 16)
                    cnt[f'out_st{ph}'] = sS.n

                @blk.vector
                def _(v, ph=ph, s0=s0, ac0=ac0):
                    for tbl in range(8):
                        v.wait_ge(sS.h, s0 + 32 * tbl + 16)
                        if tbl >= 1:
                            v.wait_ge(sS.h, s0 + 32 * tbl)
                        vv = v.tensor_add(fs[:, 0:2048], sl(LN2TM), sl(ACC[tbl]))
                        sDVE.inc(vv, 1)
                        v.wait_ge(sDVE.h, sDVE.n)
                        vedge(v, v.reduce_sum(st_sb[:, 0:1], fs[:, 0:2048], axis=AX.X))
                        vedge(v, v.tensor_scalar(st_sb[:, 1:2], st_sb[:, 0:1],
                                                 cpool[:, CP_INV:CP_INV + 1], None,
                                                 ALU.mult))
                        vv = v.tensor_mul(st_sb[:, 2:3], st_sb[:, 0:1], st_sb[:, 1:2])
                        sDVE.inc(vv, 1)
                        v.wait_ge(sDVE.h, sDVE.n)
                        v.wait_ge(sACT.h, acm2(tbl, 1))
                        vedge(v, v.tensor_sub(st_sb[:, 4:5], st_sb[:, 3:4],
                                              st_sb[:, 2:3]))
                        vv = v.tensor_scalar(st_sb[:, 5:6], st_sb[:, 4:5],
                                             cpool[:, CP_INV:CP_INV + 1], None, ALU.mult)
                        sDVE.inc(vv, 1)
                        v.wait_ge(sDVE.h, sDVE.n)
                        v.wait_ge(sACT.h, acm2(tbl, 2))
                        rr = v.reciprocal(st_sb[:, 7:8], st_sb[:, 6:7])
                        sR.inc(rr, 1)
                        v.wait_ge(sR.h, sR.n)
                        vv = v.tensor_scalar(fs[:, 2048:4096], fs[:, 0:2048],
                                             st_sb[:, 1:2], st_sb[:, 7:8],
                                             ALU.subtract, ALU.mult)
                        sDVE.inc(vv, 1)

                @blk.scalar
                def _(sc, dv0=dv0):
                    for tbl in range(8):
                        sc.wait_ge(sDVE.h, dvm2(tbl, 1))
                        cc = sc.activation(pS[:, 0:2048], fs[:, 0:2048], AF.Square,
                                           accum_out=st_sb[:, 3:4])
                        sACT.inc(cc, 1)
                        sc.wait_ge(sDVE.h, dvm2(tbl, 3))
                        cc = sc.activation(st_sb[:, 6:7], st_sb[:, 5:6], AF.Sqrt,
                                           bias=cpool[:, CP_EPS:CP_EPS + 1],
                                           scale=cpool[:, CP_ONE:CP_ONE + 1])
                        sACT.inc(cc, 1)

        with nc.Block() as blk:
            @blk.sync
            def _(sy):
                sy.wait_ge(sS.h, cnt['out_st1'])
                sy.wait_ge(sS.h, cnt['out_st0'])

    return nc



_PROG_CACHE = {}


def _get_prog():
    if 'nc' not in _PROG_CACHE:
        _PROG_CACHE['nc'] = build_program()
    return _PROG_CACHE['nc']


def kernel(L0_attn_x, L0_attn_Wq, L0_attn_bq, L0_attn_Wk, L0_attn_bk,
           L0_attn_Wv, L0_attn_bv, L0_attn_Wo, L0_attn_bo,
           L0_ffn_gate_w, L0_ffn_gate_bias, L0_ffn_shared_gw, L0_ffn_shared_dw,
           L0_ffn_up_w, L0_ffn_down_w):
    x = np.asarray(L0_attn_x, dtype=np.float32)

    perm = np.concatenate([np.arange(0, D, 2), np.arange(1, D, 2)])
    qcols = np.concatenate([h * D + perm for h in range(HQ)])
    kcols = np.concatenate([h * D + perm for h in range(HKV)])
    wq = np.ascontiguousarray(np.asarray(L0_attn_Wq, np.float32)[:, qcols])
    bq = np.ascontiguousarray(np.asarray(L0_attn_bq, np.float32)[qcols])[None, :]
    wk = np.ascontiguousarray(np.asarray(L0_attn_Wk, np.float32)[:, kcols])
    bk = np.ascontiguousarray(np.asarray(L0_attn_bk, np.float32)[kcols])[None, :]
    wv = np.ascontiguousarray(np.asarray(L0_attn_Wv, np.float32))
    bv = np.ascontiguousarray(np.asarray(L0_attn_bv, np.float32))[None, :]
    wo = np.ascontiguousarray(np.asarray(L0_attn_Wo, np.float32))
    bo = np.ascontiguousarray(
        np.asarray(L0_attn_bo, np.float32).reshape(NT, P).T)

    gw = np.ascontiguousarray(np.asarray(L0_ffn_gate_w, np.float32))
    gb = np.ascontiguousarray(np.asarray(L0_ffn_gate_bias, np.float32))[None, :]

    sgw = np.asarray(L0_ffn_shared_gw, np.float32)
    yb = sgw[0:HSH].reshape(22, P, C)
    gbk = sgw[HSH:2 * HSH].reshape(22, P, C)
    sgw_p = np.ascontiguousarray(
        np.stack([gbk, yb], axis=1).reshape(2 * HSH, C))
    sdw = np.ascontiguousarray(np.asarray(L0_ffn_shared_dw, np.float32))

    upw = np.asarray(L0_ffn_up_w, np.float32)
    gpart = upw[:, 0:HEX].reshape(NE, 11, P, C)
    upart = upw[:, HEX:2 * HEX].reshape(NE, 11, P, C)
    upw_p = np.ascontiguousarray(
        np.stack([gpart, upart], axis=2).reshape(NE, 2 * HEX, C))
    dww = np.ascontiguousarray(np.asarray(L0_ffn_down_w, np.float32))

    inv = (1.0 / (10000.0 ** (np.arange(0, D, 2, dtype=np.float32) / D))).astype(np.float32)
    pos = np.arange(T, dtype=np.float32)
    fr = np.outer(pos, inv).astype(np.float32)
    cosp = np.ascontiguousarray(
        np.cos(fr).astype(np.float32).reshape(NT, P, 32).transpose(1, 0, 2
                                                                   ).reshape(P, NT * 32)).astype(bfloat16)
    sinp = np.ascontiguousarray(
        np.sin(fr).astype(np.float32).reshape(NT, P, 32).transpose(1, 0, 2
                                                                   ).reshape(P, NT * 32)).astype(bfloat16)

    eyeb = np.eye(P, dtype=np.float32).astype(bfloat16)
    eyef = np.eye(P, dtype=np.float32)
    ub = np.zeros((P, P), dtype=np.float32)
    ub[np.triu_indices(P, 1)] = -1e9
    ubb = ub.astype(bfloat16)
    sel8 = np.zeros((8, 8 * P), dtype=np.float32)
    for e in range(8):
        sel8[e, e * P:(e + 1) * P] = 1.0
    sel8 = sel8.astype(bfloat16)

    shared = dict(wq=wq, bq=bq, wk=wk, bk=bk, wv=wv, bv=bv, wo=wo, bo_c=bo,
                  gw=gw, gb=gb, sgw=sgw_p, sdw=sdw, upw=upw_p, dww=dww,
                  cosp=cosp, sinp=sinp, eyeb=eyeb, eyef=eyef, ubias=ubb,
                  sel8=sel8)
    in_maps = []
    for c in range(8):
        m = dict(shared)
        m['x'] = np.ascontiguousarray(x[c])
        in_maps.append(m)

    nc = _get_prog()
    res = run_bass_kernel_spmd(nc, in_maps, core_ids=list(range(8)))
    globals()['LAST'] = res
    return np.stack([res.results[i]['out'] for i in range(8)], axis=0)
